# version 03
# baseline (speedup 1.0000x reference)
import contextlib
from collections import defaultdict

import numpy as np
import concourse.bass as bass
import concourse.mybir as mybir

F32 = mybir.dt.float32
BF16 = mybir.dt.bfloat16
I32 = mybir.dt.int32
AF = mybir.ActivationFunctionType
ALU = mybir.AluOpType
AX = mybir.AxisListType

ENGS = ("sync", "gpsimd", "tensor", "vector", "scalar")


class K:
    AUTO = {"vector": "VE", "scalar": "AE", "gpsimd": "GP"}

    def __init__(self, name="prog"):
        self.nc = bass.Bass("TRN2", target_bir_lowering=False)
        self.plan = {e: [] for e in ENGS}
        self.cnt = defaultdict(int)
        self.sem = {}
        self.stack = contextlib.ExitStack()
        self.phase_stacks = []
        self.dram = {}
        self.final_waits = []
        self.dma_watch = {}
        self.mksem("PE", "VE", "AE", "GP")

    def din(self, name, shape, dtype=F32):
        ap = self.nc.dram_tensor(name, list(shape), dtype, kind="ExternalInput").ap()
        self.dram[name] = ap
        return ap

    def dout(self, name, shape, dtype=F32):
        ap = self.nc.dram_tensor(name, list(shape), dtype, kind="ExternalOutput").ap()
        self.dram[name] = ap
        return ap

    def dtmp(self, name, shape, dtype=F32):
        ap = self.nc.dram_tensor(name, list(shape), dtype, kind="Internal").ap()
        self.dram[name] = ap
        return ap

    def sbuf(self, name, shape, dtype=F32):
        st = self.phase_stacks[-1] if self.phase_stacks else self.stack
        return st.enter_context(self.nc.sbuf_tensor(name, list(shape), dtype))

    def psum(self, name, shape, dtype=F32):
        st = self.phase_stacks[-1] if self.phase_stacks else self.stack
        return st.enter_context(self.nc.psum_tensor(name, list(shape), dtype))

    def push_phase(self):
        self.phase_stacks.append(contextlib.ExitStack())

    def pop_phase(self):
        self.phase_stacks.pop().close()
        self.fence()

    def fence(self):
        eng_w = [("PE", self.cnt["PE"]), ("VE", self.cnt["VE"]),
                 ("AE", self.cnt["AE"]), ("GP", self.cnt["GP"])]
        dma_w = list(self.dma_watch.items())
        for e in ENGS:
            own = self.AUTO.get(e)
            w = [(s, v) for (s, v) in eng_w if s != own] + dma_w
            self.waiton(e, *w)

    def mksem(self, *names):
        for n in names:
            if n not in self.sem:
                self.sem[n] = self.stack.enter_context(self.nc.semaphore(n))

    def op(self, eng, method, *args, wait=None, inc=None, incn=1, **kwargs):
        is_dma = method.startswith("dma") or method.startswith("indirect")
        auto = self.AUTO.get(eng) if not is_dma else None
        waits = list(wait or [])
        if auto is not None:
            waits.append((auto, self.cnt[auto]))
            if inc is None:
                inc = auto
            if inc == auto:
                incn = 1
        tgt = None
        if inc is not None:
            self.cnt[inc] += incn
            tgt = self.cnt[inc]
            if inc not in ("PE", "VE", "AE", "GP"):
                self.dma_watch[inc] = tgt
        waits = [(s, v) for (s, v) in waits if v > 0]
        semd, n_ = self.sem, incn

        def fn(e, method=method, args=args, kwargs=kwargs, waits=waits, inc=inc):
            for (s, v) in waits:
                e.wait_ge(semd[s], v)
            instr = getattr(e, method)(*args, **kwargs)
            if inc is not None:
                instr.then_inc(semd[inc], n_)
            return instr

        self.plan[eng].append(fn)
        return tgt

    def waiton(self, eng, *sv):
        sv = [(s, v) for (s, v) in sv if v > 0]
        if not sv:
            return

        def fn(e, sv=sv):
            for (s, v) in sv:
                e.wait_ge(self.sem[s], v)
        self.plan[eng].append(fn)

    def store(self, dst, src, eng="sync", sem="STG", wait=None):
        self.mksem(sem)
        t = self.op(eng, "dma_start", out=dst, in_=src, wait=wait, inc=sem, incn=16)
        self._note_final(sem, t)
        return t

    def _note_final(self, sem, tgt):
        self.final_waits = [(s, v) for (s, v) in self.final_waits if s != sem]
        self.final_waits.append((sem, tgt))

    def finish(self, eng="sync"):
        self.waiton(eng, *self.final_waits)

    def now(self, sem):
        return self.cnt[sem]

    def build(self):
        nc = self.nc
        with nc.Block() as block:
            for ename in ENGS:
                def driver(e, ename=ename):
                    for fn in self.plan[ename]:
                        fn(e)
                getattr(block, ename)(driver)
        self.stack.close()
        return nc


class LBuf:

    def __init__(self, k: K, name, n, shape, dtype):
        self.k, self.name, self.n = k, name, n
        self.bufs = [k.sbuf(f"{name}{i}", shape, dtype) for i in range(n)]
        self.sems = [f"{name}S{i}" for i in range(n)]
        k.mksem(*self.sems)
        self.uses = [0] * n
        self.released = [None] * n
        self.i = 0

    def load(self, eng, in_ap, view=None, extra_wait=(), out_view=None):
        b = self.i % self.n
        self.i += 1
        self.uses[b] += 1
        w = list(extra_wait)
        if self.released[b] is not None:
            w.append(self.released[b])
        out = self.bufs[b][:, :] if out_view is None else out_view(self.bufs[b])
        t = self.k.op(eng, "dma_start", out=out, in_=in_ap, wait=w,
                      inc=self.sems[b], incn=16)
        return self.bufs[b], (self.sems[b], t), b

    def release(self, b, sem, tgt):
        self.released[b] = (sem, tgt)


class SOut:

    def __init__(self, k: K, name, n, shape, dtype):
        self.k, self.name, self.n = k, name, n
        self.bufs = [k.sbuf(f"{name}{i}", shape, dtype) for i in range(n)]
        self.sems = [f"{name}S{i}" for i in range(n)]
        k.mksem(*self.sems)
        self.stored = [0] * n
        self.i = 0

    def acquire(self):
        b = self.i % self.n
        self.i += 1
        w = [(self.sems[b], self.stored[b] * 16)] if self.stored[b] else []
        return self.bufs[b], w, b

    def flush(self, b, dst, wait, eng="sync", src_view=None):
        src = self.bufs[b][:, :] if src_view is None else src_view(self.bufs[b])
        self.stored[b] += 1
        t = self.k.op(eng, "dma_start", out=dst, in_=src, wait=wait,
                      inc=self.sems[b], incn=16)
        self.k._note_final(self.sems[b], t)
        return t


class TPose:

    def __init__(self, k: K, name, ident):
        self.k = k
        self.ps = [k.psum(name + "a", [128, 128], BF16),
                   k.psum(name + "b", [128, 128], BF16)]
        self.ident = ident
        self.i = 0
        self.last_copy = [0, 0]

    def emit(self, src_view, dst_view, extra_wait=(), copy_extra_wait=(), rows=128):
        k, b = self.k, self.i % 2
        self.i += 1
        w = [("VE", self.last_copy[b])] + list(extra_wait)
        pe = k.op("tensor", "transpose", self.ps[b][0:rows, :], src_view,
                  self.ident[:, :], wait=w, inc="PE")
        ve = k.op("vector", "tensor_copy", wait=[("PE", pe)] + list(copy_extra_wait),
                  out=dst_view, in_=self.ps[b][0:rows, :])
        self.last_copy[b] = ve
        return pe, ve


def ts(t, n=128):
    return slice(t * n, (t + 1) * n)


RSQRT_H = float(1.0 / np.sqrt(np.float32(2048.0)))



def build_launch_A():
    k = K("A")
    xA = k.din("xA", [1024, 2048], F32)
    wqkvT = k.din("wqkvT", [2048, 6144], F32)
    bqkv = k.din("bqkv", [48, 128], F32)
    identb = k.din("identb", [128, 128], F32)
    onesf = k.din("onesf", [128, 128], F32)
    w1s = k.din("w1s", [256, 5632], F32)
    w2s = k.din("w2s", [768, 2048], F32)
    w3s = k.din("w3s", [256, 5632], F32)
    xnT_d = k.dout("xnT_d", [2048, 1024], BF16)
    qT_d = k.dout("qT_d", [2048, 1024], BF16)
    kT_d = k.dout("kT_d", [2048, 1024], BF16)
    v_d = k.dout("v_d", [1024, 2048], BF16)
    wsums_d = k.dout("wsums_d", [1, 4], F32)

    xc = k.sbuf("xc", [128, 2048], F32)
    sq = k.sbuf("sq", [128, 2048], F32)
    st1 = k.sbuf("st1", [128, 8], F32)
    xnT = k.sbuf("xnT", [128, 16 * 1024], BF16)
    vTs = k.sbuf("vTs", [128, 16 * 1024], BF16)
    ident = k.sbuf("ident", [128, 128], BF16)
    onessb = k.sbuf("onessb", [128, 128], F32)
    bq = k.sbuf("bq", [128, 48], F32)
    acc = k.sbuf("acc", [128, 4], F32)
    red = k.sbuf("red", [128, 1], F32)
    wsum_st = k.sbuf("wsum_st", [1, 4], F32)
    vst = k.sbuf("vst", [128, 2048], BF16)

    pqkv = [k.psum(f"pqkv{i}", [128, 512], F32) for i in range(2)]
    pws = k.psum("pws", [128, 4], F32)

    k.mksem("CN", "CNG")
    tp = TPose(k, "ptp", ident)
    xsb = LBuf(k, "xsb", 2, [128, 2048], F32)
    wslb = LBuf(k, "wslb", 2, [128, 16 * 128], BF16)
    wshb = LBuf(k, "wshb", 2, [128, 5632], F32)
    qkst = SOut(k, "qkst", 3, [128, 512], BF16)

    k.op("sync", "dma_start", out=bq[:, :], inc="CN", incn=16,
         in_=bqkv.rearrange("b p -> p b"), allow_slow_non_contiguous=True)
    k.op("sync", "dma_start", out=onessb[:, :], in_=onesf, inc="CN", incn=16)
    k.op("gpsimd", "dma_start", out=ident[:, :], in_=identb, inc="CNG", incn=16)
    const_cn = k.now("CN")
    const_cng = k.now("CNG")

    last_tp_pe = 0
    for t in range(8):
        xbuf, ldw, xslot = xsb.load("sync", xA[ts(t), :])
        k.op("vector", "reduce_sum", wait=[ldw],
             out=st1[:, 0:1], in_=xbuf[:, :], axis=AX.X)
        k.op("vector", "tensor_scalar", out=st1[:, 1:2], in0=st1[:, 0:1],
             scalar1=1.0 / 2048.0, scalar2=None, op0=ALU.mult)
        k.op("vector", "tensor_scalar", out=xc[:, :], in0=xbuf[:, :],
             scalar1=st1[:, 1:2], scalar2=None, op0=ALU.subtract)
        ve_xcdone = k.op("vector", "tensor_tensor", out=sq[:, :], in0=xc[:, :],
                         in1=xc[:, :], op=ALU.mult)
        xsb.release(xslot, "VE", ve_xcdone)
        k.op("vector", "reduce_sum", out=st1[:, 2:3], in_=sq[:, :], axis=AX.X)
        v_ve = k.op("vector", "tensor_scalar", out=st1[:, 3:4], in0=st1[:, 2:3],
                    scalar1=1.0 / 2048.0, scalar2=1e-5, op0=ALU.mult, op1=ALU.add)
        k.op("scalar", "activation", wait=[("VE", v_ve)], out=st1[:, 4:5],
             in_=st1[:, 3:4], func=AF.Ln)
        a2 = k.op("scalar", "activation", out=st1[:, 5:6], in_=st1[:, 4:5],
                  func=AF.Exp, scale=-0.5)
        ve2 = k.op("vector", "tensor_scalar", wait=[("AE", a2), ("PE", last_tp_pe)],
                   out=vst[:, :], in0=xc[:, :], scalar1=st1[:, 5:6], scalar2=None,
                   op0=ALU.mult)
        for c in range(16):
            pe, _ = tp.emit(vst[:, ts(c)],
                            xnT[:, c * 1024 + t * 128: c * 1024 + (t + 1) * 128],
                            extra_wait=[("VE", ve2), ("CNG", const_cng)])
            last_tp_pe = pe
    ve_xnT = k.now("VE")
    for c in range(16):
        k.store(xnT_d[ts(c), :], xnT[:, c * 1024:(c + 1) * 1024],
                sem="STX", wait=[("VE", ve_xnT)])

    def wslab_in(ob):
        return wqkvT[:, ts(ob)].rearrange("(c p) j -> p c j", p=128)

    def wslab_view(buf):
        return buf[:, :].rearrange("p (c j) -> p c j", j=128)

    pe_group = {}
    wld = {}
    _, wld[0], _ = wslb.load("gpsimd", wslab_in(0), out_view=wslab_view)
    for ob in range(48):
        if ob + 1 < 48:
            _, wld[ob + 1], _ = wslb.load(
                "gpsimd", wslab_in(ob + 1), out_view=wslab_view,
                extra_wait=[("PE", pe_group.get(ob - 1, 0))])
        w = wslb.bufs[ob % 2]
        for nh in range(2):
            pe_last = None
            for c in range(16):
                wv = []
                if c == 0:
                    wv = [wld[ob], ("VE", k.now("VE"))]
                pe_last = k.op(
                    "tensor", "matmul", pqkv[nh][:, :], w[:, ts(c)],
                    xnT[:, c * 1024 + nh * 512: c * 1024 + nh * 512 + 512],
                    start=(c == 0), stop=(c == 15),
                    wait=wv, inc=("PE" if c == 15 else None))
            pe_group[ob] = pe_last
            if ob < 32:
                buf, bw, bi = qkst.acquire()
                if ob < 16:
                    ve = k.op("vector", "tensor_scalar",
                              wait=[("PE", pe_last), ("CN", const_cn)] + bw,
                              out=buf[:, :], in0=pqkv[nh][:, :], scalar1=bq[:, ob:ob+1],
                              scalar2=RSQRT_H, op0=ALU.add, op1=ALU.mult)
                    dst = qT_d[ts(ob), nh * 512:(nh + 1) * 512]
                else:
                    ve = k.op("vector", "tensor_scalar",
                              wait=[("PE", pe_last), ("CN", const_cn)] + bw,
                              out=buf[:, :], in0=pqkv[nh][:, :], scalar1=bq[:, ob:ob+1],
                              scalar2=None, op0=ALU.add)
                    dst = kT_d[ts(ob - 16), nh * 512:(nh + 1) * 512]
                qkst.flush(bi, dst, wait=[("VE", ve)])
            else:
                vb = ob - 32
                k.op("vector", "tensor_scalar",
                     wait=[("PE", pe_last), ("CN", const_cn)],
                     out=vTs[:, vb * 1024 + nh * 512: vb * 1024 + nh * 512 + 512],
                     in0=pqkv[nh][:, :], scalar1=bq[:, ob:ob+1],
                     scalar2=None, op0=ALU.add)

    ve_v = k.now("VE")
    k.mksem("STV")
    stv = 0
    for t in range(8):
        for c in range(16):
            tp.emit(vTs[:, c * 1024 + t * 128: c * 1024 + (t + 1) * 128],
                    vst[:, ts(c)],
                    extra_wait=[("VE", ve_v)],
                    copy_extra_wait=[("STV", stv * 16)] if (c == 0 and stv) else [])
        stv += 1
        t_st = k.op("sync", "dma_start", out=v_d[ts(t), :], in_=vst[:, :],
                    wait=[("VE", k.now("VE"))], inc="STV", incn=16)
    k._note_final("STV", t_st)

    k.op("vector", "memset", acc[:, :], 0.0)
    for (wap, nch, cols, j) in [(w1s, 2, 5632, 0), (w2s, 6, 2048, 1), (w3s, 2, 5632, 2)]:
        for ch in range(nch):
            wbuf, ldw, wi = wshb.load("sync", wap[ts(ch), 0:cols],
                                      out_view=lambda b, cols=cols: b[:, 0:cols])
            k.op("vector", "reduce_sum", wait=[ldw],
                 out=red[:, :], in_=wbuf[:, 0:cols], axis=AX.X,
                 apply_absolute_value=True)
            ve = k.op("vector", "tensor_tensor", out=acc[:, j:j+1], in0=acc[:, j:j+1],
                      in1=red[:, :], op=ALU.add)
            wshb.release(wi, "VE", ve)
    pe = k.op("tensor", "matmul", pws[:, :], onessb[:, :], acc[:, :],
              wait=[("VE", k.now("VE")), ("CN", const_cn)],
              start=True, stop=True, inc="PE")
    ve = k.op("vector", "tensor_copy", wait=[("PE", pe)], out=wsum_st[:, :],
              in_=pws[0:1, :])
    k.store(wsums_d, wsum_st[:, :], sem="STW", wait=[("VE", ve)])

    k.finish()
    return k.build()


def prep_A(inp):
    x = np.ascontiguousarray(inp["L0_attn_hidden_states"].reshape(8192, 2048))
    wqkvT = np.ascontiguousarray(inp["L0_attn_w_qkv"].T)
    bqkv = np.ascontiguousarray(inp["L0_attn_b_qkv"].reshape(48, 128))
    identb = np.eye(128, dtype=np.float32)
    onesf = np.ones((128, 128), dtype=np.float32)
    w1T = np.ascontiguousarray(inp["L1_ffn_w1"].T)
    w2T = np.ascontiguousarray(inp["L1_ffn_w2"].T)
    w3T = np.ascontiguousarray(inp["L1_ffn_w3"].T)
    w2Tp = np.zeros((6144, 2048), dtype=np.float32)
    w2Tp[:5632] = w2T
    maps = []
    for c in range(8):
        maps.append(dict(
            xA=np.ascontiguousarray(x[c * 1024:(c + 1) * 1024]),
            wqkvT=wqkvT, bqkv=bqkv, identb=identb, onesf=onesf,
            w1s=np.ascontiguousarray(w1T[c * 256:(c + 1) * 256]),
            w2s=np.ascontiguousarray(w2Tp[c * 768:(c + 1) * 768]),
            w3s=np.ascontiguousarray(w3T[c * 256:(c + 1) * 256]),
        ))
    return maps




RMS_EPS = 1.1920929e-07
NW2 = float(1.0 / (5632.0 * 2048.0))


def build_launch_BCD(debug=False):
    k = K("BCD")
    xA = k.din("xA", [1024, 2048], F32)
    qT = k.din("qT", [2048, 1024], BF16)
    kTf = k.din("kTf", [2048, 8192], BF16)
    vf = k.din("vf", [8192, 2048], BF16)
    xnT_in = k.din("xnT_in", [2048, 1024], BF16)
    woutT = k.din("woutT", [2048, 2048], F32)
    bout = k.din("bout", [1, 2048], F32)
    identb = k.din("identb", [128, 128], F32)
    onesf = k.din("onesf", [128, 128], F32)
    desc = k.din("desc", [1, 8], F32)
    WgT = k.din("WgT", [2048, 8], F32)
    A1fT = k.din("A1fT", [2048, 256], F32)
    A3fT = k.din("A3fT", [2048, 256], F32)
    B1fT = k.din("B1fT", [256, 7168], F32)
    B3fT = k.din("B3fT", [256, 7168], F32)
    A2fT = k.din("A2fT", [7168, 256], F32)
    B2fT = k.din("B2fT", [256, 2048], F32)
    w13T0 = k.din("w13T0", [2048, 56 * 256], F32)
    bba2 = k.din("bba2", [56, 128, 768], F32)
    w2T0 = k.din("w2T0", [7168, 2048], F32)
    pat1 = k.din("pat1", [128, 256], F32)
    pat2 = k.din("pat2", [128, 256], F32)
    wkT = k.din("wkT", [2048, 2048], F32)
    wvT = k.din("wvT", [2048, 2048], F32)
    wsall = k.din("wsall", [8, 4], F32)
    w1s = k.din("w1s", [256, 5632], F32)
    w2s = k.din("w2s", [768, 2048], F32)
    w3s = k.din("w3s", [256, 5632], F32)
    x1_d = k.dout("x1_d", [1024, 2048], F32)
    kL1_d = k.dout("kL1_d", [2048, 1024], BF16)
    vL1_d = k.dout("vL1_d", [1024, 2048], BF16)
    t1_d = k.dout("t1_d", [256, 5632], BF16)
    t2_d = k.dout("t2_d", [768, 2048], BF16)
    t3_d = k.dout("t3_d", [256, 5632], BF16)
    dmk = k.dout if debug else k.dtmp
    xr_d = dmk("xr_d", [1024, 2048], F32)
    z_d = dmk("z_d", [7168, 1024], BF16)

    ident = k.sbuf("ident", [128, 128], BF16)
    onesb = k.sbuf("onesb", [128, 128], BF16)
    onesfs = k.sbuf("onesfs", [128, 128], F32)
    boutrep = k.sbuf("boutrep", [128, 2048], F32)
    k.mksem("CN", "CNG")
    k.op("gpsimd", "dma_start", out=ident[:, :], in_=identb, inc="CNG", incn=16)
    k.op("gpsimd", "dma_start", out=onesb[:, :], in_=onesf, inc="CNG", incn=16)
    k.op("sync", "dma_start", out=onesfs[:, :], in_=onesf, inc="CN", incn=16)
    k.op("sync", "dma_start", out=boutrep[:, :],
         in_=bout.broadcast_to((128, 2048)), inc="CN", incn=16)
    CN = k.now("CN")
    CNG = k.now("CNG")

    k.push_phase()
    qTB = k.sbuf("qTB", [128, 16 * 512], BF16)
    k.mksem("QTB")
    Pst = k.sbuf("Pst", [128, 64 * 512], BF16)
    rtmp = k.sbuf("rtmp", [128, 512], F32)
    rinv = k.sbuf("rinv", [128, 512], F32)
    OT = k.sbuf("OT", [128, 16 * 512], BF16)
    evt = k.sbuf("evt", [128, 512], F32)
    pool = [k.psum(f"apool{i}", [128, 512], F32) for i in range(6)]
    pread = [("VE", 0)] * 6
    ktl = LBuf(k, "ktl", 2, [128, 16 * 128], BF16)
    vsl = LBuf(k, "vsl", 2, [128, 64 * 128], BF16)
    wol = LBuf(k, "wol", 2, [128, 8 * 512], BF16)
    xld = LBuf(k, "xld", 2, [128, 2048], F32)
    xrst = SOut(k, "xrst", 3, [128, 512], F32)
    vw128 = lambda b: b[:, :].rearrange("p (c j) -> p c j", j=128)
    vw512 = lambda b: b[:, :].rearrange("p (c j) -> p c j", j=512)

    qtb_rel = ("VE", 0)
    for TB in range(2):
        qld = ("QTB", k.op("sync", "dma_start", wait=[qtb_rel],
                           out=qTB[:, :].rearrange("p (c j) -> p c j", j=512),
                           in_=qT[:, TB * 512:(TB + 1) * 512]
                           .rearrange("(c p) j -> p c j", p=128),
                           inc="QTB", incn=16))
        pst_free_pe = k.now("PE")
        for kc in range(64):
            kt, kld, ksl = ktl.load(
                "sync",
                kTf[:, kc * 128:(kc + 1) * 128].rearrange("(c p) j -> p c j", p=128),
                out_view=vw128)
            sl = kc % 2
            pe_last = None
            for c in range(16):
                wv = [kld, qld, pread[sl], ("CNG", CNG)] if c == 0 else []
                pe_last = k.op("tensor", "matmul", pool[sl][:, :],
                               kt[:, ts(c)], qTB[:, c * 512:(c + 1) * 512],
                               start=(c == 0), stop=(c == 15), wait=wv,
                               inc=("PE" if c == 15 else None))
            ktl.release(ksl, "PE", pe_last)
            ae = k.op("scalar", "activation", wait=[("PE", pe_last), ("PE", pst_free_pe)],
                      out=Pst[:, kc * 512:(kc + 1) * 512], in_=pool[sl][:, :],
                      func=AF.Exp)
            pread[sl] = ("AE", ae)
            rw_ = [("AE", ae)] + ([pread[2]] if kc == 0 else [])
            pe_r = k.op("tensor", "matmul", pool[2][:, :], onesb[:, :],
                        Pst[:, kc * 512:(kc + 1) * 512],
                        start=(kc == 0), stop=(kc == 63), wait=rw_,
                        inc=("PE" if kc == 63 else None))
        exp_last = k.now("AE")
        aeL = k.op("scalar", "activation", wait=[("PE", pe_r)], out=rtmp[:, :],
                   in_=pool[2][:, :], func=AF.Ln)
        pread[2] = ("AE", aeL)
        aeR = k.op("scalar", "activation", out=rinv[:, :], in_=rtmp[:, :],
                   func=AF.Exp, scale=-1.0)
        for db in range(16):
            vt, vld, vslot = vsl.load(
                "sync",
                vf[:, db * 128:(db + 1) * 128].rearrange("(kc p) j -> p kc j", p=128),
                out_view=vw128)
            sl = 3 + db % 2
            pe_last = None
            for kc in range(64):
                wv = [vld, pread[sl], ("AE", exp_last)] if kc == 0 else []
                pe_last = k.op("tensor", "matmul", pool[sl][:, :],
                               vt[:, ts(kc)], Pst[:, kc * 512:(kc + 1) * 512],
                               start=(kc == 0), stop=(kc == 63), wait=wv,
                               inc=("PE" if kc == 63 else None))
            vsl.release(vslot, "PE", pe_last)
            ve = k.op("vector", "tensor_tensor",
                      wait=[("PE", pe_last), ("PE", pst_free_pe), ("AE", aeR)],
                      out=OT[:, db * 512:(db + 1) * 512], in0=pool[sl][:, :],
                      in1=rinv[:, :], op=ALU.mult)
            pread[sl] = ("VE", ve)
        ot_ve = k.now("VE")
        qtb_rel = ("PE", k.now("PE"))
        for ob in range(4):
            wolds = []
            for hf in range(2):
                wb_, wld_, wslot_ = wol.load(
                    "gpsimd",
                    woutT[hf * 1024:(hf + 1) * 1024, ob * 512:(ob + 1) * 512]
                    .rearrange("(c p) j -> p c j", p=128),
                    out_view=vw512)
                wolds.append((wld_, wb_, wslot_))
            for tt in range(4):
                g = TB * 4 + tt
                sl = tt % 2
                pe_last = None
                for dc in range(16):
                    wv = []
                    if dc == 0:
                        wv = [wolds[0][0], pread[sl], ("VE", ot_ve)]
                    if dc == 8:
                        wv = [wolds[1][0]]
                    wbuf = wolds[0][1] if dc < 8 else wolds[1][1]
                    pe_last = k.op("tensor", "matmul", pool[sl][:, :],
                                   OT[:, dc * 512 + tt * 128: dc * 512 + (tt + 1) * 128],
                                   wbuf[:, (dc % 8) * 512:(dc % 8 + 1) * 512],
                                   start=(dc == 0), stop=(dc == 15), wait=wv,
                                   inc=("PE" if dc == 15 else None))
                xt, xldw, xslot = xld.load("sync", xA[ts(g), :])
                ve1 = k.op("vector", "tensor_tensor", wait=[("PE", pe_last), ("CN", CN)],
                           out=evt[:, :], in0=pool[sl][:, :],
                           in1=boutrep[:, ob * 512:(ob + 1) * 512], op=ALU.add)
                pread[sl] = ("VE", ve1)
                sb_, bw, bi = xrst.acquire()
                ve2 = k.op("vector", "tensor_tensor", wait=[xldw] + bw,
                           out=sb_[:, :], in0=evt[:, :],
                           in1=xt[:, ob * 512:(ob + 1) * 512], op=ALU.add)
                xld.release(xslot, "VE", ve2)
                xrst.flush(bi, xr_d[ts(g), ob * 512:(ob + 1) * 512], wait=[("VE", ve2)])
            wol.release(wolds[0][2], "PE", k.now("PE"))
            wol.release(wolds[1][2], "PE", k.now("PE"))
    k.pop_phase()

    k.push_phase()
    xnT = k.sbuf("xnT", [128, 16 * 1024], BF16)
    k.mksem("XNT")
    XNL = ("XNT", k.op("sync", "dma_start",
                       out=xnT[:, :].rearrange("p (c t) -> p c t", t=1024),
                       in_=xnT_in.rearrange("(c p) t -> p c t", p=128),
                       inc="XNT", incn=16))
    mr1 = k.sbuf("mr1", [128, 2 * 1024], BF16)
    mr2 = k.sbuf("mr2", [128, 2 * 1024], BF16)
    rw1rep = k.sbuf("rw1rep", [128, 1024], F32)
    rw2rep = k.sbuf("rw2rep", [128, 1024], F32)
    za11 = k.sbuf("za11", [128, 2 * 1024], BF16)
    za12 = k.sbuf("za12", [128, 2 * 1024], BF16)
    za31 = k.sbuf("za31", [128, 2 * 1024], BF16)
    za32 = k.sbuf("za32", [128, 2 * 1024], BF16)
    za2w1 = k.sbuf("za2w1", [128, 2 * 1024], BF16)
    za2w2 = k.sbuf("za2w2", [128, 2 * 1024], BF16)

    k.push_phase()
    WgTs = k.sbuf("WgTs", [128, 16 * 8], BF16)
    k.op("gpsimd", "dma_start", out=WgTs[:, :].rearrange("p (c j) -> p c j", j=8),
         in_=WgT.rearrange("(c p) j -> p c j", p=128), inc="CNG", incn=16)
    descr = k.sbuf("descr", [128, 8], F32)
    k.op("sync", "dma_start", out=descr[:, :], in_=desc.broadcast_to((128, 8)),
         inc="CN", incn=16)
    A1s = k.sbuf("A1s", [128, 16 * 256], BF16)
    A3s = k.sbuf("A3s", [128, 16 * 256], BF16)
    k.op("gpsimd", "dma_start", out=A1s[:, :].rearrange("p (c j) -> p c j", j=256),
         in_=A1fT.rearrange("(c p) j -> p c j", p=128), inc="CNG", incn=16)
    k.op("gpsimd", "dma_start", out=A3s[:, :].rearrange("p (c j) -> p c j", j=256),
         in_=A3fT.rearrange("(c p) j -> p c j", p=128), inc="CNG", incn=16)
    p1s = k.sbuf("p1s", [128, 256], BF16)
    p2s = k.sbuf("p2s", [128, 256], BF16)
    k.op("gpsimd", "dma_start", out=p1s[:, :], in_=pat1, inc="CNG", incn=16)
    k.op("gpsimd", "dma_start", out=p2s[:, :], in_=pat2, inc="CNG", incn=16)
    CN2, CNG2 = k.now("CN"), k.now("CNG")

    m1stg = k.sbuf("m1stg", [128, 1024], BF16)
    m2stg = k.sbuf("m2stg", [128, 1024], BF16)
    estg = k.sbuf("estg", [128, 1024], BF16)
    mt1stg = k.sbuf("mt1stg", [128, 1024], F32)
    mt2stg = k.sbuf("mt2stg", [128, 1024], F32)
    esb = k.sbuf("esb", [128, 8], F32)
    tq = k.sbuf("tq", [128, 48], F32)
    stk = k.sbuf("stk", [128, 24], BF16)
    num1 = k.sbuf("num1", [128, 1024], F32)
    num2 = k.sbuf("num2", [128, 1024], F32)
    rpool = [k.psum(f"rpool{i}", [128, 512], F32) for i in range(3)]
    plg = k.psum("plg", [128, 8], F32)
    tpr = TPose(k, "tpr", ident)
    k.op("vector", "memset", m1stg[:, :], 0.0)
    k.op("vector", "memset", m2stg[:, :], 0.0)
    k.op("vector", "memset", estg[:, :], 0.0)
    k.op("vector", "memset", mt1stg[:, :], 0.0)
    k.op("vector", "memset", mt2stg[:, :], 0.0)

    lg_read = ("VE", 0)
    for g in range(8):
        pe_last = None
        for c in range(16):
            wv = [XNL, ("CNG", CNG2), lg_read] if c == 0 else []
            pe_last = k.op("tensor", "matmul", plg[:, :],
                           xnT[:, c * 1024 + g * 128: c * 1024 + (g + 1) * 128],
                           WgTs[:, c * 8:(c + 1) * 8],
                           start=(c == 0), stop=(c == 15), wait=wv,
                           inc=("PE" if c == 15 else None))
        k.op("vector", "reduce_max", wait=[("PE", pe_last)], out=tq[:, 0:1],
             in_=plg[:, :], axis=AX.X, negate=True)
        v_nm = k.now("VE")
        ae = k.op("scalar", "activation", wait=[("VE", v_nm)],
                  out=esb[:, :], in_=plg[:, :], func=AF.Exp, bias=tq[:, 0:1])
        lg_read = ("AE", ae)
        k.op("vector", "reduce_max", wait=[("AE", ae), ("CN", CN2)],
             out=tq[:, 1:2], in_=esb[:, :], axis=AX.X)
        k.op("vector", "tensor_scalar", out=tq[:, 8:16], in0=esb[:, :],
             scalar1=tq[:, 1:2], scalar2=None, op0=ALU.is_ge)
        k.op("vector", "tensor_tensor", out=tq[:, 8:16], in0=tq[:, 8:16],
             in1=descr[:, :], op=ALU.mult)
        k.op("vector", "reduce_max", out=tq[:, 2:3], in_=tq[:, 8:16], axis=AX.X)
        k.op("vector", "tensor_scalar", out=tq[:, 16:24], in0=tq[:, 8:16],
             scalar1=tq[:, 2:3], scalar2=None, op0=ALU.is_ge)
        k.op("vector", "tensor_copy", out=stk[:, 0:8], in_=tq[:, 16:24])
        k.op("vector", "scalar_tensor_tensor", out=tq[:, 24:32], in0=tq[:, 16:24],
             scalar=1.0, in1=esb[:, :], op0=ALU.subtract, op1=ALU.mult)
        k.op("vector", "tensor_reduce", out=tq[:, 3:4], in_=tq[:, 24:32],
             axis=AX.X, op=ALU.min)
        k.op("vector", "tensor_scalar", out=tq[:, 32:40], in0=tq[:, 24:32],
             scalar1=tq[:, 3:4], scalar2=None, op0=ALU.is_le)
        k.op("vector", "tensor_tensor", out=tq[:, 32:40], in0=tq[:, 32:40],
             in1=descr[:, :], op=ALU.mult)
        k.op("vector", "reduce_max", out=tq[:, 4:5], in_=tq[:, 32:40], axis=AX.X)
        k.op("vector", "tensor_scalar", out=stk[:, 8:16], in0=tq[:, 32:40],
             scalar1=tq[:, 4:5], scalar2=None, op0=ALU.is_ge)
        ve_e = k.op("vector", "tensor_copy", out=stk[:, 16:24], in_=esb[:, :])
        tpr.emit(stk[:, 0:8], m1stg[0:8, ts(g)], extra_wait=[("VE", ve_e)], rows=8)
        tpr.emit(stk[:, 8:16], m2stg[0:8, ts(g)], rows=8)
        tpr.emit(stk[:, 16:24], estg[0:8, ts(g)], rows=8)
    ve1 = k.op("vector", "tensor_tensor", out=mt1stg[0:8, :], in0=m1stg[0:8, :],
               in1=estg[0:8, :], op=ALU.mult)
    ve2 = k.op("vector", "tensor_tensor", out=mt2stg[0:8, :], in0=m2stg[0:8, :],
               in1=estg[0:8, :], op=ALU.mult)
    for half in range(2):
        p1 = k.op("tensor", "matmul", rpool[0][:, :], onesfs[:, :],
                  mt1stg[:, half * 512:(half + 1) * 512], start=True, stop=True,
                  wait=[("VE", k.now("VE")), ("CN", CN)], inc="PE")
        p2 = k.op("tensor", "matmul", rpool[1][:, :], onesfs[:, :],
                  mt2stg[:, half * 512:(half + 1) * 512], start=True, stop=True,
                  inc="PE")
        k.op("vector", "tensor_copy", wait=[("PE", p1)],
             out=num1[:, half * 512:(half + 1) * 512], in_=rpool[0][:, :])
        k.op("vector", "tensor_copy", wait=[("PE", p2)],
             out=num2[:, half * 512:(half + 1) * 512], in_=rpool[1][:, :])
    ve3 = k.op("vector", "tensor_tensor", out=rw1rep[:, :], in0=num1[:, :],
               in1=num2[:, :], op=ALU.add)
    k.op("scalar", "activation", wait=[("VE", ve3)], out=rw2rep[:, :],
         in_=rw1rep[:, :], func=AF.Ln)
    aeEx = k.op("scalar", "activation", out=rw2rep[:, :], in_=rw2rep[:, :],
                func=AF.Exp, scale=-1.0)
    k.op("vector", "tensor_tensor", wait=[("AE", aeEx)], out=rw1rep[:, :],
         in0=num1[:, :], in1=rw2rep[:, :], op=ALU.mult)
    ve5 = k.op("vector", "tensor_tensor", out=rw2rep[:, :],
               in0=num2[:, :], in1=rw2rep[:, :], op=ALU.mult)
    for eb in range(2):
        for half in range(2):
            pm1 = k.op("tensor", "matmul", rpool[0][:, :], p1s[:, ts(eb)],
                       m1stg[:, half * 512:(half + 1) * 512], start=True, stop=True,
                       wait=[("VE", k.now("VE")), ("CNG", CNG2)], inc="PE")
            pm2 = k.op("tensor", "matmul", rpool[1][:, :], p2s[:, ts(eb)],
                       m2stg[:, half * 512:(half + 1) * 512], start=True, stop=True,
                       inc="PE")
            off = eb * 1024 + half * 512
            k.op("vector", "tensor_copy", wait=[("PE", pm1)],
                 out=mr1[:, off:off + 512], in_=rpool[0][:, :])
            k.op("vector", "tensor_copy", wait=[("PE", pm2)],
                 out=mr2[:, off:off + 512], in_=rpool[1][:, :])
    for (Asrc, z1, z2) in [(A1s, za11, za12), (A3s, za31, za32)]:
        for eb in range(2):
            for half in range(2):
                pe_last = None
                for c in range(16):
                    wv = [("VE", k.now("VE"))] if c == 0 else []
                    pe_last = k.op("tensor", "matmul", rpool[2][:, :],
                                   Asrc[:, c * 256 + eb * 128: c * 256 + (eb + 1) * 128],
                                   xnT[:, c * 1024 + half * 512: c * 1024 + (half + 1) * 512],
                                   start=(c == 0), stop=(c == 15), wait=wv,
                                   inc=("PE" if c == 15 else None))
                off = eb * 1024 + half * 512
                k.op("vector", "tensor_tensor", wait=[("PE", pe_last)],
                     out=z1[:, off:off + 512], in0=rpool[2][:, :],
                     in1=mr1[:, off:off + 512], op=ALU.mult)
                k.op("vector", "tensor_tensor",
                     out=z2[:, off:off + 512], in0=rpool[2][:, :],
                     in1=mr2[:, off:off + 512], op=ALU.mult)
    k.pop_phase()

    k.push_phase()
    fpool = [k.psum(f"fpool{i}", [128, 512], F32) for i in range(8)]
    fread = [("VE", 0)] * 8
    w13l = LBuf(k, "w13l", 3, [128, 16 * 256], BF16)
    bbal = LBuf(k, "bbal", 3, [128, 768], BF16)
    base1sb = k.sbuf("base1sb", [128, 512], F32)
    base3sb = k.sbuf("base3sb", [128, 512], F32)
    x1b = k.sbuf("x1b", [128, 512], F32)
    s1b = k.sbuf("s1b", [128, 512], BF16)
    x3b = k.sbuf("x3b", [128, 512], BF16)
    ub = k.sbuf("ub", [128, 512], BF16)
    tb1 = k.sbuf("tb1", [128, 512], BF16)
    tb2 = k.sbuf("tb2", [128, 512], BF16)
    zst = SOut(k, "zst", 2, [128, 512], BF16)
    vwc = lambda b: b[:, :].rearrange("p (c j) -> p c j", j=128)

    def wslab(src, fb):
        return src[:, fb * 128:(fb + 1) * 128].rearrange("(c p) j -> p c j", p=128)

    def bslab(src, fb):
        return src[:, fb * 128:(fb + 1) * 128].rearrange("(e p) j -> p e j", p=128)

    prev_silu = 0
    for TB in range(2):
        cols = slice(TB * 512, (TB + 1) * 512)
        pe_a2_tgts = []
        for fb in range(56):
            wbuf_, w13ld, w13i = w13l.load(
                "gpsimd", w13T0[:, fb * 256:(fb + 1) * 256]
                .rearrange("(c p) j -> p c j", p=128),
                out_view=lambda b: b[:, :].rearrange("p (c j) -> p c j", j=256))
            bbuf_, bbld, bbi = bbal.load("gpsimd", bba2[fb, :, :])
            w1ld = w3ld = w13ld
            b1ld = b3ld = a2ld = bbld
            w1b = wbuf_
            w3b = wbuf_
            b1b, b3b, a2b = bbuf_, bbuf_, bbuf_
            pe1 = None
            for c in range(16):
                wv = [w1ld, fread[0], XNL] if c == 0 else []
                pe1 = k.op("tensor", "matmul", fpool[0][:, :],
                           w1b[:, c * 256: c * 256 + 128],
                           xnT[:, c * 1024 + TB * 512: c * 1024 + TB * 512 + 512],
                           start=(c == 0), stop=(c == 15), wait=wv,
                           inc=("PE" if c == 15 else None))
            ve_b1 = k.op("vector", "tensor_copy", wait=[("PE", pe1)],
                         out=base1sb[:, :], in_=fpool[0][:, :])
            fread[0] = ("VE", ve_b1)
            pe3 = None
            for c in range(16):
                wv = [w3ld, fread[1]] if c == 0 else []
                pe3 = k.op("tensor", "matmul", fpool[1][:, :],
                           w3b[:, c * 256 + 128: c * 256 + 256],
                           xnT[:, c * 1024 + TB * 512: c * 1024 + TB * 512 + 512],
                           start=(c == 0), stop=(c == 15), wait=wv,
                           inc=("PE" if c == 15 else None))
            w13l.release(w13i, "PE", pe3)
            ve_b3 = k.op("vector", "tensor_copy", wait=[("PE", pe3)],
                         out=base3sb[:, :], in_=fpool[1][:, :])
            fread[1] = ("VE", ve_b3)
            for kk, (za1k, za3k, rwrep, tbk, sA, sB) in enumerate(
                    [(za11, za31, rw1rep, tb1, 4, 6), (za12, za32, rw2rep, tb2, 5, 7)]):
                peL = None
                for eb in range(2):
                    wv = [b1ld, fread[2]] if eb == 0 else []
                    peL = k.op("tensor", "matmul", fpool[2][:, :],
                               b1b[:, eb * 128:(eb + 1) * 128],
                               za1k[:, eb * 1024 + TB * 512: eb * 1024 + TB * 512 + 512],
                               start=(eb == 0), stop=(eb == 1), wait=wv,
                               inc=("PE" if eb == 1 else None))
                ve_x1 = k.op("vector", "scalar_tensor_tensor",
                             wait=[("PE", peL), ("AE", prev_silu)],
                             out=x1b[:, :], in0=fpool[2][:, :],
                             scalar=0.0, in1=base1sb[:, :], op0=ALU.bypass, op1=ALU.add)
                fread[2] = ("VE", ve_x1)
                ae_s = k.op("scalar", "activation", wait=[("VE", ve_x1)],
                            out=s1b[:, :], in_=x1b[:, :], func=AF.Silu)
                prev_silu = ae_s
                peL3 = None
                for eb in range(2):
                    wv = [b3ld, fread[3]] if eb == 0 else []
                    peL3 = k.op("tensor", "matmul", fpool[3][:, :],
                                b3b[:, 256 + eb * 128: 256 + (eb + 1) * 128],
                                za3k[:, eb * 1024 + TB * 512: eb * 1024 + TB * 512 + 512],
                                start=(eb == 0), stop=(eb == 1), wait=wv,
                                inc=("PE" if eb == 1 else None))
                ve_x3 = k.op("vector", "scalar_tensor_tensor",
                             wait=[("PE", peL3)], out=x3b[:, :], in0=fpool[3][:, :],
                             scalar=0.0, in1=base3sb[:, :], op0=ALU.bypass, op1=ALU.add)
                fread[3] = ("VE", ve_x3)
                ve_u = k.op("vector", "tensor_tensor", wait=[("AE", ae_s)],
                            out=ub[:, :], in0=s1b[:, :], in1=x3b[:, :], op=ALU.mult)
                ve_t = k.op("vector", "tensor_tensor",
                            wait=[("PE", pe_a2_tgts[-1] if pe_a2_tgts else 0),
                                  ("GP", k.now("GP"))],
                            out=tbk[:, :], in0=ub[:, :], in1=rwrep[:, cols],
                            op=ALU.mult)
                for eb in range(2):
                    slot = sA if eb == 0 else sB
                    wv = [a2ld, ("VE", ve_t)]
                    if fb == 0:
                        wv.append(fread[slot])
                    t_ = k.op("tensor", "matmul", fpool[slot][:, :],
                              a2b[:, 512 + eb * 128: 512 + (eb + 1) * 128],
                              tbk[:, :], start=(fb == 0), stop=(fb == 55), wait=wv,
                              inc="PE")
                    pe_a2_tgts.append(t_)
            pe_now = k.now("PE")
            bbal.release(bbi, "PE", pe_now)
            zb, zw, zi = zst.acquire()
            ve_z = k.op("gpsimd", "tensor_tensor",
                        wait=zw + [("VE", k.now("VE"))], out=zb[:, :],
                        in0=tb1[:, :], in1=tb2[:, :], op=ALU.add)
            zst.flush(zi, z_d[fb * 128:(fb + 1) * 128, cols], wait=[("GP", ve_z)])
        pe_fin = k.now("PE")
        for kk, (za2wk, mrk, sA, sB) in enumerate(
                [(za2w1, mr1, 4, 6), (za2w2, mr2, 5, 7)]):
            for eb in range(2):
                slot = sA if eb == 0 else sB
                off = eb * 1024 + TB * 512
                ve = k.op("vector", "tensor_tensor", wait=[("PE", pe_fin)],
                          out=za2wk[:, off:off + 512], in0=fpool[slot][:, :],
                          in1=mrk[:, off:off + 512], op=ALU.mult)
                fread[slot] = ("VE", ve)
    k.pop_phase()

    k.push_phase()
    dpool = [k.psum(f"dpool{i}", [128, 512], F32) for i in range(8)]
    dread = [("VE", 0)] * 8
    zl = LBuf(k, "zl", 2, [128, 1024], BF16)
    w2l = LBuf(k, "w2l", 2, [128, 512], BF16)
    xrl = LBuf(k, "xrl", 2, [128, 512], F32)
    B2s = k.sbuf("B2s", [128, 2 * 2048], BF16)
    k.mksem("B2L")
    b2ld = ("B2L", k.op("gpsimd", "dma_start",
                        out=B2s[:, :].rearrange("p (e j) -> p e j", j=2048),
                        in_=B2fT.rearrange("(e p) j -> p e j", p=128),
                        inc="B2L", incn=16))
    x1st = SOut(k, "x1st", 3, [128, 512], F32)
    for hb in range(4):
        hcols = slice(hb * 512, (hb + 1) * 512)
        mm_t = {}
        for fb in range(56):
            zb_, zld, zi = zl.load("sync", z_d[fb * 128:(fb + 1) * 128, :])
            wb_, wld, wi = w2l.load("gpsimd", w2T0[fb * 128:(fb + 1) * 128, hcols])
            for g in range(8):
                wv = [zld, wld] if True else []
                if fb == 0:
                    wv = wv + [dread[g]]
                mm_t[g] = k.op("tensor", "matmul", dpool[g][:, :],
                               zb_[:, ts(g)], wb_[:, :],
                               start=(fb == 0), stop=False, wait=wv, inc="PE")
            zl.release(zi, "PE", mm_t[7])
            w2l.release(wi, "PE", mm_t[7])
        for g in range(8):
            for kk, za2wk in enumerate([za2w1, za2w2]):
                for eb in range(2):
                    wv = [b2ld] if (hb == 0 and g == 0 and kk == 0 and eb == 0) else []
                    mm_t[g] = k.op("tensor", "matmul", dpool[g][:, :],
                                   za2wk[:, eb * 1024 + g * 128: eb * 1024 + (g + 1) * 128],
                                   B2s[:, eb * 2048 + hb * 512: eb * 2048 + hb * 512 + 512],
                                   start=False, stop=(kk == 1 and eb == 1),
                                   wait=wv, inc="PE")
            xr_, xrw, xri = xrl.load("sync", xr_d[ts(g), hcols])
            sb_, bw, bi = x1st.acquire()
            ve = k.op("vector", "tensor_tensor",
                      wait=[("PE", mm_t[g]), xrw] + bw,
                      out=sb_[:, :], in0=dpool[g][:, :], in1=xr_[:, :], op=ALU.add)
            dread[g] = ("VE", ve)
            xrl.release(xri, "VE", ve)
            x1st.flush(bi, x1_d[ts(g), hcols], wait=[("VE", ve)])
    k.pop_phase()
    k.pop_phase()

    k.push_phase()
    xn1T = k.sbuf("xn1T", [128, 16 * 1024], BF16)
    xc = k.sbuf("xc", [128, 2048], F32)
    sq = k.sbuf("sq", [128, 2048], F32)
    st1 = k.sbuf("st1", [128, 8], F32)
    vst = k.sbuf("vst", [128, 2048], BF16)
    vTs = k.sbuf("vTs", [128, 16 * 1024], BF16)
    sqb = k.sbuf("sqb", [128, 512], F32)
    rstb = k.sbuf("rstb", [128, 512], F32)
    x1l = LBuf(k, "x1l", 2, [128, 2048], F32)
    wkl = LBuf(k, "wkl", 2, [128, 16 * 128], BF16)
    kst = SOut(k, "kst", 2, [128, 512], BF16)
    lpool = [k.psum(f"lpool{i}", [128, 512], F32) for i in range(3)]
    lread = [("VE", 0)] * 3
    tpl = TPose(k, "tpl", ident)
    last_tp_pe = 0
    for g in range(8):
        xt, xldw, xsl = x1l.load("sync", x1_d[ts(g), :])
        k.op("vector", "reduce_sum", wait=[xldw], out=st1[:, 0:1],
             in_=xt[:, :], axis=AX.X)
        k.op("vector", "tensor_scalar", out=st1[:, 1:2], in0=st1[:, 0:1],
             scalar1=1.0 / 2048.0, scalar2=None, op0=ALU.mult)
        k.op("vector", "tensor_scalar", out=xc[:, :], in0=xt[:, :],
             scalar1=st1[:, 1:2], scalar2=None, op0=ALU.subtract)
        ve_rel = k.op("vector", "tensor_tensor", out=sq[:, :], in0=xc[:, :],
                      in1=xc[:, :], op=ALU.mult)
        x1l.release(xsl, "VE", ve_rel)
        k.op("vector", "reduce_sum", out=st1[:, 2:3], in_=sq[:, :], axis=AX.X)
        v_ve = k.op("vector", "tensor_scalar", out=st1[:, 3:4], in0=st1[:, 2:3],
                    scalar1=1.0 / 2048.0, scalar2=1e-5, op0=ALU.mult, op1=ALU.add)
        k.op("scalar", "activation", wait=[("VE", v_ve)], out=st1[:, 4:5],
             in_=st1[:, 3:4], func=AF.Ln)
        a2 = k.op("scalar", "activation", out=st1[:, 5:6], in_=st1[:, 4:5],
                  func=AF.Exp, scale=-0.5)
        ve2 = k.op("vector", "tensor_scalar", wait=[("AE", a2), ("PE", last_tp_pe)],
                   out=vst[:, :], in0=xc[:, :], scalar1=st1[:, 5:6], scalar2=None,
                   op0=ALU.mult)
        for c in range(16):
            pe, _ = tpl.emit(vst[:, ts(c)],
                             xn1T[:, c * 1024 + g * 128: c * 1024 + (g + 1) * 128],
                             extra_wait=[("VE", ve2)])
            last_tp_pe = pe
    xn1_ve = k.now("VE")
    for which, wT in [("k", wkT), ("v", wvT)]:
        for ob in range(16):
            _, wld, wi = wkl.load("gpsimd", wslab(wT, ob), out_view=vwc)
            for nh in range(2):
                sl = nh
                pe_last = None
                for c in range(16):
                    wv = [wld, lread[sl], ("VE", xn1_ve)] if c == 0 else []
                    pe_last = k.op("tensor", "matmul", lpool[sl][:, :],
                                   wkl.bufs[wi][:, ts(c)],
                                   xn1T[:, c * 1024 + nh * 512: c * 1024 + nh * 512 + 512],
                                   start=(c == 0), stop=(c == 15), wait=wv,
                                   inc=("PE" if c == 15 else None))
                if which == "k":
                    ae_sq = k.op("scalar", "activation", wait=[("PE", pe_last)],
                                 out=sqb[:, :], in_=lpool[sl][:, :], func=AF.Square)
                    pe_ms = k.op("tensor", "matmul", lpool[2][:, :], onesfs[:, :],
                                 sqb[:, :], start=True, stop=True,
                                 wait=[("AE", ae_sq), lread[2], ("CN", CN)], inc="PE")
                    v_ts = k.op("vector", "tensor_scalar", wait=[("PE", pe_ms)],
                                out=rstb[:, :], in0=lpool[2][:, :],
                                scalar1=1.0 / 128.0, scalar2=RMS_EPS,
                                op0=ALU.mult, op1=ALU.add)
                    lread[2] = ("VE", v_ts)
                    k.op("scalar", "activation", wait=[("VE", v_ts)], out=rstb[:, :],
                         in_=rstb[:, :], func=AF.Ln)
                    ae_r = k.op("scalar", "activation", out=rstb[:, :], in_=rstb[:, :],
                                func=AF.Exp, scale=-0.5)
                    sb_, bw, bi = kst.acquire()
                    ve = k.op("vector", "tensor_tensor", wait=[("AE", ae_r)] + bw,
                              out=sb_[:, :], in0=lpool[sl][:, :], in1=rstb[:, :],
                              op=ALU.mult)
                    lread[sl] = ("VE", ve)
                    kst.flush(bi, kL1_d[ts(ob), nh * 512:(nh + 1) * 512],
                              wait=[("VE", ve)])
                else:
                    ve = k.op("vector", "tensor_copy", wait=[("PE", pe_last)],
                              out=vTs[:, ob * 1024 + nh * 512: ob * 1024 + nh * 512 + 512],
                              in_=lpool[sl][:, :])
                    lread[sl] = ("VE", ve)
            wkl.release(wi, "PE", k.now("PE"))
    ve_v = k.now("VE")
    k.mksem("STV")
    stv = 0
    for g in range(8):
        for c in range(16):
            tpl.emit(vTs[:, c * 1024 + g * 128: c * 1024 + (g + 1) * 128],
                     vst[:, ts(c)],
                     extra_wait=[("VE", ve_v)],
                     copy_extra_wait=[("STV", stv * 16)] if (c == 0 and stv) else [])
        stv += 1
        t_st = k.op("sync", "dma_start", out=vL1_d[ts(g), :], in_=vst[:, :],
                    wait=[("VE", k.now("VE"))], inc="STV", incn=16)
    k._note_final("STV", t_st)
    k.pop_phase()

    k.push_phase()
    wsz = k.sbuf("wsz", [128, 4], F32)
    th = k.sbuf("th", [128, 4], F32)
    nth = k.sbuf("nth", [128, 4], F32)
    ubuf = k.sbuf("ubuf", [128, 5632], BF16)
    vbuf = k.sbuf("vbuf", [128, 5632], BF16)
    wshl = LBuf(k, "wshl", 2, [128, 5632], F32)
    tst = SOut(k, "tst", 2, [128, 5632], BF16)
    ppool = k.psum("ppool", [128, 4], F32)
    k.mksem("WSL")
    ve_mz = k.op("vector", "memset", wsz[:, :], 0.0)
    wsld = ("WSL", k.op("sync", "dma_start", out=wsz[0:8, :], in_=wsall,
                        wait=[("VE", ve_mz)], inc="WSL", incn=16))
    pe_ws = k.op("tensor", "matmul", ppool[:, :], onesfs[:, :], wsz[:, :],
                 wait=[wsld, ("VE", ve_mz), ("CN", CN)], start=True, stop=True,
                 inc="PE")
    k.op("vector", "tensor_scalar", wait=[("PE", pe_ws)], out=th[:, :],
         in0=ppool[:, :], scalar1=NW2, scalar2=1e-5, op0=ALU.mult, op1=ALU.max)
    k.op("vector", "tensor_scalar", out=th[:, :], in0=th[:, :],
         scalar1=0.5, scalar2=None, op0=ALU.mult)
    k.op("vector", "tensor_scalar", out=nth[:, :], in0=th[:, :],
         scalar1=-1.0, scalar2=None, op0=ALU.mult)
    for (wap, nch, cols, j, td) in [(w1s, 2, 5632, 0, t1_d), (w2s, 6, 2048, 1, t2_d),
                                    (w3s, 2, 5632, 2, t3_d)]:
        for ch in range(nch):
            wb_, wld, wi = wshl.load("sync", wap[ts(ch), 0:cols],
                                     out_view=lambda b, cols=cols: b[:, 0:cols])
            k.op("vector", "tensor_scalar", wait=[wld], out=ubuf[:, 0:cols],
                 in0=wb_[:, 0:cols], scalar1=th[:, j:j+1], scalar2=None, op0=ALU.is_gt)
            ve_w = k.op("vector", "tensor_scalar", out=vbuf[:, 0:cols],
                        in0=wb_[:, 0:cols], scalar1=nth[:, j:j+1], scalar2=None,
                        op0=ALU.is_ge)
            wshl.release(wi, "VE", ve_w)
            sb_, bw, bi = tst.acquire()
            ve_t = k.op("vector", "tensor_tensor", wait=bw, out=sb_[:, 0:cols],
                        in0=ubuf[:, 0:cols], in1=vbuf[:, 0:cols], op=ALU.add)
            tst.flush(bi, td[ts(ch), 0:cols], wait=[("VE", ve_t)],
                      src_view=lambda b, cols=cols: b[:, 0:cols])
    k.pop_phase()
    k.finish()
    return k.build()


def prep_BCD(inp, resA):
    x = np.ascontiguousarray(inp["L0_attn_hidden_states"].reshape(8192, 2048))
    kTf = np.ascontiguousarray(np.concatenate([r["kT_d"] for r in resA], axis=1))
    vf = np.ascontiguousarray(np.concatenate([r["v_d"] for r in resA], axis=0))
    wsall = np.ascontiguousarray(np.stack([r["wsums_d"][0] for r in resA]))
    woutT = np.ascontiguousarray(inp["L0_attn_w_out"].T)
    bout = np.ascontiguousarray(inp["L0_attn_b_out"].reshape(1, 2048))
    identb = np.eye(128, dtype=np.float32)
    onesf = np.ones((128, 128), dtype=np.float32)
    desc = np.arange(8, 0, -1, dtype=np.float32).reshape(1, 8)
    WgT = np.ascontiguousarray(inp["L0_ffn_Wg"].T)
    A1fT = np.ascontiguousarray(inp["L0_ffn_A1"].reshape(256, 2048).T)
    A3fT = np.ascontiguousarray(inp["L0_ffn_A3"].reshape(256, 2048).T)
    B1fT = np.ascontiguousarray(inp["L0_ffn_B1"].transpose(0, 2, 1).reshape(256, 7168))
    B3fT = np.ascontiguousarray(inp["L0_ffn_B3"].transpose(0, 2, 1).reshape(256, 7168))
    A2fT = np.ascontiguousarray(inp["L0_ffn_A2"].reshape(256, 7168).T)
    B2fT = np.ascontiguousarray(inp["L0_ffn_B2"].transpose(0, 2, 1).reshape(256, 2048))
    w1T0 = np.ascontiguousarray(inp["L0_ffn_W1"].T)
    w3T0 = np.ascontiguousarray(inp["L0_ffn_W3"].T)
    w2T0 = np.ascontiguousarray(inp["L0_ffn_W2"].T)
    w13T0 = np.empty((2048, 56 * 256), dtype=np.float32)
    for fb in range(56):
        w13T0[:, fb * 256:fb * 256 + 128] = w1T0[:, fb * 128:(fb + 1) * 128]
        w13T0[:, fb * 256 + 128:(fb + 1) * 256] = w3T0[:, fb * 128:(fb + 1) * 128]
    bba2 = np.empty((56, 128, 768), dtype=np.float32)
    for fb in range(56):
        bba2[fb, :, 0:256] = B1fT[:, fb * 128:(fb + 1) * 128].reshape(2, 128, 128).transpose(1, 0, 2).reshape(128, 256)
        bba2[fb, :, 256:512] = B3fT[:, fb * 128:(fb + 1) * 128].reshape(2, 128, 128).transpose(1, 0, 2).reshape(128, 256)
        bba2[fb, :, 512:768] = A2fT[fb * 128:(fb + 1) * 128, :]
    pat1 = np.zeros((128, 256), dtype=np.float32)
    pat2 = np.zeros((128, 256), dtype=np.float32)
    for e in range(8):
        pat1[e, e * 32:(e + 1) * 32] = 1.0
        pat2[e, e * 32:(e + 1) * 32] = 1.0
    wkT = np.ascontiguousarray(inp["L1_attn_wk"].T)
    wvT = np.ascontiguousarray(inp["L1_attn_wv"].T)
    w1T = np.ascontiguousarray(inp["L1_ffn_w1"].T)
    w2T = np.ascontiguousarray(inp["L1_ffn_w2"].T)
    w3T = np.ascontiguousarray(inp["L1_ffn_w3"].T)
    w2Tp = np.zeros((6144, 2048), dtype=np.float32)
    w2Tp[:5632] = w2T
    maps = []
    for c in range(8):
        maps.append(dict(
            xA=np.ascontiguousarray(x[c * 1024:(c + 1) * 1024]),
            qT=np.ascontiguousarray(resA[c]["qT_d"]),
            kTf=kTf, vf=vf,
            xnT_in=np.ascontiguousarray(resA[c]["xnT_d"]),
            woutT=woutT, bout=bout, identb=identb, onesf=onesf, desc=desc,
            WgT=WgT, A1fT=A1fT, A3fT=A3fT, B1fT=B1fT, B3fT=B3fT,
            A2fT=A2fT, B2fT=B2fT, w13T0=w13T0, bba2=bba2, w2T0=w2T0,
            pat1=pat1, pat2=pat2, wkT=wkT, wvT=wvT, wsall=wsall,
            w1s=np.ascontiguousarray(w1T[c * 256:(c + 1) * 256]),
            w2s=np.ascontiguousarray(w2Tp[c * 768:(c + 1) * 768]),
            w3s=np.ascontiguousarray(w3T[c * 256:(c + 1) * 256]),
        ))
    return maps


def build_launch_EF(debug=False):
    k = K("EF")
    x1f = k.din("x1f", [1024, 2048], F32)
    knf = k.din("knf", [2048, 8192], BF16)
    vnf = k.din("vnf", [8192, 2048], BF16)
    wqT = k.din("wqT", [2048, 2048], F32)
    woT = k.din("woT", [2048, 2048], F32)
    lqk = k.din("lqk", [4, 256], F32)
    m01_d = k.din("m01_d", [8, 8, 128, 128], BF16)
    identb = k.din("identb", [128, 128], F32)
    onesf = k.din("onesf", [128, 128], F32)
    t1f = k.din("t1f", [2048, 5632], BF16)
    t3f = k.din("t3f", [2048, 5632], BF16)
    t2f = k.din("t2f", [5632, 2048], BF16)
    wsall = k.din("wsall", [8, 4], F32)
    out_d = k.dout("out_d", [1024, 2048], F32)
    dmk = k.dout if debug else k.dtmp
    x1r_d = dmk("x1r_d", [1024, 2048], F32)
    h_d = dmk("h_d", [1024, 5632], F32)
    hqT_d = dmk("hqT_d", [5632, 1024], BF16)
    lam_d = k.dtmp("lam_d", [1, 1], F32)
    sc_d = k.dtmp("sc_d", [1024, 4], F32)

    ident = k.sbuf("ident", [128, 128], BF16)
    onesfs = k.sbuf("onesfs", [128, 128], F32)
    onesb = k.sbuf("onesb", [128, 128], BF16)
    k.mksem("CN", "CNG")
    k.op("gpsimd", "dma_start", out=ident[:, :], in_=identb, inc="CNG", incn=16)
    k.op("gpsimd", "dma_start", out=onesb[:, :], in_=onesf, inc="CNG", incn=16)
    k.op("sync", "dma_start", out=onesfs[:, :], in_=onesf, inc="CN", incn=16)
    CN, CNG = k.now("CN"), k.now("CNG")
    vwc = lambda b: b[:, :].rearrange("p (c j) -> p c j", j=128)

    def wslab(src, ob):
        return src[:, ob * 128:(ob + 1) * 128].rearrange("(c p) j -> p c j", p=128)

    qnT = k.sbuf("qnT", [128, 16 * 1024], BF16)
    xqT_d = k.dtmp("xqT_d", [2048, 1024], BF16)
    sstat = k.sbuf("sstat", [128, 8 * 8], F32)
    lamn = k.sbuf("lamn", [128, 1], F32)
    k.push_phase()
    xn1T = k.sbuf("xn1T", [128, 16 * 1024], BF16)
    ltmp = k.sbuf("ltmp", [4, 256], F32)
    lsc = k.sbuf("lsc", [1, 8], F32)
    xqtile = k.sbuf("xqtile", [128, 16 * 128], BF16)
    k.mksem("XQS")
    k.mksem("LQK", "LAM")
    lrow = k.sbuf("lrow", [1, 1024], F32)
    k.op("sync", "dma_start", out=lrow[:, :],
         in_=lqk.rearrange("a b -> (a b)").unsqueeze(0), inc="LQK", incn=16)
    ve = k.op("vector", "tensor_tensor", wait=[("LQK", k.now("LQK"))],
              out=ltmp[0:1, 0:256], in0=lrow[:, 0:256], in1=lrow[:, 256:512],
              op=ALU.mult)
    k.op("vector", "reduce_sum", out=lsc[:, 0:1], in_=ltmp[0:1, 0:256], axis=AX.X)
    k.op("vector", "tensor_tensor", out=ltmp[0:1, 0:256], in0=lrow[:, 512:768],
         in1=lrow[:, 768:1024], op=ALU.mult)
    v_d2 = k.op("vector", "reduce_sum", out=lsc[:, 1:2], in_=ltmp[0:1, 0:256],
                axis=AX.X)
    ae1 = k.op("scalar", "activation", wait=[("VE", v_d2)], out=lsc[:, 2:3],
               in_=lsc[:, 0:1], func=AF.Exp)
    ae2 = k.op("scalar", "activation", out=lsc[:, 3:4], in_=lsc[:, 1:2], func=AF.Exp)
    k.op("vector", "tensor_tensor", wait=[("AE", ae2)], out=lsc[:, 4:5],
         in0=lsc[:, 3:4], in1=lsc[:, 2:3], op=ALU.subtract)
    ve_l = k.op("vector", "tensor_scalar", out=lsc[:, 5:6], in0=lsc[:, 4:5],
                scalar1=-0.8, scalar2=None, op0=ALU.add)
    st_l = k.op("sync", "dma_start", out=lam_d, in_=lsc[0:1, 5:6],
                wait=[("VE", ve_l)], inc="LAM", incn=16)
    k.mksem("LAM2")
    k.op("sync", "dma_start", wait=[("LAM", st_l)],
         out=lamn[:, :], in_=lam_d.broadcast_to((128, 1)), inc="LAM2", incn=16)

    xc = k.sbuf("xc", [128, 2048], F32)
    sq = k.sbuf("sq", [128, 2048], F32)
    st1 = k.sbuf("st1", [128, 16], F32)
    vst = k.sbuf("vst", [128, 2048], BF16)
    xnf = k.sbuf("xnf", [128, 2048], F32)
    ci32 = k.sbuf("ci32", [128, 2048], I32)
    cf32 = k.sbuf("cf32", [128, 2048], F32)
    dbuf = k.sbuf("dbuf", [128, 2048], F32)
    c1b = k.sbuf("c1b", [128, 2048], F32)
    qtm = k.sbuf("qtm", [128, 2048], BF16)
    x1l = LBuf(k, "x1l", 2, [128, 2048], F32)
    tpp = TPose(k, "tpp", ident)
    wsz = k.sbuf("wsz", [128, 4], F32)
    msw = k.sbuf("msw", [128, 4], F32)
    k.mksem("WSL")
    ve_mz = k.op("vector", "memset", wsz[:, :], 0.0)
    wsld = ("WSL", k.op("sync", "dma_start", out=wsz[0:8, :], in_=wsall,
                        wait=[("VE", ve_mz)], inc="WSL", incn=16))
    ppool0 = k.psum("ppool0", [128, 4], F32)
    pe_ws = k.op("tensor", "matmul", ppool0[:, :], onesfs[:, :], wsz[:, :],
                 wait=[wsld, ("VE", ve_mz), ("CN", CN)], start=True, stop=True,
                 inc="PE")
    k.op("vector", "tensor_scalar", wait=[("PE", pe_ws)], out=msw[:, :],
         in0=ppool0[:, :], scalar1=NW2, scalar2=1e-5, op0=ALU.mult, op1=ALU.max)
    LN127 = float(np.log(np.float32(127.0)))
    ln127b = k.sbuf("ln127b", [128, 1], F32)
    k.op("vector", "memset", ln127b[:, :], LN127)
    last_tp_pe = 0
    for j in range(8):
        xt, xldw, xsl = x1l.load("sync", x1f[ts(j), :])
        k.op("vector", "reduce_sum", wait=[xldw], out=st1[:, 0:1],
             in_=xt[:, :], axis=AX.X)
        k.op("vector", "tensor_scalar", out=st1[:, 1:2], in0=st1[:, 0:1],
             scalar1=1.0 / 2048.0, scalar2=None, op0=ALU.mult)
        k.op("vector", "tensor_scalar", out=xc[:, :], in0=xt[:, :],
             scalar1=st1[:, 1:2], scalar2=None, op0=ALU.subtract)
        ve_rel = k.op("vector", "tensor_tensor", out=sq[:, :], in0=xc[:, :],
                      in1=xc[:, :], op=ALU.mult)
        x1l.release(xsl, "VE", ve_rel)
        k.op("vector", "reduce_sum", out=st1[:, 2:3], in_=sq[:, :], axis=AX.X)
        v_ve = k.op("vector", "tensor_scalar", out=st1[:, 3:4], in0=st1[:, 2:3],
                    scalar1=1.0 / 2048.0, scalar2=1e-5, op0=ALU.mult, op1=ALU.add)
        k.op("scalar", "activation", wait=[("VE", v_ve)], out=st1[:, 4:5],
             in_=st1[:, 3:4], func=AF.Ln)
        a2 = k.op("scalar", "activation", out=st1[:, 5:6], in_=st1[:, 4:5],
                  func=AF.Exp, scale=-0.5)
        ve_xn = k.op("vector", "tensor_scalar", wait=[("AE", a2)],
                     out=xnf[:, :], in0=xc[:, :], scalar1=st1[:, 5:6], scalar2=None,
                     op0=ALU.mult)
        ve2 = k.op("vector", "tensor_copy", wait=[("PE", last_tp_pe)],
                   out=vst[:, :], in_=xnf[:, :])
        for c in range(16):
            pe, _ = tpp.emit(vst[:, ts(c)],
                             xn1T[:, c * 1024 + j * 128: c * 1024 + (j + 1) * 128],
                             extra_wait=[("VE", ve2), ("CNG", CNG)])
            last_tp_pe = pe
        k.op("vector", "tensor_reduce", out=st1[:, 6:7], in_=xnf[:, :], axis=AX.X,
             op=ALU.max, apply_absolute_value=True)
        v_am = k.op("vector", "tensor_scalar", out=st1[:, 7:8], in0=st1[:, 6:7],
                    scalar1=1e-5, scalar2=None, op0=ALU.max)
        k.op("scalar", "activation", wait=[("VE", v_am)], out=st1[:, 8:9],
             in_=st1[:, 7:8], func=AF.Ln)
        a_sx = k.op("scalar", "activation", out=st1[:, 9:10], in_=st1[:, 8:9],
                    func=AF.Exp, scale=-1.0, bias=ln127b[:, 0:1])
        k.op("vector", "tensor_scalar", wait=[("AE", a_sx)], out=st1[:, 10:11],
             in0=st1[:, 7:8], scalar1=1.0 / 127.0, scalar2=None, op0=ALU.mult)
        k.op("vector", "tensor_scalar", out=xnf[:, :], in0=xnf[:, :],
             scalar1=st1[:, 9:10], scalar2=None, op0=ALU.mult)
        k.op("vector", "tensor_copy", out=ci32[:, :], in_=xnf[:, :])
        k.op("vector", "tensor_copy", out=cf32[:, :], in_=ci32[:, :])
        k.op("vector", "tensor_tensor", out=dbuf[:, :], in0=xnf[:, :],
             in1=cf32[:, :], op=ALU.subtract)
        k.op("vector", "scalar_tensor_tensor", out=c1b[:, :], in0=dbuf[:, :],
             scalar=0.5, in1=cf32[:, :], op0=ALU.is_gt, op1=ALU.add)
        ve_q = k.op("vector", "scalar_tensor_tensor", out=qtm[:, :], in0=dbuf[:, :],
                    scalar=-0.5, in1=c1b[:, :], op0=ALU.is_lt, op1=ALU.subtract)
        k.op("vector", "reduce_sum", out=sstat[:, j * 8 + 0: j * 8 + 1],
             in_=qtm[:, :], axis=AX.X)
        k.op("vector", "tensor_tensor", out=sstat[:, j * 8 + 1: j * 8 + 2],
             in0=st1[:, 10:11], in1=msw[:, 0:1], op=ALU.mult)
        k.op("vector", "tensor_scalar", out=sstat[:, j * 8 + 1: j * 8 + 2],
             in0=sstat[:, j * 8 + 1: j * 8 + 2], scalar1=-1.0, scalar2=None,
             op0=ALU.mult)
        k.op("vector", "tensor_tensor", out=sstat[:, j * 8 + 2: j * 8 + 3],
             in0=st1[:, 10:11], in1=msw[:, 2:3], op=ALU.mult)
        k.op("vector", "tensor_scalar", out=sstat[:, j * 8 + 2: j * 8 + 3],
             in0=sstat[:, j * 8 + 2: j * 8 + 3], scalar1=-1.0, scalar2=None,
             op0=ALU.mult)
        ve_sx = k.op("vector", "tensor_copy", out=sstat[:, j * 8 + 3: j * 8 + 4],
                     in_=st1[:, 10:11])
        for c in range(16):
            pe, _ = tpp.emit(qtm[:, ts(c)], xqtile[:, ts(c)],
                             extra_wait=[("VE", ve_q)],
                             copy_extra_wait=[("XQS", j * 16)] if c == 0 else [])
            last_tp_pe = pe
        k.op("sync", "dma_start", wait=[("VE", k.now("VE"))],
             out=xqT_d[:, j * 128:(j + 1) * 128].rearrange("(c p) o -> p c o", p=128),
             in_=xqtile[:, :].rearrange("p (c o) -> p c o", o=128),
             inc="XQS", incn=16)
    xn1_ve = k.now("VE")
    wql = LBuf(k, "wql", 2, [128, 16 * 128], BF16)
    sqb = k.sbuf("sqb", [128, 512], F32)
    rstb = k.sbuf("rstb", [128, 512], F32)
    qpool = [k.psum(f"qpool{i}", [128, 512], F32) for i in range(3)]
    qread = [("VE", 0)] * 3
    for ob in range(16):
        _, wld, wi = wql.load("gpsimd", wslab(wqT, ob), out_view=vwc)
        for nh in range(2):
            sl = nh
            pe_last = None
            for c in range(16):
                wv = [wld, qread[sl], ("VE", xn1_ve)] if c == 0 else []
                pe_last = k.op("tensor", "matmul", qpool[sl][:, :],
                               wql.bufs[wi][:, ts(c)],
                               xn1T[:, c * 1024 + nh * 512: c * 1024 + nh * 512 + 512],
                               start=(c == 0), stop=(c == 15), wait=wv,
                               inc=("PE" if c == 15 else None))
            ae_sq = k.op("scalar", "activation", wait=[("PE", pe_last)],
                         out=sqb[:, :], in_=qpool[sl][:, :], func=AF.Square)
            pe_ms = k.op("tensor", "matmul", qpool[2][:, :], onesfs[:, :],
                         sqb[:, :], start=True, stop=True,
                         wait=[("AE", ae_sq), qread[2], ("CN", CN)], inc="PE")
            v_ts = k.op("vector", "tensor_scalar", wait=[("PE", pe_ms)],
                        out=rstb[:, :], in0=qpool[2][:, :],
                        scalar1=1.0 / 128.0, scalar2=RMS_EPS,
                        op0=ALU.mult, op1=ALU.add)
            qread[2] = ("VE", v_ts)
            k.op("scalar", "activation", wait=[("VE", v_ts)], out=rstb[:, :],
                 in_=rstb[:, :], func=AF.Ln)
            ae_r = k.op("scalar", "activation", out=rstb[:, :], in_=rstb[:, :],
                        func=AF.Exp, scale=-0.5)
            ve = k.op("vector", "scalar_tensor_tensor", wait=[("AE", ae_r)],
                      out=qnT[:, ob * 1024 + nh * 512: ob * 1024 + nh * 512 + 512],
                      in0=qpool[sl][:, :], scalar=float(1.0 / np.sqrt(np.float32(128.0))),
                      in1=rstb[:, :], op0=ALU.mult, op1=ALU.mult)
            qread[sl] = ("VE", ve)
        wql.release(wi, "PE", k.now("PE"))
    k.pop_phase()

    k.push_phase()
    PstB = [k.sbuf(f"Pst1_{i}", [128, 64 * 256], BF16) for i in range(2)]
    pst_rel = [("PE", 0), ("PE", 0)]
    mbias = k.sbuf("mbias", [128, 8 * 128], BF16)
    k.mksem("MBL")
    rtmp = k.sbuf("rtmp1", [128, 256], F32)
    rinv12 = k.sbuf("rinv12", [128, 256], F32)
    r2l = k.sbuf("r2l", [128, 128], F32)
    ya = k.sbuf("ya", [128, 128], F32)
    yb = k.sbuf("yb", [128, 128], F32)
    yT = k.sbuf("yT", [128, 16 * 128], BF16)
    ksl = LBuf(k, "ksl1", 2, [128, 64 * 128], BF16)
    vsl = LBuf(k, "vsl1", 1, [128, 64 * 256], BF16)
    wol = LBuf(k, "wol1", 2, [128, 8 * 512], BF16)
    x1fl = LBuf(k, "x1fl", 2, [128, 512], F32)
    xrst = SOut(k, "xrst1", 3, [128, 512], F32)
    sA = k.psum("sA", [128, 1024], F32)
    sB = k.psum("sB", [128, 1024], F32)
    spool = [sA, sB]
    prs = k.psum("prs", [128, 256], F32)
    pc1 = k.psum("pc1", [128, 256], F32)
    pc2 = k.psum("pc2", [128, 256], F32)
    ppj = k.psum("ppj", [128, 512], F32)
    sread = [("AE", 0), ("AE", 0)]
    rsread = ("AE", 0)
    cread = [("VE", 0), ("VE", 0)]
    pjread = ("VE", 0)
    mb_t = 0
    for j in range(8):
        nkt = 8 * j + 8
        mld = ("MBL", k.op("sync", "dma_start",
                           wait=[("VE", k.now("VE"))],
                           out=mbias[:, :].rearrange("p (i q) -> p i q", q=128),
                           in_=m01_d[j, :, :, :].rearrange("i p q -> p i q"),
                           inc="MBL", incn=16))
        for h in range(8):
            pb = (j * 8 + h) % 2
            Pst = PstB[pb]
            kbufs = []
            for s in range(2):
                kb_, kld_, ki_ = ksl.load(
                    "sync",
                    knf[(h * 2 + s) * 128:(h * 2 + s + 1) * 128, 0:nkt * 128]
                    .rearrange("p (kt j2) -> p kt j2", j2=128),
                    out_view=lambda b: b[:, 0:nkt * 128]
                    .rearrange("p (kt j2) -> p kt j2", j2=128))
                kbufs.append((kb_, kld_, ki_))
            vb_, vld_, vi_ = vsl.load(
                "sync",
                vnf[0:nkt * 128, h * 256:(h + 1) * 256]
                .rearrange("(kt p) d -> p kt d", p=128),
                out_view=lambda b: b[:, 0:nkt * 256]
                .rearrange("p (kt d) -> p kt d", d=256))
            for kt0 in range(0, nkt, 4):
                sl = (kt0 // 4) % 2
                pe_l = None
                for dkt in range(min(4, nkt - kt0)):
                    kt = kt0 + dkt
                    for s in range(2):
                        wv = []
                        if dkt == 0 and s == 0:
                            wv = [kbufs[0][1], kbufs[1][1], sread[sl],
                                  pst_rel[pb] if kt0 == 0 else ("PE", 0)]
                        pe_l = k.op("tensor", "matmul",
                                    spool[sl][:, dkt * 256 + s * 128: dkt * 256 + (s + 1) * 128],
                                    kbufs[s][0][:, kt * 128:(kt + 1) * 128],
                                    qnT[:, (h * 2 + s) * 1024 + j * 128:
                                        (h * 2 + s) * 1024 + (j + 1) * 128],
                                    start=True, stop=True, wait=wv, inc="PE")
                nb = min(4, nkt - kt0)
                ae = k.op("scalar", "activation", wait=[("PE", pe_l)],
                          out=Pst[:, kt0 * 256:(kt0 + nb) * 256],
                          in_=spool[sl][:, 0:nb * 256], func=AF.Exp)
                sread[sl] = ("AE", ae)
            ve_m = None
            for i in range(8):
                kt = 8 * j + i
                for s in range(2):
                    ve_m = k.op("vector", "tensor_tensor",
                                wait=([mld, ("AE", k.now("AE"))] if (i == 0 and s == 0) else []),
                                out=Pst[:, kt * 256 + s * 128: kt * 256 + (s + 1) * 128],
                                in0=Pst[:, kt * 256 + s * 128: kt * 256 + (s + 1) * 128],
                                in1=mbias[:, i * 128:(i + 1) * 128], op=ALU.mult)
            pe_r = None
            for kt in range(nkt):
                wv = [("VE", ve_m), rsread] if kt == 0 else []
                pe_r = k.op("tensor", "matmul", prs[:, 0:256], onesb[:, :],
                            Pst[:, kt * 256:(kt + 1) * 256],
                            start=(kt == 0), stop=(kt == nkt - 1), wait=wv,
                            inc=("PE" if kt == nkt - 1 else None))
            aeL = k.op("scalar", "activation", wait=[("PE", pe_r)], out=rtmp[:, :],
                       in_=prs[:, 0:256], func=AF.Ln)
            rsread = ("AE", aeL)
            aeR = k.op("scalar", "activation", out=rinv12[:, :], in_=rtmp[:, :],
                       func=AF.Exp, scale=-1.0)
            ve_rl = k.op("vector", "tensor_scalar",
                         wait=[("AE", aeR), ("LAM2", 16)],
                         out=r2l[:, :], in0=rinv12[:, 128:256],
                         scalar1=lamn[:, 0:1], scalar2=None, op0=ALU.mult)
            pe_c_last = 0
            for db in range(2):
                pcx = pc1 if db == 0 else pc2
                pe_c = None
                for kt in range(nkt):
                    wv = [vld_, cread[db], ("VE", ve_rl)] if kt == 0 else []
                    pe_c = k.op("tensor", "matmul", pcx[:, 0:256],
                                vb_[:, kt * 256 + db * 128: kt * 256 + (db + 1) * 128],
                                Pst[:, kt * 256:(kt + 1) * 256],
                                start=(kt == 0), stop=(kt == nkt - 1), wait=wv,
                                inc=("PE" if kt == nkt - 1 else None))
                pe_c_last = pe_c
                ve_a = k.op("vector", "tensor_tensor", wait=[("PE", pe_c)],
                            out=ya[:, :], in0=pcx[:, 0:128],
                            in1=rinv12[:, 0:128], op=ALU.mult)
                ve_b = k.op("vector", "tensor_tensor",
                            out=yb[:, :], in0=pcx[:, 128:256],
                            in1=r2l[:, :], op=ALU.mult)
                ve_y = k.op("vector", "tensor_tensor",
                            out=yT[:, (h * 2 + db) * 128:(h * 2 + db + 1) * 128],
                            in0=ya[:, :], in1=yb[:, :], op=ALU.add)
                cread[db] = ("VE", ve_y)
            pst_rel[pb] = ("PE", pe_c_last)
            ksl.release(kbufs[0][2], "PE", pe_c_last)
            ksl.release(kbufs[1][2], "PE", pe_c_last)
            vsl.release(vi_, "PE", pe_c_last)
        ve_yT = k.now("VE")
        for ob in range(4):
            wolds = []
            for hf in range(2):
                wb_, wld_, wslot_ = wol.load(
                    "gpsimd",
                    woT[hf * 1024:(hf + 1) * 1024, ob * 512:(ob + 1) * 512]
                    .rearrange("(c p) o -> p c o", p=128),
                    out_view=lambda b: b[:, :].rearrange("p (c o) -> p c o", o=512))
                wolds.append((wld_, wb_, wslot_))
            pe_last = None
            for cc in range(16):
                wv = []
                if cc == 0:
                    wv = [wolds[0][0], pjread, ("VE", ve_yT)]
                if cc == 8:
                    wv = [wolds[1][0]]
                wbuf = wolds[0][1] if cc < 8 else wolds[1][1]
                pe_last = k.op("tensor", "matmul", ppj[:, :],
                               yT[:, cc * 128:(cc + 1) * 128],
                               wbuf[:, (cc % 8) * 512:(cc % 8 + 1) * 512],
                               start=(cc == 0), stop=(cc == 15), wait=wv,
                               inc=("PE" if cc == 15 else None))
            wol.release(wolds[0][2], "PE", pe_last)
            wol.release(wolds[1][2], "PE", pe_last)
            xt_, xw_, xi_ = x1fl.load("sync", x1f[ts(j), ob * 512:(ob + 1) * 512])
            sb_, bw, bi = xrst.acquire()
            ve = k.op("vector", "scalar_tensor_tensor",
                      wait=[("PE", pe_last), xw_] + bw,
                      out=sb_[:, :], in0=ppj[:, :], scalar=0.2,
                      in1=xt_[:, :], op0=ALU.mult, op1=ALU.add)
            pjread = ("VE", ve)
            x1fl.release(xi_, "VE", ve)
            xrst.flush(bi, x1r_d[ts(j), ob * 512:(ob + 1) * 512], wait=[("VE", ve)])
    k.pop_phase()

    k.push_phase()
    xqT = k.sbuf("xqT", [128, 16 * 1024], BF16)
    k.mksem("XQL")
    XQL = ("XQL", k.op("sync", "dma_start",
                       out=xqT[:, :].rearrange("p (c t) -> p c t", t=1024),
                       in_=xqT_d.rearrange("(c p) t -> p c t", p=128),
                       inc="XQL", incn=16))
    t1l = LBuf(k, "t1l", 2, [128, 16 * 512], BF16)
    t3l = LBuf(k, "t3l", 2, [128, 16 * 512], BF16)
    o1b = k.sbuf("o1b", [128, 512], F32)
    o3b = k.sbuf("o3b", [128, 512], F32)
    s1b = k.sbuf("s1b", [128, 512], F32)
    hst = SOut(k, "hst", 2, [128, 512], F32)
    hpool = [k.psum(f"hpool{i}", [128, 512], F32) for i in range(4)]
    hread = [("VE", 0)] * 4
    vw512b = lambda b: b[:, :].rearrange("p (c j) -> p c j", j=512)
    for fcol in range(11):
        _, t1ld, t1i = t1l.load("sync",
                                t1f[:, fcol * 512:(fcol + 1) * 512]
                                .rearrange("(c p) j -> p c j", p=128),
                                out_view=vw512b)
        _, t3ld, t3i = t3l.load("sync",
                                t3f[:, fcol * 512:(fcol + 1) * 512]
                                .rearrange("(c p) j -> p c j", p=128),
                                out_view=vw512b)
        for j in range(8):
            slA, slB = (0, 1) if j % 2 == 0 else (2, 3)
            pe1 = None
            for c in range(16):
                wv = [t1ld, hread[slA], XQL] if c == 0 else []
                pe1 = k.op("tensor", "matmul", hpool[slA][:, :],
                           xqT[:, c * 1024 + j * 128: c * 1024 + (j + 1) * 128],
                           t1l.bufs[t1i][:, c * 512:(c + 1) * 512],
                           start=(c == 0), stop=(c == 15), wait=wv,
                           inc=("PE" if c == 15 else None))
            pe3 = None
            for c in range(16):
                wv = [t3ld, hread[slB]] if c == 0 else []
                pe3 = k.op("tensor", "matmul", hpool[slB][:, :],
                           xqT[:, c * 1024 + j * 128: c * 1024 + (j + 1) * 128],
                           t3l.bufs[t3i][:, c * 512:(c + 1) * 512],
                           start=(c == 0), stop=(c == 15), wait=wv,
                           inc=("PE" if c == 15 else None))
            ve_o1 = k.op("vector", "tensor_scalar", wait=[("PE", pe1)],
                         out=o1b[:, :], in0=hpool[slA][:, :],
                         scalar1=sstat[:, j * 8 + 0: j * 8 + 1],
                         scalar2=sstat[:, j * 8 + 1: j * 8 + 2],
                         op0=ALU.subtract, op1=ALU.mult)
            hread[slA] = ("VE", ve_o1)
            ae_s = k.op("scalar", "activation", wait=[("VE", ve_o1)],
                        out=s1b[:, :], in_=o1b[:, :], func=AF.Silu)
            ve_o3 = k.op("vector", "tensor_scalar", wait=[("PE", pe3)],
                         out=o3b[:, :], in0=hpool[slB][:, :],
                         scalar1=sstat[:, j * 8 + 0: j * 8 + 1],
                         scalar2=sstat[:, j * 8 + 2: j * 8 + 3],
                         op0=ALU.subtract, op1=ALU.mult)
            hread[slB] = ("VE", ve_o3)
            sb_, bw, bi = hst.acquire()
            ve_h = k.op("vector", "tensor_tensor", wait=[("AE", ae_s)] + bw,
                        out=sb_[:, :], in0=s1b[:, :], in1=o3b[:, :], op=ALU.mult)
            hst.flush(bi, h_d[ts(j), fcol * 512:(fcol + 1) * 512],
                      wait=[("VE", ve_h)])
        t1l.release(t1i, "PE", k.now("PE"))
        t3l.release(t3i, "PE", k.now("PE"))
    k.pop_phase()

    k.push_phase()
    hld = LBuf(k, "hld", 2, [128, 5632], F32)
    hc32 = k.sbuf("hc32", [128, 5632], I32)
    hcf = k.sbuf("hcf", [128, 5632], F32)
    hdb = k.sbuf("hdb", [128, 5632], F32)
    hc1 = k.sbuf("hc1", [128, 5632], F32)
    hqtm = k.sbuf("hqtm", [128, 5632], BF16)
    hqT = k.sbuf("hqT", [128, 44 * 128], BF16)
    hstt = k.sbuf("hstt", [128, 16], F32)
    ln127c = k.sbuf("ln127c", [128, 1], F32)
    k.op("vector", "memset", ln127c[:, :], LN127)
    tph = TPose(k, "tph", ident)
    k.mksem("HQS")
    hq_st = 0
    for j in range(8):
        hb_, hldw, hi = hld.load("sync", h_d[ts(j), :])
        k.op("vector", "tensor_reduce", wait=[hldw], out=hstt[:, 0:1],
             in_=hb_[:, :], axis=AX.X, op=ALU.max, apply_absolute_value=True)
        v_am = k.op("vector", "tensor_scalar", out=hstt[:, 1:2], in0=hstt[:, 0:1],
                    scalar1=1e-5, scalar2=None, op0=ALU.max)
        k.op("scalar", "activation", wait=[("VE", v_am)], out=hstt[:, 2:3],
             in_=hstt[:, 1:2], func=AF.Ln)
        a_sx = k.op("scalar", "activation", out=hstt[:, 3:4], in_=hstt[:, 2:3],
                    func=AF.Exp, scale=-1.0, bias=ln127c[:, 0:1])
        k.op("vector", "tensor_scalar", wait=[("AE", a_sx)], out=hstt[:, 4:5],
             in0=hstt[:, 1:2], scalar1=1.0 / 127.0, scalar2=None, op0=ALU.mult)
        ve_y = k.op("vector", "tensor_scalar", out=hdb[:, :], in0=hb_[:, :],
                    scalar1=hstt[:, 3:4], scalar2=None, op0=ALU.mult)
        a_c1 = k.op("scalar", "copy", wait=[("VE", ve_y)], out=hc32[:, :],
                    in_=hdb[:, :])
        a_c2 = k.op("scalar", "copy", out=hcf[:, :], in_=hc32[:, :])
        ve_rel = k.op("vector", "tensor_tensor", wait=[("AE", a_c2)],
                      out=hdb[:, :], in0=hdb[:, :],
                      in1=hcf[:, :], op=ALU.subtract)
        hld.release(hi, "VE", ve_rel)
        k.op("vector", "scalar_tensor_tensor", out=hc1[:, :], in0=hdb[:, :],
             scalar=0.5, in1=hcf[:, :], op0=ALU.is_gt, op1=ALU.add)
        ve_q = k.op("vector", "scalar_tensor_tensor", out=hqtm[:, :], in0=hdb[:, :],
                    scalar=-0.5, in1=hc1[:, :], op0=ALU.is_lt, op1=ALU.subtract)
        k.op("vector", "reduce_sum", out=sstat[:, j * 8 + 4: j * 8 + 5],
             in_=hqtm[:, :], axis=AX.X)
        ve_sx = k.op("vector", "tensor_copy", out=sstat[:, j * 8 + 5: j * 8 + 6],
                     in_=hstt[:, 4:5])
        for c in range(44):
            pe, ve_c = tph.emit(hqtm[:, ts(c)], hqT[:, ts(c)],
                                extra_wait=[("VE", ve_q)],
                                copy_extra_wait=[("HQS", hq_st)] if c == 0 else [])
        st = k.op("sync", "dma_start",
                  wait=[("VE", k.now("VE")), ("HQS", hq_st)],
                  out=hqT_d[:, j * 128:(j + 1) * 128]
                  .rearrange("(c p) o -> p c o", p=128),
                  in_=hqT[:, :].rearrange("p (c o) -> p c o", o=128),
                  inc="HQS", incn=16)
        hq_st = st
    k._note_final("HQS", hq_st)
    k.pop_phase()

    k.push_phase()
    wsz2 = k.sbuf("wsz2", [128, 4], F32)
    msw2 = k.sbuf("msw2", [128, 4], F32)
    k.mksem("WSL2")
    ve_mz = k.op("vector", "memset", wsz2[:, :], 0.0)
    wsld = ("WSL2", k.op("sync", "dma_start", out=wsz2[0:8, :], in_=wsall,
                         wait=[("VE", ve_mz)], inc="WSL2", incn=16))
    k.push_phase()
    ppool2 = k.psum("ppool2", [128, 4], F32)
    pe_ws = k.op("tensor", "matmul", ppool2[:, :], onesfs[:, :], wsz2[:, :],
                 wait=[wsld, ("VE", ve_mz)], start=True, stop=True, inc="PE")
    k.op("vector", "tensor_scalar", wait=[("PE", pe_ws)], out=msw2[:, :],
         in0=ppool2[:, :], scalar1=NW2, scalar2=1e-5, op0=ALU.mult, op1=ALU.max)
    k.pop_phase()
    for j in range(8):
        k.op("vector", "tensor_tensor", out=sstat[:, j * 8 + 6: j * 8 + 7],
             in0=sstat[:, j * 8 + 5: j * 8 + 6], in1=msw2[:, 1:2], op=ALU.mult)
        k.op("vector", "tensor_scalar", out=sstat[:, j * 8 + 6: j * 8 + 7],
             in0=sstat[:, j * 8 + 6: j * 8 + 7], scalar1=-1.0, scalar2=None,
             op0=ALU.mult)
    hql = LBuf(k, "hql", 2, [128, 1024], BF16)
    t2l = LBuf(k, "t2l", 2, [128, 512], BF16)
    xrl = LBuf(k, "xrl2", 2, [128, 512], F32)
    ost = SOut(k, "ost", 3, [128, 512], F32)
    tbv = k.sbuf("tbv", [128, 512], F32)
    opool = [k.psum(f"opool{i}", [128, 512], F32) for i in range(8)]
    oread = [("VE", 0)] * 8
    for hb in range(4):
        hcols = slice(hb * 512, (hb + 1) * 512)
        mm_t = {}
        for fc in range(44):
            hb_, hqld, hqi = hql.load("sync", hqT_d[fc * 128:(fc + 1) * 128, :])
            tb_, t2ld, t2i = t2l.load("sync", t2f[fc * 128:(fc + 1) * 128, hcols])
            for j in range(8):
                wv = [hqld, t2ld]
                if fc == 0:
                    wv = wv + [oread[j]]
                mm_t[j] = k.op("tensor", "matmul", opool[j][:, :],
                               hb_[:, ts(j)], tb_[:, :],
                               start=(fc == 0), stop=(fc == 43), wait=wv, inc="PE")
            hql.release(hqi, "PE", mm_t[7])
            t2l.release(t2i, "PE", mm_t[7])
        for j in range(8):
            xr_, xrw, xri = xrl.load("sync", x1r_d[ts(j), hcols])
            ve_t = k.op("vector", "tensor_scalar", wait=[("PE", mm_t[j])],
                        out=tbv[:, :], in0=opool[j][:, :],
                        scalar1=sstat[:, j * 8 + 4: j * 8 + 5],
                        scalar2=sstat[:, j * 8 + 6: j * 8 + 7],
                        op0=ALU.subtract, op1=ALU.mult)
            oread[j] = ("VE", ve_t)
            sb_, bw, bi = ost.acquire()
            ve = k.op("vector", "tensor_tensor", wait=[xrw] + bw,
                      out=sb_[:, :], in0=tbv[:, :], in1=xr_[:, :], op=ALU.add)
            xrl.release(xri, "VE", ve)
            ost.flush(bi, out_d[ts(j), hcols], wait=[("VE", ve)])
    k.pop_phase()
    k.finish()
    return k.build()


FOLD = [tuple(8 * j + c for j in range(8)) for c in range(8)]


def prep_EF(inp, resB):
    x1 = np.concatenate([r["x1_d"] for r in resB], axis=0)
    knf = np.concatenate([r["kL1_d"] for r in resB], axis=1)
    vnf = np.concatenate([r["vL1_d"] for r in resB], axis=0)
    t1f = np.concatenate([r["t1_d"] for r in resB], axis=0)
    t3f = np.concatenate([r["t3_d"] for r in resB], axis=0)
    t2f = np.concatenate([r["t2_d"] for r in resB], axis=0)[:5632]
    wqT = np.ascontiguousarray(inp["L1_attn_wq"].T)
    woT = np.ascontiguousarray(inp["L1_attn_wo"].T)
    lqk = np.stack([inp["L1_attn_lq1"], inp["L1_attn_lk1"],
                    inp["L1_attn_lq2"], inp["L1_attn_lk2"]]).astype(np.float32)
    identb = np.eye(128, dtype=np.float32)
    onesf = np.ones((128, 128), dtype=np.float32)
    import ml_dtypes
    w1T = np.ascontiguousarray(inp["L1_ffn_w1"].T)
    w2T = np.ascontiguousarray(inp["L1_ffn_w2"].T)
    w3T = np.ascontiguousarray(inp["L1_ffn_w3"].T)
    w2Tp = np.zeros((6144, 2048), dtype=np.float32)
    w2Tp[:5632] = w2T
    wsall = np.stack([np.array([np.abs(w1T[c*256:(c+1)*256]).sum(),
                                np.abs(w2Tp[c*768:(c+1)*768]).sum(),
                                np.abs(w3T[c*256:(c+1)*256]).sum(), 0.0],
                               dtype=np.float32) for c in range(8)])
    maps = []
    for c in range(8):
        rows = np.concatenate([np.arange(t * 128, (t + 1) * 128) for t in FOLD[c]])
        m01 = np.zeros((8, 8, 128, 128), dtype=np.float32)
        for j in range(8):
            qt = 8 * j + c
            for i in range(8):
                kt = 8 * j + i
                kg = kt * 128 + np.arange(128)[:, None]
                qg = qt * 128 + np.arange(128)[None, :]
                m01[j, i] = (kg <= qg).astype(np.float32)
        maps.append(dict(
            x1f=np.ascontiguousarray(x1[rows]),
            knf=knf, vnf=vnf, wqT=wqT, woT=woT, lqk=lqk,
            m01_d=m01.astype(ml_dtypes.bfloat16),
            identb=identb, onesf=onesf,
            t1f=t1f, t3f=t3f, t2f=t2f, wsall=wsall,
        ))
    return maps


def assemble_out(resE):
    full = np.zeros((8192, 2048), dtype=np.float32)
    for c in range(8):
        o = resE[c]["out_d"]
        for ji, t in enumerate(FOLD[c]):
            full[t * 128:(t + 1) * 128] = o[ji * 128:(ji + 1) * 128]
    return full.reshape(1, 8192, 2048)


from concourse.bass_utils import run_bass_kernel_spmd

_CACHE = {}


def _get(name, fn):
    if name not in _CACHE:
        _CACHE[name] = fn()
    return _CACHE[name]


def kernel(**inputs):
    inp = {k_: np.asarray(v) for k_, v in inputs.items()}
    ncA = _get("A", build_launch_A)
    resA = run_bass_kernel_spmd(ncA, prep_A(inp), core_ids=list(range(8))).results
    ncB = _get("BCD", build_launch_BCD)
    resB = run_bass_kernel_spmd(ncB, prep_BCD(inp, resA), core_ids=list(range(8))).results
    ncE = _get("EF", build_launch_EF)
    resE = run_bass_kernel_spmd(ncE, prep_EF(inp, resB), core_ids=list(range(8))).results
    return assemble_out(resE)


# version 04
# speedup vs baseline: 1.3389x; 1.3389x over previous
import contextlib
from collections import defaultdict

import numpy as np
import concourse.bass as bass
import concourse.mybir as mybir

F32 = mybir.dt.float32
BF16 = mybir.dt.bfloat16
I32 = mybir.dt.int32
AF = mybir.ActivationFunctionType
ALU = mybir.AluOpType
AX = mybir.AxisListType

ENGS = ("sync", "gpsimd", "tensor", "vector", "scalar")


class K:
    AUTO = {"vector": "VE", "scalar": "AE", "gpsimd": "GP"}

    def __init__(self, name="prog"):
        self.nc = bass.Bass("TRN2", target_bir_lowering=False)
        self.plan = {e: [] for e in ENGS}
        self.cnt = defaultdict(int)
        self.sem = {}
        self.stack = contextlib.ExitStack()
        self.phase_stacks = []
        self.dram = {}
        self.final_waits = []
        self.dma_watch = {}
        self.last_small = {}
        self.mksem("PE", "VE", "AE", "GP")
        self.nc.detect_race_conditions = False

    def din(self, name, shape, dtype=F32):
        ap = self.nc.dram_tensor(name, list(shape), dtype, kind="ExternalInput").ap()
        self.dram[name] = ap
        return ap

    def dout(self, name, shape, dtype=F32):
        ap = self.nc.dram_tensor(name, list(shape), dtype, kind="ExternalOutput").ap()
        self.dram[name] = ap
        return ap

    def dtmp(self, name, shape, dtype=F32):
        ap = self.nc.dram_tensor(name, list(shape), dtype, kind="Internal").ap()
        self.dram[name] = ap
        return ap

    def sbuf(self, name, shape, dtype=F32):
        st = self.phase_stacks[-1] if self.phase_stacks else self.stack
        return st.enter_context(self.nc.sbuf_tensor(name, list(shape), dtype))

    def psum(self, name, shape, dtype=F32):
        st = self.phase_stacks[-1] if self.phase_stacks else self.stack
        return st.enter_context(self.nc.psum_tensor(name, list(shape), dtype))

    def push_phase(self):
        self.phase_stacks.append(contextlib.ExitStack())

    def pop_phase(self):
        self.phase_stacks.pop().close()
        self.fence()

    def fence(self):
        eng_w = [("PE", self.cnt["PE"]), ("VE", self.cnt["VE"]),
                 ("AE", self.cnt["AE"]), ("GP", self.cnt["GP"])]
        dma_w = list(self.dma_watch.items())
        for e in ENGS:
            own = self.AUTO.get(e)
            w = [(s, v) for (s, v) in eng_w if s != own] + dma_w
            self.waiton(e, *w)

    def mksem(self, *names):
        for n in names:
            if n not in self.sem:
                self.sem[n] = self.stack.enter_context(self.nc.semaphore(n))

    _DTSZ = {F32: 4, BF16: 2, I32: 4}

    def _outbytes(self, args, kwargs):
        ap = kwargs.get("out")
        if ap is None and args:
            ap = args[0]
        try:
            return ap.free_size() * self._DTSZ.get(ap.dtype, 4)
        except Exception:
            return 0

    def op(self, eng, method, *args, wait=None, inc=None, incn=1, **kwargs):
        is_dma = method.startswith("dma") or method.startswith("indirect")
        auto = self.AUTO.get(eng) if not is_dma else None
        waits = list(wait or [])
        if auto is not None:
            if self.last_small.get(eng, True):
                waits.append((auto, self.cnt[auto]))
            self.last_small[eng] = self._outbytes(args, kwargs) < 2048
            if inc is None:
                inc = auto
            if inc == auto:
                incn = 1
        tgt = None
        if inc is not None:
            self.cnt[inc] += incn
            tgt = self.cnt[inc]
            if inc not in ("PE", "VE", "AE", "GP"):
                self.dma_watch[inc] = tgt
        waits = [(s, v) for (s, v) in waits if v > 0]
        semd, n_ = self.sem, incn

        def fn(e, method=method, args=args, kwargs=kwargs, waits=waits, inc=inc):
            for (s, v) in waits:
                e.wait_ge(semd[s], v)
            instr = getattr(e, method)(*args, **kwargs)
            if inc is not None:
                instr.then_inc(semd[inc], n_)
            return instr

        self.plan[eng].append(fn)
        return tgt

    def waiton(self, eng, *sv):
        sv = [(s, v) for (s, v) in sv if v > 0]
        if not sv:
            return

        def fn(e, sv=sv):
            for (s, v) in sv:
                e.wait_ge(self.sem[s], v)
        self.plan[eng].append(fn)

    def store(self, dst, src, eng="sync", sem="STG", wait=None):
        self.mksem(sem)
        t = self.op(eng, "dma_start", out=dst, in_=src, wait=wait, inc=sem, incn=16)
        self._note_final(sem, t)
        return t

    def _note_final(self, sem, tgt):
        self.final_waits = [(s, v) for (s, v) in self.final_waits if s != sem]
        self.final_waits.append((sem, tgt))

    def finish(self, eng="sync"):
        self.waiton(eng, *self.final_waits)

    def now(self, sem):
        return self.cnt[sem]

    def build(self):
        nc = self.nc
        with nc.Block() as block:
            for ename in ENGS:
                def driver(e, ename=ename):
                    for fn in self.plan[ename]:
                        fn(e)
                getattr(block, ename)(driver)
        self.stack.close()
        return nc


class LBuf:

    def __init__(self, k: K, name, n, shape, dtype):
        self.k, self.name, self.n = k, name, n
        self.bufs = [k.sbuf(f"{name}{i}", shape, dtype) for i in range(n)]
        self.sems = [f"{name}S{i}" for i in range(n)]
        k.mksem(*self.sems)
        self.uses = [0] * n
        self.released = [None] * n
        self.i = 0

    def load(self, eng, in_ap, view=None, extra_wait=(), out_view=None):
        b = self.i % self.n
        self.i += 1
        self.uses[b] += 1
        w = list(extra_wait)
        if self.released[b] is not None:
            w.append(self.released[b])
        out = self.bufs[b][:, :] if out_view is None else out_view(self.bufs[b])
        t = self.k.op(eng, "dma_start", out=out, in_=in_ap, wait=w,
                      inc=self.sems[b], incn=16)
        return self.bufs[b], (self.sems[b], t), b

    def release(self, b, sem, tgt):
        self.released[b] = (sem, tgt)


class SOut:

    def __init__(self, k: K, name, n, shape, dtype):
        self.k, self.name, self.n = k, name, n
        self.bufs = [k.sbuf(f"{name}{i}", shape, dtype) for i in range(n)]
        self.sems = [f"{name}S{i}" for i in range(n)]
        k.mksem(*self.sems)
        self.stored = [0] * n
        self.i = 0

    def acquire(self):
        b = self.i % self.n
        self.i += 1
        w = [(self.sems[b], self.stored[b] * 16)] if self.stored[b] else []
        return self.bufs[b], w, b

    def flush(self, b, dst, wait, eng="sync", src_view=None):
        src = self.bufs[b][:, :] if src_view is None else src_view(self.bufs[b])
        self.stored[b] += 1
        t = self.k.op(eng, "dma_start", out=dst, in_=src, wait=wait,
                      inc=self.sems[b], incn=16)
        self.k._note_final(self.sems[b], t)
        return t


class TPose:

    def __init__(self, k: K, name, ident, alt=False):
        self.k = k
        self.ps = [k.psum(name + "a", [128, 128], BF16),
                   k.psum(name + "b", [128, 128], BF16)]
        self.ident = ident
        self.i = 0
        self.alt = alt
        self.last_copy = [("VE", 0), ("VE", 0)]

    def emit(self, src_view, dst_view, extra_wait=(), copy_extra_wait=(), rows=128):
        k, b = self.k, self.i % 2
        use_sc = self.alt and (self.i % 2 == 1)
        self.i += 1
        w = [self.last_copy[b]] + list(extra_wait)
        pe = k.op("tensor", "transpose", self.ps[b][0:rows, :], src_view,
                  self.ident[:, :], wait=w, inc="PE")
        if use_sc:
            ve = k.op("scalar", "copy", wait=[("PE", pe)] + list(copy_extra_wait),
                      out=dst_view, in_=self.ps[b][0:rows, :])
            self.last_copy[b] = ("AE", ve)
        else:
            ve = k.op("vector", "tensor_copy", wait=[("PE", pe)] + list(copy_extra_wait),
                      out=dst_view, in_=self.ps[b][0:rows, :])
            self.last_copy[b] = ("VE", ve)
        return pe, ve


def ts(t, n=128):
    return slice(t * n, (t + 1) * n)


RSQRT_H = float(1.0 / np.sqrt(np.float32(2048.0)))



def build_launch_A():
    k = K("A")
    xA = k.din("xA", [1024, 2048], F32)
    wqkvR = k.din("wqkvR", [48, 128, 2048], F32)
    bqkv = k.din("bqkv", [48, 128], F32)
    identb = k.din("identb", [128, 128], F32)
    onesf = k.din("onesf", [128, 128], F32)
    w1s = k.din("w1s", [256, 5632], F32)
    w2s = k.din("w2s", [768, 2048], F32)
    w3s = k.din("w3s", [256, 5632], F32)
    xnT_d = k.dout("xnT_d", [2048, 1024], BF16)
    qT_d = k.dout("qT_d", [2048, 1024], BF16)
    kT_d = k.dout("kT_d", [2048, 1024], BF16)
    v_d = k.dout("v_d", [1024, 2048], BF16)
    wsums_d = k.dout("wsums_d", [1, 4], F32)

    xc = k.sbuf("xc", [128, 2048], F32)
    sq = k.sbuf("sq", [128, 2048], F32)
    st1 = k.sbuf("st1", [128, 8], F32)
    xnT = k.sbuf("xnT", [128, 16 * 1024], BF16)
    vTs = k.sbuf("vTs", [128, 16 * 1024], BF16)
    ident = k.sbuf("ident", [128, 128], BF16)
    onessb = k.sbuf("onessb", [128, 128], F32)
    bq = k.sbuf("bq", [128, 48], F32)
    acc = k.sbuf("acc", [128, 4], F32)
    red = k.sbuf("red", [128, 1], F32)
    wsum_st = k.sbuf("wsum_st", [1, 4], F32)
    vst = k.sbuf("vst", [128, 2048], BF16)

    pqkv = [k.psum(f"pqkv{i}", [128, 512], F32) for i in range(2)]
    pws = k.psum("pws", [128, 4], F32)

    k.mksem("CN", "CNG")
    tp = TPose(k, "ptp", ident)
    xsb = LBuf(k, "xsb", 2, [128, 2048], F32)
    wslb = LBuf(k, "wslb", 2, [128, 16 * 128], BF16)
    wshb = LBuf(k, "wshb", 2, [128, 5632], F32)
    qkst = SOut(k, "qkst", 3, [128, 512], BF16)

    k.op("sync", "dma_start", out=bq[:, :], inc="CN", incn=16,
         in_=bqkv.rearrange("b p -> p b"), allow_slow_non_contiguous=True)
    k.op("sync", "dma_start", out=onessb[:, :], in_=onesf, inc="CN", incn=16)
    k.op("gpsimd", "dma_start", out=ident[:, :], in_=identb, inc="CNG", incn=16)
    const_cn = k.now("CN")
    const_cng = k.now("CNG")

    last_tp_pe = 0
    for t in range(8):
        xbuf, ldw, xslot = xsb.load("sync", xA[ts(t), :])
        k.op("vector", "reduce_sum", wait=[ldw],
             out=st1[:, 0:1], in_=xbuf[:, :], axis=AX.X)
        k.op("vector", "tensor_scalar", out=st1[:, 1:2], in0=st1[:, 0:1],
             scalar1=1.0 / 2048.0, scalar2=None, op0=ALU.mult)
        k.op("vector", "tensor_scalar", out=xc[:, :], in0=xbuf[:, :],
             scalar1=st1[:, 1:2], scalar2=None, op0=ALU.subtract)
        ve_xcdone = k.op("vector", "tensor_tensor", out=sq[:, :], in0=xc[:, :],
                         in1=xc[:, :], op=ALU.mult)
        xsb.release(xslot, "VE", ve_xcdone)
        k.op("vector", "reduce_sum", out=st1[:, 2:3], in_=sq[:, :], axis=AX.X)
        v_ve = k.op("vector", "tensor_scalar", out=st1[:, 3:4], in0=st1[:, 2:3],
                    scalar1=1.0 / 2048.0, scalar2=1e-5, op0=ALU.mult, op1=ALU.add)
        k.op("scalar", "activation", wait=[("VE", v_ve)], out=st1[:, 4:5],
             in_=st1[:, 3:4], func=AF.Ln)
        a2 = k.op("scalar", "activation", out=st1[:, 5:6], in_=st1[:, 4:5],
                  func=AF.Exp, scale=-0.5)
        ve2 = k.op("vector", "tensor_scalar", wait=[("AE", a2), ("PE", last_tp_pe)],
                   out=vst[:, :], in0=xc[:, :], scalar1=st1[:, 5:6], scalar2=None,
                   op0=ALU.mult)
        for c in range(16):
            pe, _ = tp.emit(vst[:, ts(c)],
                            xnT[:, c * 1024 + t * 128: c * 1024 + (t + 1) * 128],
                            extra_wait=[("VE", ve2), ("CNG", const_cng)])
            last_tp_pe = pe
    ve_xnT = k.now("VE")
    for c in range(16):
        k.store(xnT_d[ts(c), :], xnT[:, c * 1024:(c + 1) * 1024],
                sem="STX", wait=[("VE", ve_xnT)])

    pe_group = {}
    wld = {}
    _, wld[0], _ = wslb.load("gpsimd", wqkvR[0, :, :])
    for ob in range(48):
        if ob + 1 < 48:
            _, wld[ob + 1], _ = wslb.load(
                "gpsimd", wqkvR[ob + 1, :, :],
                extra_wait=[("PE", pe_group.get(ob - 1, 0))])
        w = wslb.bufs[ob % 2]
        for nh in range(2):
            pe_last = None
            for c in range(16):
                wv = []
                if c == 0:
                    wv = [wld[ob], ("VE", k.now("VE"))]
                pe_last = k.op(
                    "tensor", "matmul", pqkv[nh][:, :], w[:, ts(c)],
                    xnT[:, c * 1024 + nh * 512: c * 1024 + nh * 512 + 512],
                    start=(c == 0), stop=(c == 15),
                    wait=wv, inc=("PE" if c == 15 else None))
            pe_group[ob] = pe_last
            if ob < 32:
                buf, bw, bi = qkst.acquire()
                if ob < 16:
                    ve = k.op("vector", "tensor_scalar",
                              wait=[("PE", pe_last), ("CN", const_cn)] + bw,
                              out=buf[:, :], in0=pqkv[nh][:, :], scalar1=bq[:, ob:ob+1],
                              scalar2=RSQRT_H, op0=ALU.add, op1=ALU.mult)
                    dst = qT_d[ts(ob), nh * 512:(nh + 1) * 512]
                else:
                    ve = k.op("vector", "tensor_scalar",
                              wait=[("PE", pe_last), ("CN", const_cn)] + bw,
                              out=buf[:, :], in0=pqkv[nh][:, :], scalar1=bq[:, ob:ob+1],
                              scalar2=None, op0=ALU.add)
                    dst = kT_d[ts(ob - 16), nh * 512:(nh + 1) * 512]
                qkst.flush(bi, dst, wait=[("VE", ve)])
            else:
                vb = ob - 32
                k.op("vector", "tensor_scalar",
                     wait=[("PE", pe_last), ("CN", const_cn)],
                     out=vTs[:, vb * 1024 + nh * 512: vb * 1024 + nh * 512 + 512],
                     in0=pqkv[nh][:, :], scalar1=bq[:, ob:ob+1],
                     scalar2=None, op0=ALU.add)

    ve_v = k.now("VE")
    k.mksem("STV")
    stv = 0
    for t in range(8):
        for c in range(16):
            tp.emit(vTs[:, c * 1024 + t * 128: c * 1024 + (t + 1) * 128],
                    vst[:, ts(c)],
                    extra_wait=[("VE", ve_v)],
                    copy_extra_wait=[("STV", stv * 16)] if (c == 0 and stv) else [])
        stv += 1
        t_st = k.op("sync", "dma_start", out=v_d[ts(t), :], in_=vst[:, :],
                    wait=[("VE", k.now("VE"))], inc="STV", incn=16)
    k._note_final("STV", t_st)

    k.op("vector", "memset", acc[:, :], 0.0)
    for (wap, nch, cols, j) in [(w1s, 2, 5632, 0), (w2s, 6, 2048, 1), (w3s, 2, 5632, 2)]:
        for ch in range(nch):
            wbuf, ldw, wi = wshb.load("sync", wap[ts(ch), 0:cols],
                                      out_view=lambda b, cols=cols: b[:, 0:cols])
            k.op("vector", "reduce_sum", wait=[ldw],
                 out=red[:, :], in_=wbuf[:, 0:cols], axis=AX.X,
                 apply_absolute_value=True)
            ve = k.op("vector", "tensor_tensor", out=acc[:, j:j+1], in0=acc[:, j:j+1],
                      in1=red[:, :], op=ALU.add)
            wshb.release(wi, "VE", ve)
    pe = k.op("tensor", "matmul", pws[:, :], onessb[:, :], acc[:, :],
              wait=[("VE", k.now("VE")), ("CN", const_cn)],
              start=True, stop=True, inc="PE")
    ve = k.op("vector", "tensor_copy", wait=[("PE", pe)], out=wsum_st[:, :],
              in_=pws[0:1, :])
    k.store(wsums_d, wsum_st[:, :], sem="STW", wait=[("VE", ve)])

    k.finish()
    return k.build()


def prep_A(inp):
    x = np.ascontiguousarray(inp["L0_attn_hidden_states"].reshape(8192, 2048))
    wqkvT = np.ascontiguousarray(inp["L0_attn_w_qkv"].T)
    wqkvR = np.ascontiguousarray(
        wqkvT.reshape(16, 128, 48, 128).transpose(2, 1, 0, 3).reshape(48, 128, 2048))
    bqkv = np.ascontiguousarray(inp["L0_attn_b_qkv"].reshape(48, 128))
    identb = np.eye(128, dtype=np.float32)
    onesf = np.ones((128, 128), dtype=np.float32)
    w1T = np.ascontiguousarray(inp["L1_ffn_w1"].T)
    w2T = np.ascontiguousarray(inp["L1_ffn_w2"].T)
    w3T = np.ascontiguousarray(inp["L1_ffn_w3"].T)
    w2Tp = np.zeros((6144, 2048), dtype=np.float32)
    w2Tp[:5632] = w2T
    maps = []
    for c in range(8):
        maps.append(dict(
            xA=np.ascontiguousarray(x[c * 1024:(c + 1) * 1024]),
            wqkvR=wqkvR, bqkv=bqkv, identb=identb, onesf=onesf,
            w1s=np.ascontiguousarray(w1T[c * 256:(c + 1) * 256]),
            w2s=np.ascontiguousarray(w2Tp[c * 768:(c + 1) * 768]),
            w3s=np.ascontiguousarray(w3T[c * 256:(c + 1) * 256]),
        ))
    return maps




RMS_EPS = 1.1920929e-07
NW2 = float(1.0 / (5632.0 * 2048.0))


def build_launch_BCD(debug=False):
    k = K("BCD")
    xA = k.din("xA", [1024, 2048], F32)
    qR = k.din("qR", [2, 128, 16 * 512], BF16)
    kR = k.din("kR", [64, 128, 16 * 128], BF16)
    vR = k.din("vR", [16, 128, 64 * 128], BF16)
    xnR = k.din("xnR", [128, 16 * 1024], BF16)
    woutT = k.din("woutT", [2048, 2048], F32)
    bout = k.din("bout", [1, 2048], F32)
    identb = k.din("identb", [128, 128], F32)
    onesf = k.din("onesf", [128, 128], F32)
    desc = k.din("desc", [1, 8], F32)
    WgT = k.din("WgT", [2048, 8], F32)
    A1fT = k.din("A1fT", [2048, 256], F32)
    A3fT = k.din("A3fT", [2048, 256], F32)
    B1fT = k.din("B1fT", [256, 7168], F32)
    B3fT = k.din("B3fT", [256, 7168], F32)
    A2fT = k.din("A2fT", [7168, 256], F32)
    a2R = k.din("a2R", [128, 56 * 256], F32)
    B2fT = k.din("B2fT", [256, 2048], F32)
    w13R = k.din("w13R", [56, 128, 16 * 256], F32)
    bba2 = k.din("bba2", [56, 128, 768], F32)
    w2R = k.din("w2R", [4, 14, 128, 4 * 512], F32)
    pat1 = k.din("pat1", [128, 256], F32)
    pat2 = k.din("pat2", [128, 256], F32)
    wkR = k.din("wkR", [16, 128, 16 * 128], F32)
    wvR = k.din("wvR", [16, 128, 16 * 128], F32)
    wsall = k.din("wsall", [8, 4], F32)
    w1s = k.din("w1s", [256, 5632], F32)
    w2s = k.din("w2s", [768, 2048], F32)
    w3s = k.din("w3s", [256, 5632], F32)
    x1_d = k.dout("x1_d", [1024, 2048], F32)
    kL1_d = k.dout("kL1_d", [2048, 1024], BF16)
    vL1_d = k.dout("vL1_d", [1024, 2048], BF16)
    t1_d = k.dout("t1_d", [256, 5632], BF16)
    t2_d = k.dout("t2_d", [768, 2048], BF16)
    t3_d = k.dout("t3_d", [256, 5632], BF16)
    dmk = k.dout if debug else k.dtmp
    xr_d = dmk("xr_d", [1024, 2048], F32)
    z_d = dmk("z_d", [56, 128, 1024], BF16)
    t_d = k.dtmp("t_d", [2, 56, 128, 1024], BF16)

    ident = k.sbuf("ident", [128, 128], BF16)
    onesb = k.sbuf("onesb", [128, 128], BF16)
    onesfs = k.sbuf("onesfs", [128, 128], F32)
    boutrep = k.sbuf("boutrep", [128, 2048], F32)
    k.mksem("CN", "CNG")
    k.op("gpsimd", "dma_start", out=ident[:, :], in_=identb, inc="CNG", incn=16)
    k.op("gpsimd", "dma_start", out=onesb[:, :], in_=onesf, inc="CNG", incn=16)
    k.op("sync", "dma_start", out=onesfs[:, :], in_=onesf, inc="CN", incn=16)
    k.op("sync", "dma_start", out=boutrep[:, :],
         in_=bout.broadcast_to((128, 2048)), inc="CN", incn=16)
    CN = k.now("CN")
    CNG = k.now("CNG")

    k.push_phase()
    qTB = k.sbuf("qTB", [128, 16 * 512], BF16)
    k.mksem("QTB")
    Pst = k.sbuf("Pst", [128, 64 * 512], BF16)
    rtmp = k.sbuf("rtmp", [128, 512], F32)
    rinv = k.sbuf("rinv", [128, 512], F32)
    OT = k.sbuf("OT", [128, 16 * 512], BF16)
    evt = k.sbuf("evt", [128, 512], F32)
    pool = [k.psum(f"apool{i}", [128, 512], F32) for i in range(6)]
    pread = [("VE", 0)] * 6
    ktl = LBuf(k, "ktl", 2, [128, 16 * 128], BF16)
    vsl = LBuf(k, "vsl", 2, [128, 64 * 128], BF16)
    wol = LBuf(k, "wol", 2, [128, 8 * 512], BF16)
    xld = LBuf(k, "xld", 2, [128, 2048], F32)
    xrst = SOut(k, "xrst", 3, [128, 512], F32)
    vw128 = lambda b: b[:, :].rearrange("p (c j) -> p c j", j=128)
    vw512 = lambda b: b[:, :].rearrange("p (c j) -> p c j", j=512)

    qtb_rel = ("VE", 0)
    for TB in range(2):
        qld = ("QTB", k.op("sync", "dma_start", wait=[qtb_rel],
                           out=qTB[:, :], in_=qR[TB, :, :],
                           inc="QTB", incn=16))
        pst_free_pe = k.now("PE")
        exp_t = {}

        def rowsum0(kcr):
            rw_ = [("AE", exp_t[kcr])] + ([pread[2]] if kcr == 0 else [])
            return k.op("tensor", "matmul", pool[2][:, :], onesb[:, :],
                        Pst[:, kcr * 512:(kcr + 1) * 512],
                        start=(kcr == 0), stop=(kcr == 63), wait=rw_,
                        inc=("PE" if kcr == 63 else None))

        for kc in range(64):
            kt, kld, ksl = ktl.load("sync", kR[kc, :, :])
            sl = kc % 2
            pe_last = None
            for c in range(16):
                wv = [kld, qld, pread[sl], ("CNG", CNG)] if c == 0 else []
                pe_last = k.op("tensor", "matmul", pool[sl][:, :],
                               kt[:, ts(c)], qTB[:, c * 512:(c + 1) * 512],
                               start=(c == 0), stop=(c == 15), wait=wv,
                               inc=("PE" if c == 15 else None))
            ktl.release(ksl, "PE", pe_last)
            ae = k.op("scalar", "activation", wait=[("PE", pe_last), ("PE", pst_free_pe)],
                      out=Pst[:, kc * 512:(kc + 1) * 512], in_=pool[sl][:, :],
                      func=AF.Exp)
            pread[sl] = ("AE", ae)
            exp_t[kc] = ae
            if kc >= 2:
                pe_r = rowsum0(kc - 2)
        pe_r = rowsum0(62)
        pe_r = rowsum0(63)
        exp_last = k.now("AE")
        aeL = k.op("scalar", "activation", wait=[("PE", pe_r)], out=rtmp[:, :],
                   in_=pool[2][:, :], func=AF.Ln)
        pread[2] = ("AE", aeL)
        aeR = k.op("scalar", "activation", out=rinv[:, :], in_=rtmp[:, :],
                   func=AF.Exp, scale=-1.0)
        for db in range(16):
            vt, vld, vslot = vsl.load("sync", vR[db, :, :])
            sl = 3 + db % 2
            pe_last = None
            for kc in range(64):
                wv = [vld, pread[sl], ("AE", exp_last)] if kc == 0 else []
                pe_last = k.op("tensor", "matmul", pool[sl][:, :],
                               vt[:, ts(kc)], Pst[:, kc * 512:(kc + 1) * 512],
                               start=(kc == 0), stop=(kc == 63), wait=wv,
                               inc=("PE" if kc == 63 else None))
            vsl.release(vslot, "PE", pe_last)
            ve = k.op("vector", "tensor_tensor",
                      wait=[("PE", pe_last), ("PE", pst_free_pe), ("AE", aeR)],
                      out=OT[:, db * 512:(db + 1) * 512], in0=pool[sl][:, :],
                      in1=rinv[:, :], op=ALU.mult)
            pread[sl] = ("VE", ve)
        ot_ve = k.now("VE")
        qtb_rel = ("PE", k.now("PE"))
        for ob in range(4):
            wolds = []
            for hf in range(2):
                wb_, wld_, wslot_ = wol.load(
                    "gpsimd",
                    woutT[hf * 1024:(hf + 1) * 1024, ob * 512:(ob + 1) * 512]
                    .rearrange("(c p) j -> p c j", p=128),
                    out_view=vw512)
                wolds.append((wld_, wb_, wslot_))
            for tt in range(4):
                g = TB * 4 + tt
                sl = tt % 2
                pe_last = None
                for dc in range(16):
                    wv = []
                    if dc == 0:
                        wv = [wolds[0][0], pread[sl], ("VE", ot_ve)]
                    if dc == 8:
                        wv = [wolds[1][0]]
                    wbuf = wolds[0][1] if dc < 8 else wolds[1][1]
                    pe_last = k.op("tensor", "matmul", pool[sl][:, :],
                                   OT[:, dc * 512 + tt * 128: dc * 512 + (tt + 1) * 128],
                                   wbuf[:, (dc % 8) * 512:(dc % 8 + 1) * 512],
                                   start=(dc == 0), stop=(dc == 15), wait=wv,
                                   inc=("PE" if dc == 15 else None))
                xt, xldw, xslot = xld.load("sync", xA[ts(g), :])
                ve1 = k.op("vector", "tensor_tensor", wait=[("PE", pe_last), ("CN", CN)],
                           out=evt[:, :], in0=pool[sl][:, :],
                           in1=boutrep[:, ob * 512:(ob + 1) * 512], op=ALU.add)
                pread[sl] = ("VE", ve1)
                sb_, bw, bi = xrst.acquire()
                ve2 = k.op("vector", "tensor_tensor", wait=[xldw] + bw,
                           out=sb_[:, :], in0=evt[:, :],
                           in1=xt[:, ob * 512:(ob + 1) * 512], op=ALU.add)
                xld.release(xslot, "VE", ve2)
                xrst.flush(bi, xr_d[ts(g), ob * 512:(ob + 1) * 512], wait=[("VE", ve2)])
            wol.release(wolds[0][2], "PE", k.now("PE"))
            wol.release(wolds[1][2], "PE", k.now("PE"))
    k.pop_phase()

    k.push_phase()
    xnT = k.sbuf("xnT", [128, 16 * 1024], BF16)
    k.mksem("XNT")
    XNL = ("XNT", k.op("sync", "dma_start",
                       out=xnT[:, :], in_=xnR, inc="XNT", incn=16))
    mr1 = k.sbuf("mr1", [128, 2 * 1024], BF16)
    mr2 = k.sbuf("mr2", [128, 2 * 1024], BF16)
    rw1rep = k.sbuf("rw1rep", [128, 1024], F32)
    rw2rep = k.sbuf("rw2rep", [128, 1024], F32)
    za11 = k.sbuf("za11", [128, 2 * 1024], BF16)
    za12 = k.sbuf("za12", [128, 2 * 1024], BF16)
    za31 = k.sbuf("za31", [128, 2 * 1024], BF16)
    za32 = k.sbuf("za32", [128, 2 * 1024], BF16)
    za2w1 = k.sbuf("za2w1", [128, 2 * 1024], BF16)
    za2w2 = k.sbuf("za2w2", [128, 2 * 1024], BF16)

    k.push_phase()
    WgTs = k.sbuf("WgTs", [128, 16 * 8], BF16)
    k.op("gpsimd", "dma_start", out=WgTs[:, :].rearrange("p (c j) -> p c j", j=8),
         in_=WgT.rearrange("(c p) j -> p c j", p=128), inc="CNG", incn=16)
    descr = k.sbuf("descr", [128, 8], F32)
    k.op("sync", "dma_start", out=descr[:, :], in_=desc.broadcast_to((128, 8)),
         inc="CN", incn=16)
    A1s = k.sbuf("A1s", [128, 16 * 256], BF16)
    A3s = k.sbuf("A3s", [128, 16 * 256], BF16)
    k.op("gpsimd", "dma_start", out=A1s[:, :].rearrange("p (c j) -> p c j", j=256),
         in_=A1fT.rearrange("(c p) j -> p c j", p=128), inc="CNG", incn=16)
    k.op("gpsimd", "dma_start", out=A3s[:, :].rearrange("p (c j) -> p c j", j=256),
         in_=A3fT.rearrange("(c p) j -> p c j", p=128), inc="CNG", incn=16)
    p1s = k.sbuf("p1s", [128, 256], BF16)
    p2s = k.sbuf("p2s", [128, 256], BF16)
    k.op("gpsimd", "dma_start", out=p1s[:, :], in_=pat1, inc="CNG", incn=16)
    k.op("gpsimd", "dma_start", out=p2s[:, :], in_=pat2, inc="CNG", incn=16)
    CN2, CNG2 = k.now("CN"), k.now("CNG")

    m1stg = k.sbuf("m1stg", [128, 1024], BF16)
    m2stg = k.sbuf("m2stg", [128, 1024], BF16)
    estg = k.sbuf("estg", [128, 1024], BF16)
    mt1stg = k.sbuf("mt1stg", [128, 1024], F32)
    mt2stg = k.sbuf("mt2stg", [128, 1024], F32)
    esb = k.sbuf("esb", [128, 8], F32)
    tq = k.sbuf("tq", [128, 48], F32)
    stk = k.sbuf("stk", [128, 24], BF16)
    num1 = k.sbuf("num1", [128, 1024], F32)
    num2 = k.sbuf("num2", [128, 1024], F32)
    rpool = [k.psum(f"rpool{i}", [128, 512], F32) for i in range(3)]
    plg = k.psum("plg", [128, 8], F32)
    tpr = TPose(k, "tpr", ident)
    k.op("vector", "memset", m1stg[:, :], 0.0)
    k.op("vector", "memset", m2stg[:, :], 0.0)
    k.op("vector", "memset", estg[:, :], 0.0)
    k.op("vector", "memset", mt1stg[:, :], 0.0)
    k.op("vector", "memset", mt2stg[:, :], 0.0)

    lg_read = ("VE", 0)
    for g in range(8):
        pe_last = None
        for c in range(16):
            wv = [XNL, ("CNG", CNG2), lg_read] if c == 0 else []
            pe_last = k.op("tensor", "matmul", plg[:, :],
                           xnT[:, c * 1024 + g * 128: c * 1024 + (g + 1) * 128],
                           WgTs[:, c * 8:(c + 1) * 8],
                           start=(c == 0), stop=(c == 15), wait=wv,
                           inc=("PE" if c == 15 else None))
        k.op("vector", "reduce_max", wait=[("PE", pe_last)], out=tq[:, 0:1],
             in_=plg[:, :], axis=AX.X, negate=True)
        v_nm = k.now("VE")
        ae = k.op("scalar", "activation", wait=[("VE", v_nm)],
                  out=esb[:, :], in_=plg[:, :], func=AF.Exp, bias=tq[:, 0:1])
        lg_read = ("AE", ae)
        k.op("vector", "reduce_max", wait=[("AE", ae), ("CN", CN2)],
             out=tq[:, 1:2], in_=esb[:, :], axis=AX.X)
        k.op("vector", "tensor_scalar", out=tq[:, 8:16], in0=esb[:, :],
             scalar1=tq[:, 1:2], scalar2=None, op0=ALU.is_ge)
        k.op("vector", "tensor_tensor", out=tq[:, 8:16], in0=tq[:, 8:16],
             in1=descr[:, :], op=ALU.mult)
        k.op("vector", "reduce_max", out=tq[:, 2:3], in_=tq[:, 8:16], axis=AX.X)
        k.op("vector", "tensor_scalar", out=tq[:, 16:24], in0=tq[:, 8:16],
             scalar1=tq[:, 2:3], scalar2=None, op0=ALU.is_ge)
        k.op("vector", "tensor_copy", out=stk[:, 0:8], in_=tq[:, 16:24])
        k.op("vector", "scalar_tensor_tensor", out=tq[:, 24:32], in0=tq[:, 16:24],
             scalar=1.0, in1=esb[:, :], op0=ALU.subtract, op1=ALU.mult)
        k.op("vector", "tensor_reduce", out=tq[:, 3:4], in_=tq[:, 24:32],
             axis=AX.X, op=ALU.min)
        k.op("vector", "tensor_scalar", out=tq[:, 32:40], in0=tq[:, 24:32],
             scalar1=tq[:, 3:4], scalar2=None, op0=ALU.is_le)
        k.op("vector", "tensor_tensor", out=tq[:, 32:40], in0=tq[:, 32:40],
             in1=descr[:, :], op=ALU.mult)
        k.op("vector", "reduce_max", out=tq[:, 4:5], in_=tq[:, 32:40], axis=AX.X)
        k.op("vector", "tensor_scalar", out=stk[:, 8:16], in0=tq[:, 32:40],
             scalar1=tq[:, 4:5], scalar2=None, op0=ALU.is_ge)
        ve_e = k.op("vector", "tensor_copy", out=stk[:, 16:24], in_=esb[:, :])
        tpr.emit(stk[:, 0:8], m1stg[0:8, ts(g)], extra_wait=[("VE", ve_e)], rows=8)
        tpr.emit(stk[:, 8:16], m2stg[0:8, ts(g)], rows=8)
        tpr.emit(stk[:, 16:24], estg[0:8, ts(g)], rows=8)
    ve1 = k.op("vector", "tensor_tensor", out=mt1stg[0:8, :], in0=m1stg[0:8, :],
               in1=estg[0:8, :], op=ALU.mult)
    ve2 = k.op("vector", "tensor_tensor", out=mt2stg[0:8, :], in0=m2stg[0:8, :],
               in1=estg[0:8, :], op=ALU.mult)
    for half in range(2):
        p1 = k.op("tensor", "matmul", rpool[0][:, :], onesfs[:, :],
                  mt1stg[:, half * 512:(half + 1) * 512], start=True, stop=True,
                  wait=[("VE", k.now("VE")), ("CN", CN)], inc="PE")
        p2 = k.op("tensor", "matmul", rpool[1][:, :], onesfs[:, :],
                  mt2stg[:, half * 512:(half + 1) * 512], start=True, stop=True,
                  inc="PE")
        k.op("vector", "tensor_copy", wait=[("PE", p1)],
             out=num1[:, half * 512:(half + 1) * 512], in_=rpool[0][:, :])
        k.op("vector", "tensor_copy", wait=[("PE", p2)],
             out=num2[:, half * 512:(half + 1) * 512], in_=rpool[1][:, :])
    ve3 = k.op("vector", "tensor_tensor", out=rw1rep[:, :], in0=num1[:, :],
               in1=num2[:, :], op=ALU.add)
    k.op("scalar", "activation", wait=[("VE", ve3)], out=rw2rep[:, :],
         in_=rw1rep[:, :], func=AF.Ln)
    aeEx = k.op("scalar", "activation", out=rw2rep[:, :], in_=rw2rep[:, :],
                func=AF.Exp, scale=-1.0)
    k.op("vector", "tensor_tensor", wait=[("AE", aeEx)], out=rw1rep[:, :],
         in0=num1[:, :], in1=rw2rep[:, :], op=ALU.mult)
    ve5 = k.op("vector", "tensor_tensor", out=rw2rep[:, :],
               in0=num2[:, :], in1=rw2rep[:, :], op=ALU.mult)
    for eb in range(2):
        for half in range(2):
            pm1 = k.op("tensor", "matmul", rpool[0][:, :], p1s[:, ts(eb)],
                       m1stg[:, half * 512:(half + 1) * 512], start=True, stop=True,
                       wait=[("VE", k.now("VE")), ("CNG", CNG2)], inc="PE")
            pm2 = k.op("tensor", "matmul", rpool[1][:, :], p2s[:, ts(eb)],
                       m2stg[:, half * 512:(half + 1) * 512], start=True, stop=True,
                       inc="PE")
            off = eb * 1024 + half * 512
            k.op("vector", "tensor_copy", wait=[("PE", pm1)],
                 out=mr1[:, off:off + 512], in_=rpool[0][:, :])
            k.op("vector", "tensor_copy", wait=[("PE", pm2)],
                 out=mr2[:, off:off + 512], in_=rpool[1][:, :])
    for (Asrc, z1, z2) in [(A1s, za11, za12), (A3s, za31, za32)]:
        for eb in range(2):
            for half in range(2):
                pe_last = None
                for c in range(16):
                    wv = [("VE", k.now("VE"))] if c == 0 else []
                    pe_last = k.op("tensor", "matmul", rpool[2][:, :],
                                   Asrc[:, c * 256 + eb * 128: c * 256 + (eb + 1) * 128],
                                   xnT[:, c * 1024 + half * 512: c * 1024 + (half + 1) * 512],
                                   start=(c == 0), stop=(c == 15), wait=wv,
                                   inc=("PE" if c == 15 else None))
                off = eb * 1024 + half * 512
                k.op("vector", "tensor_tensor", wait=[("PE", pe_last)],
                     out=z1[:, off:off + 512], in0=rpool[2][:, :],
                     in1=mr1[:, off:off + 512], op=ALU.mult)
                k.op("vector", "tensor_tensor",
                     out=z2[:, off:off + 512], in0=rpool[2][:, :],
                     in1=mr2[:, off:off + 512], op=ALU.mult)
    k.pop_phase()

    k.push_phase()
    fpool = [k.psum(f"fpool{i}", [128, 512], F32) for i in range(8)]
    fread = [("VE", 0)] * 8
    w13l = LBuf(k, "w13l", 3, [128, 16 * 256], BF16)
    bbal = LBuf(k, "bbal", 3, [128, 768], BF16)
    a2full = k.sbuf("a2full", [128, 56 * 256], BF16)
    tres = LBuf(k, "tres", 3, [128, 512], BF16)
    k.mksem("A2F")
    bbq = [( "A2F", k.op("gpsimd", "dma_start", out=a2full[:, :],
                         in_=a2R, inc="A2F", incn=16))]
    base1sb = [k.sbuf(f"base1sb{i}", [128, 512], F32) for i in range(2)]
    base3sb = [k.sbuf(f"base3sb{i}", [128, 512], F32) for i in range(2)]
    x1b = [k.sbuf(f"x1b{i}", [128, 512], F32) for i in range(2)]
    s1b = [k.sbuf(f"s1b{i}", [128, 512], BF16) for i in range(2)]
    x3b = [k.sbuf(f"x3b{i}", [128, 512], BF16) for i in range(2)]
    ub = k.sbuf("ub", [128, 512], BF16)
    tspill = [SOut(k, f"tsp{i}", 2, [128, 512], BF16) for i in range(2)]
    tlast = [None, None]
    zst = SOut(k, "zst", 2, [128, 512], BF16)
    vwc = lambda b: b[:, :].rearrange("p (c j) -> p c j", j=128)

    def wslab(src, fb):
        return src[:, fb * 128:(fb + 1) * 128].rearrange("(c p) j -> p c j", p=128)

    def bslab(src, fb):
        return src[:, fb * 128:(fb + 1) * 128].rearrange("(e p) j -> p e j", p=128)

    prev_silu = 0
    pend = {}
    for TB in range(2):
        cols = slice(TB * 512, (TB + 1) * 512)
        pe_a2_tgts = []

        def chains(fbp):
            nonlocal prev_silu
            bbuf_, bbi, bpar = pend.pop(fbp)
            b1b = b3b = a2b = bbuf_
            zas = [(za11, za31, rw1rep), (za12, za32, rw2rep)]
            l1s, l3s = (2, 4), (3, 5)
            peL1, peL3 = {}, {}
            for kk in range(2):
                za1k = zas[kk][0]
                pe_ = None
                for eb in range(2):
                    wv = [fread[l1s[kk]]] if eb == 0 else []
                    pe_ = k.op("tensor", "matmul", fpool[l1s[kk]][:, :],
                               b1b[:, eb * 128:(eb + 1) * 128],
                               za1k[:, eb * 1024 + TB * 512: eb * 1024 + TB * 512 + 512],
                               start=(eb == 0), stop=(eb == 1), wait=wv,
                               inc=("PE" if eb == 1 else None))
                peL1[kk] = pe_
            for kk in range(2):
                za3k = zas[kk][1]
                pe_ = None
                for eb in range(2):
                    wv = [fread[l3s[kk]]] if eb == 0 else []
                    pe_ = k.op("tensor", "matmul", fpool[l3s[kk]][:, :],
                                b3b[:, 256 + eb * 128: 256 + (eb + 1) * 128],
                                za3k[:, eb * 1024 + TB * 512: eb * 1024 + TB * 512 + 512],
                                start=(eb == 0), stop=(eb == 1), wait=wv,
                                inc=("PE" if eb == 1 else None))
                peL3[kk] = pe_
            sil = {}
            for kk in range(2):
                ve_x1 = k.op("vector", "scalar_tensor_tensor",
                             wait=[("PE", peL1[kk]), ("AE", prev_silu)],
                             out=x1b[kk][:, :], in0=fpool[l1s[kk]][:, :],
                             scalar=0.0, in1=base1sb[bpar][:, :],
                             op0=ALU.bypass, op1=ALU.add)
                fread[l1s[kk]] = ("VE", ve_x1)
                ae_s = k.op("scalar", "activation", wait=[("VE", ve_x1)],
                            out=s1b[kk][:, :], in_=x1b[kk][:, :], func=AF.Silu)
                sil[kk] = ae_s
                prev_silu = ae_s
            vx3 = {}
            for kk in range(2):
                ve_x3 = k.op("vector", "scalar_tensor_tensor",
                             wait=[("PE", peL3[kk])], out=x3b[kk][:, :],
                             in0=fpool[l3s[kk]][:, :],
                             scalar=0.0, in1=base3sb[bpar][:, :],
                             op0=ALU.bypass, op1=ALU.add)
                fread[l3s[kk]] = ("VE", ve_x3)
            for kk in range(2):
                rwrep = zas[kk][2]
                ve_u = k.op("vector", "tensor_tensor", wait=[("AE", sil[kk])],
                            out=ub[:, :], in0=s1b[kk][:, :], in1=x3b[kk][:, :],
                            op=ALU.mult)
                tsb, tsw, tsi = tspill[kk].acquire()
                ve_t = k.op("vector", "tensor_tensor", wait=tsw,
                            out=tsb[:, :], in0=ub[:, :], in1=rwrep[:, cols],
                            op=ALU.mult)
                tspill[kk].flush(tsi, t_d[kk, fbp, :, TB * 512:(TB + 1) * 512],
                                 wait=[("VE", ve_t)])
                tlast[kk] = tsb
            pe_now = k.now("PE")
            bbal.release(bbi, "PE", pe_now)
            zb, zw, zi = zst.acquire()
            ve_z = k.op("vector", "tensor_tensor", wait=zw, out=zb[:, :],
                        in0=tlast[0][:, :], in1=tlast[1][:, :], op=ALU.add)
            zst.flush(zi, z_d[fbp, :, cols], wait=[("VE", ve_z)])

        for fb in range(56):
            wbuf_, w13ld, w13i = w13l.load("gpsimd", w13R[fb, :, :])
            bbuf_, bbld, bbi = bbal.load("gpsimd", bba2[fb, :, :])
            bpar = fb % 2
            pe1 = None
            for c in range(16):
                wv = [w13ld, fread[0], XNL] if c == 0 else []
                pe1 = k.op("tensor", "matmul", fpool[0][:, :],
                           wbuf_[:, c * 256: c * 256 + 128],
                           xnT[:, c * 1024 + TB * 512: c * 1024 + TB * 512 + 512],
                           start=(c == 0), stop=(c == 15), wait=wv,
                           inc=("PE" if c == 15 else None))
            pe3 = None
            for c in range(16):
                wv = [fread[1], bbld] if c == 0 else []
                pe3 = k.op("tensor", "matmul", fpool[1][:, :],
                           wbuf_[:, c * 256 + 128: c * 256 + 256],
                           xnT[:, c * 1024 + TB * 512: c * 1024 + TB * 512 + 512],
                           start=(c == 0), stop=(c == 15), wait=wv,
                           inc=("PE" if c == 15 else None))
            w13l.release(w13i, "PE", pe3)
            pend[fb] = (bbuf_, bbi, bpar)
            if fb > 0:
                chains(fb - 1)
            ve_b1 = k.op("vector", "tensor_copy", wait=[("PE", pe1)],
                         out=base1sb[bpar][:, :], in_=fpool[0][:, :])
            fread[0] = ("VE", ve_b1)
            ve_b3 = k.op("vector", "tensor_copy", wait=[("PE", pe3)],
                         out=base3sb[bpar][:, :], in_=fpool[1][:, :])
            fread[1] = ("VE", ve_b3)
        chains(55)
        k.waiton("sync", *[(tspill[i].sems[b], tspill[i].stored[b] * 16)
                           for i in range(2) for b in range(2)])
        for kk in range(2):
            for eb in range(2):
                slot = (0 if kk == 0 else 6) + eb
                for fb in range(56):
                    tb_, tld, ti = tres.load("sync", t_d[kk, fb, :, cols])
                    wv = [tld]
                    if fb == 0:
                        wv.append(fread[slot])
                        wv.append(bbq[0])
                    t_ = k.op("tensor", "matmul", fpool[slot][:, :],
                              a2full[:, fb * 256 + eb * 128: fb * 256 + (eb + 1) * 128],
                              tb_[:, :], start=(fb == 0), stop=(fb == 55), wait=wv,
                              inc="PE")
                    tres.release(ti, "PE", t_)
        pe_fin = k.now("PE")
        for kk, (za2wk, mrk, sA, sB) in enumerate(
                [(za2w1, mr1, 0, 1), (za2w2, mr2, 6, 7)]):
            for eb in range(2):
                slot = sA if eb == 0 else sB
                off = eb * 1024 + TB * 512
                ve = k.op("vector", "tensor_tensor", wait=[("PE", pe_fin)],
                          out=za2wk[:, off:off + 512], in0=fpool[slot][:, :],
                          in1=mrk[:, off:off + 512], op=ALU.mult)
                fread[slot] = ("VE", ve)
    k.pop_phase()

    k.push_phase()
    dpool = [k.psum(f"dpool{i}", [128, 512], F32) for i in range(8)]
    dread = [("VE", 0)] * 8
    zl = LBuf(k, "zl", 2, [128, 4 * 1024], BF16)
    w2l = LBuf(k, "w2l", 2, [128, 4 * 512], BF16)
    xrl = LBuf(k, "xrl", 2, [128, 512], F32)
    B2s = k.sbuf("B2s", [128, 2 * 2048], BF16)
    k.mksem("B2L")
    b2ld = ("B2L", k.op("gpsimd", "dma_start",
                        out=B2s[:, :].rearrange("p (e j) -> p e j", j=2048),
                        in_=B2fT.rearrange("(e p) j -> p e j", p=128),
                        inc="B2L", incn=16))
    x1st = SOut(k, "x1st", 3, [128, 512], F32)
    for hb in range(4):
        hcols = slice(hb * 512, (hb + 1) * 512)
        mm_t = {}
        for g4 in range(14):
            zb_, zld, zi = zl.load(
                "sync", z_d[4 * g4:4 * g4 + 4, :, :].rearrange("f p t -> p f t"))
            wb_, wld, wi = w2l.load("gpsimd", w2R[hb, g4, :, :])
            for fi in range(4):
                fb = 4 * g4 + fi
                for g in range(8):
                    wv = []
                    if fi == 0 and g == 0:
                        wv = [zld, wld]
                    if fb == 0:
                        wv = wv + [dread[g]]
                    mm_t[g] = k.op("tensor", "matmul", dpool[g][:, :],
                                   zb_[:, fi * 1024 + g * 128: fi * 1024 + (g + 1) * 128],
                                   wb_[:, fi * 512:(fi + 1) * 512],
                                   start=(fb == 0), stop=False, wait=wv, inc="PE")
            zl.release(zi, "PE", mm_t[7])
            w2l.release(wi, "PE", mm_t[7])
        for g in range(8):
            for kk, za2wk in enumerate([za2w1, za2w2]):
                for eb in range(2):
                    wv = [b2ld] if (hb == 0 and g == 0 and kk == 0 and eb == 0) else []
                    mm_t[g] = k.op("tensor", "matmul", dpool[g][:, :],
                                   za2wk[:, eb * 1024 + g * 128: eb * 1024 + (g + 1) * 128],
                                   B2s[:, eb * 2048 + hb * 512: eb * 2048 + hb * 512 + 512],
                                   start=False, stop=(kk == 1 and eb == 1),
                                   wait=wv, inc="PE")
            xr_, xrw, xri = xrl.load("sync", xr_d[ts(g), hcols])
            sb_, bw, bi = x1st.acquire()
            ve = k.op("vector", "tensor_tensor",
                      wait=[("PE", mm_t[g]), xrw] + bw,
                      out=sb_[:, :], in0=dpool[g][:, :], in1=xr_[:, :], op=ALU.add)
            dread[g] = ("VE", ve)
            xrl.release(xri, "VE", ve)
            x1st.flush(bi, x1_d[ts(g), hcols], wait=[("VE", ve)])
    k.pop_phase()
    k.pop_phase()

    k.push_phase()
    xn1T = k.sbuf("xn1T", [128, 16 * 1024], BF16)
    xc = k.sbuf("xc", [128, 2048], F32)
    sq = k.sbuf("sq", [128, 2048], F32)
    st1 = k.sbuf("st1", [128, 8], F32)
    vst = k.sbuf("vst", [128, 2048], BF16)
    vTs = k.sbuf("vTs", [128, 16 * 1024], BF16)
    sqb = k.sbuf("sqb", [128, 512], F32)
    rstb = k.sbuf("rstb", [128, 512], F32)
    x1l = LBuf(k, "x1l", 2, [128, 2048], F32)
    wkl = LBuf(k, "wkl", 2, [128, 16 * 128], BF16)
    kst = SOut(k, "kst", 2, [128, 512], BF16)
    lpool = [k.psum(f"lpool{i}", [128, 512], F32) for i in range(3)]
    lread = [("VE", 0)] * 3
    tpl = TPose(k, "tpl", ident, alt=True)
    last_tp_pe = 0
    for g in range(8):
        xt, xldw, xsl = x1l.load("sync", x1_d[ts(g), :])
        k.op("vector", "reduce_sum", wait=[xldw], out=st1[:, 0:1],
             in_=xt[:, :], axis=AX.X)
        k.op("vector", "tensor_scalar", out=st1[:, 1:2], in0=st1[:, 0:1],
             scalar1=1.0 / 2048.0, scalar2=None, op0=ALU.mult)
        k.op("vector", "tensor_scalar", out=xc[:, :], in0=xt[:, :],
             scalar1=st1[:, 1:2], scalar2=None, op0=ALU.subtract)
        ve_rel = k.op("vector", "tensor_tensor", out=sq[:, :], in0=xc[:, :],
                      in1=xc[:, :], op=ALU.mult)
        x1l.release(xsl, "VE", ve_rel)
        k.op("vector", "reduce_sum", out=st1[:, 2:3], in_=sq[:, :], axis=AX.X)
        v_ve = k.op("vector", "tensor_scalar", out=st1[:, 3:4], in0=st1[:, 2:3],
                    scalar1=1.0 / 2048.0, scalar2=1e-5, op0=ALU.mult, op1=ALU.add)
        k.op("scalar", "activation", wait=[("VE", v_ve)], out=st1[:, 4:5],
             in_=st1[:, 3:4], func=AF.Ln)
        a2 = k.op("scalar", "activation", out=st1[:, 5:6], in_=st1[:, 4:5],
                  func=AF.Exp, scale=-0.5)
        ve2 = k.op("vector", "tensor_scalar", wait=[("AE", a2), ("PE", last_tp_pe)],
                   out=vst[:, :], in0=xc[:, :], scalar1=st1[:, 5:6], scalar2=None,
                   op0=ALU.mult)
        for c in range(16):
            pe, _ = tpl.emit(vst[:, ts(c)],
                             xn1T[:, c * 1024 + g * 128: c * 1024 + (g + 1) * 128],
                             extra_wait=[("VE", ve2)])
            last_tp_pe = pe
    xn1_ve = k.now("VE")
    xn1_ae = k.now("AE")
    for which, wT in [("k", wkR), ("v", wvR)]:
        for ob in range(16):
            _, wld, wi = wkl.load("gpsimd", wT[ob, :, :])
            for nh in range(2):
                sl = nh
                pe_last = None
                for c in range(16):
                    wv = ([wld, lread[sl], ("VE", xn1_ve), ("AE", xn1_ae)]
                          if c == 0 else [])
                    pe_last = k.op("tensor", "matmul", lpool[sl][:, :],
                                   wkl.bufs[wi][:, ts(c)],
                                   xn1T[:, c * 1024 + nh * 512: c * 1024 + nh * 512 + 512],
                                   start=(c == 0), stop=(c == 15), wait=wv,
                                   inc=("PE" if c == 15 else None))
                if which == "k":
                    ae_sq = k.op("scalar", "activation", wait=[("PE", pe_last)],
                                 out=sqb[:, :], in_=lpool[sl][:, :], func=AF.Square)
                    pe_ms = k.op("tensor", "matmul", lpool[2][:, :], onesfs[:, :],
                                 sqb[:, :], start=True, stop=True,
                                 wait=[("AE", ae_sq), lread[2], ("CN", CN)], inc="PE")
                    v_ts = k.op("vector", "tensor_scalar", wait=[("PE", pe_ms)],
                                out=rstb[:, :], in0=lpool[2][:, :],
                                scalar1=1.0 / 128.0, scalar2=RMS_EPS,
                                op0=ALU.mult, op1=ALU.add)
                    lread[2] = ("VE", v_ts)
                    k.op("scalar", "activation", wait=[("VE", v_ts)], out=rstb[:, :],
                         in_=rstb[:, :], func=AF.Ln)
                    ae_r = k.op("scalar", "activation", out=rstb[:, :], in_=rstb[:, :],
                                func=AF.Exp, scale=-0.5)
                    sb_, bw, bi = kst.acquire()
                    ve = k.op("vector", "tensor_tensor", wait=[("AE", ae_r)] + bw,
                              out=sb_[:, :], in0=lpool[sl][:, :], in1=rstb[:, :],
                              op=ALU.mult)
                    lread[sl] = ("VE", ve)
                    kst.flush(bi, kL1_d[ts(ob), nh * 512:(nh + 1) * 512],
                              wait=[("VE", ve)])
                else:
                    ve = k.op("vector", "tensor_copy", wait=[("PE", pe_last)],
                              out=vTs[:, ob * 1024 + nh * 512: ob * 1024 + nh * 512 + 512],
                              in_=lpool[sl][:, :])
                    lread[sl] = ("VE", ve)
            wkl.release(wi, "PE", k.now("PE"))
    ve_v = k.now("VE")
    k.mksem("STV")
    stv = 0
    for g in range(8):
        for c in range(16):
            tpl.emit(vTs[:, c * 1024 + g * 128: c * 1024 + (g + 1) * 128],
                     vst[:, ts(c)],
                     extra_wait=[("VE", ve_v)],
                     copy_extra_wait=[("STV", stv * 16)] if (c < 2 and stv) else [])
        stv += 1
        t_st = k.op("sync", "dma_start", out=vL1_d[ts(g), :], in_=vst[:, :],
                    wait=[("VE", k.now("VE")), ("AE", k.now("AE"))],
                    inc="STV", incn=16)
    k._note_final("STV", t_st)
    k.pop_phase()

    k.push_phase()
    wsz = k.sbuf("wsz", [128, 4], F32)
    th = k.sbuf("th", [128, 4], F32)
    nth = k.sbuf("nth", [128, 4], F32)
    ubuf = k.sbuf("ubuf", [128, 5632], BF16)
    vbuf = k.sbuf("vbuf", [128, 5632], BF16)
    wshl = LBuf(k, "wshl", 2, [128, 5632], F32)
    tst = SOut(k, "tst", 2, [128, 5632], BF16)
    ppool = k.psum("ppool", [128, 4], F32)
    k.mksem("WSL")
    ve_mz = k.op("vector", "memset", wsz[:, :], 0.0)
    wsld = ("WSL", k.op("sync", "dma_start", out=wsz[0:8, :], in_=wsall,
                        wait=[("VE", ve_mz)], inc="WSL", incn=16))
    pe_ws = k.op("tensor", "matmul", ppool[:, :], onesfs[:, :], wsz[:, :],
                 wait=[wsld, ("VE", ve_mz), ("CN", CN)], start=True, stop=True,
                 inc="PE")
    k.op("vector", "tensor_scalar", wait=[("PE", pe_ws)], out=th[:, :],
         in0=ppool[:, :], scalar1=NW2, scalar2=1e-5, op0=ALU.mult, op1=ALU.max)
    k.op("vector", "tensor_scalar", out=th[:, :], in0=th[:, :],
         scalar1=0.5, scalar2=None, op0=ALU.mult)
    k.op("vector", "tensor_scalar", out=nth[:, :], in0=th[:, :],
         scalar1=-1.0, scalar2=None, op0=ALU.mult)
    for (wap, nch, cols, j, td) in [(w1s, 2, 5632, 0, t1_d), (w2s, 6, 2048, 1, t2_d),
                                    (w3s, 2, 5632, 2, t3_d)]:
        for ch in range(nch):
            wb_, wld, wi = wshl.load("sync", wap[ts(ch), 0:cols],
                                     out_view=lambda b, cols=cols: b[:, 0:cols])
            k.op("vector", "tensor_scalar", wait=[wld], out=ubuf[:, 0:cols],
                 in0=wb_[:, 0:cols], scalar1=th[:, j:j+1], scalar2=None, op0=ALU.is_gt)
            ve_w = k.op("vector", "tensor_scalar", out=vbuf[:, 0:cols],
                        in0=wb_[:, 0:cols], scalar1=nth[:, j:j+1], scalar2=None,
                        op0=ALU.is_ge)
            wshl.release(wi, "VE", ve_w)
            sb_, bw, bi = tst.acquire()
            ve_t = k.op("vector", "tensor_tensor", wait=bw, out=sb_[:, 0:cols],
                        in0=ubuf[:, 0:cols], in1=vbuf[:, 0:cols], op=ALU.add)
            tst.flush(bi, td[ts(ch), 0:cols], wait=[("VE", ve_t)],
                      src_view=lambda b, cols=cols: b[:, 0:cols])
    k.pop_phase()
    k.finish()
    return k.build()


def prep_BCD(inp, resA):
    x = np.ascontiguousarray(inp["L0_attn_hidden_states"].reshape(8192, 2048))
    kTf = np.ascontiguousarray(np.concatenate([r["kT_d"] for r in resA], axis=1))
    vf = np.ascontiguousarray(np.concatenate([r["v_d"] for r in resA], axis=0))
    kRf = np.ascontiguousarray(
        kTf.reshape(16, 128, 64, 128).transpose(2, 1, 0, 3).reshape(64, 128, 2048))
    vRf = np.ascontiguousarray(
        vf.reshape(64, 128, 16, 128).transpose(2, 1, 0, 3).reshape(16, 128, 8192))
    wsall = np.ascontiguousarray(np.stack([r["wsums_d"][0] for r in resA]))
    woutT = np.ascontiguousarray(inp["L0_attn_w_out"].T)
    bout = np.ascontiguousarray(inp["L0_attn_b_out"].reshape(1, 2048))
    identb = np.eye(128, dtype=np.float32)
    onesf = np.ones((128, 128), dtype=np.float32)
    desc = np.arange(8, 0, -1, dtype=np.float32).reshape(1, 8)
    WgT = np.ascontiguousarray(inp["L0_ffn_Wg"].T)
    A1fT = np.ascontiguousarray(inp["L0_ffn_A1"].reshape(256, 2048).T)
    A3fT = np.ascontiguousarray(inp["L0_ffn_A3"].reshape(256, 2048).T)
    B1fT = np.ascontiguousarray(inp["L0_ffn_B1"].transpose(0, 2, 1).reshape(256, 7168))
    B3fT = np.ascontiguousarray(inp["L0_ffn_B3"].transpose(0, 2, 1).reshape(256, 7168))
    A2fT = np.ascontiguousarray(inp["L0_ffn_A2"].reshape(256, 7168).T)
    B2fT = np.ascontiguousarray(inp["L0_ffn_B2"].transpose(0, 2, 1).reshape(256, 2048))
    w1T0 = np.ascontiguousarray(inp["L0_ffn_W1"].T)
    w3T0 = np.ascontiguousarray(inp["L0_ffn_W3"].T)
    w2T0 = np.ascontiguousarray(inp["L0_ffn_W2"].T)
    w2R = np.empty((4, 14, 128, 4 * 512), dtype=np.float32)
    for hb in range(4):
        for g4 in range(14):
            for fi in range(4):
                w2R[hb, g4, :, fi * 512:(fi + 1) * 512] = \
                    w2T0[(4 * g4 + fi) * 128:(4 * g4 + fi + 1) * 128,
                         hb * 512:(hb + 1) * 512]
    w13R = np.empty((56, 128, 16 * 256), dtype=np.float32)
    w1R4 = w1T0.reshape(16, 128, 56, 128)
    w3R4 = w3T0.reshape(16, 128, 56, 128)
    for fb in range(56):
        blk = w13R[fb].reshape(128, 16, 256)
        blk[:, :, 0:128] = w1R4[:, :, fb, :].transpose(1, 0, 2)
        blk[:, :, 128:256] = w3R4[:, :, fb, :].transpose(1, 0, 2)
    bba2 = np.empty((56, 128, 768), dtype=np.float32)
    for fb in range(56):
        bba2[fb, :, 0:256] = B1fT[:, fb * 128:(fb + 1) * 128].reshape(2, 128, 128).transpose(1, 0, 2).reshape(128, 256)
        bba2[fb, :, 256:512] = B3fT[:, fb * 128:(fb + 1) * 128].reshape(2, 128, 128).transpose(1, 0, 2).reshape(128, 256)
        bba2[fb, :, 512:768] = A2fT[fb * 128:(fb + 1) * 128, :]
    a2R = np.ascontiguousarray(
        A2fT.reshape(56, 128, 256).transpose(1, 0, 2).reshape(128, 56 * 256))
    pat1 = np.zeros((128, 256), dtype=np.float32)
    pat2 = np.zeros((128, 256), dtype=np.float32)
    for e in range(8):
        pat1[e, e * 32:(e + 1) * 32] = 1.0
        pat2[e, e * 32:(e + 1) * 32] = 1.0
    wkT = np.ascontiguousarray(inp["L1_attn_wk"].T)
    wvT = np.ascontiguousarray(inp["L1_attn_wv"].T)
    _rt = lambda w: np.ascontiguousarray(
        w.reshape(16, 128, 16, 128).transpose(2, 1, 0, 3).reshape(16, 128, 2048))
    wkR = _rt(wkT)
    wvR = _rt(wvT)
    w1T = np.ascontiguousarray(inp["L1_ffn_w1"].T)
    w2T = np.ascontiguousarray(inp["L1_ffn_w2"].T)
    w3T = np.ascontiguousarray(inp["L1_ffn_w3"].T)
    w2Tp = np.zeros((6144, 2048), dtype=np.float32)
    w2Tp[:5632] = w2T
    maps = []
    for c in range(8):
        qTc = resA[c]["qT_d"]
        qRc = np.ascontiguousarray(
            qTc.reshape(16, 128, 2, 512).transpose(2, 1, 0, 3).reshape(2, 128, 8192))
        xnc = resA[c]["xnT_d"]
        xnRc = np.ascontiguousarray(
            xnc.reshape(16, 128, 1024).transpose(1, 0, 2).reshape(128, 16384))
        maps.append(dict(
            xA=np.ascontiguousarray(x[c * 1024:(c + 1) * 1024]),
            qR=qRc, kR=kRf, vR=vRf, xnR=xnRc,
            woutT=woutT, bout=bout, identb=identb, onesf=onesf, desc=desc,
            WgT=WgT, A1fT=A1fT, A3fT=A3fT, B1fT=B1fT, B3fT=B3fT,
            A2fT=A2fT, a2R=a2R, B2fT=B2fT, w13R=w13R, bba2=bba2, w2R=w2R,
            pat1=pat1, pat2=pat2, wkR=wkR, wvR=wvR, wsall=wsall,
            w1s=np.ascontiguousarray(w1T[c * 256:(c + 1) * 256]),
            w2s=np.ascontiguousarray(w2Tp[c * 768:(c + 1) * 768]),
            w3s=np.ascontiguousarray(w3T[c * 256:(c + 1) * 256]),
        ))
    return maps


def build_launch_EF(debug=False):
    k = K("EF")
    x1f = k.din("x1f", [1024, 2048], F32)
    knR = k.din("knR", [16, 128, 64 * 128], BF16)
    vnR = k.din("vnR", [8, 128, 64 * 256], BF16)
    wqR = k.din("wqR", [16, 128, 16 * 128], F32)
    woT = k.din("woT", [2048, 2048], F32)
    lqk = k.din("lqk", [4, 256], F32)
    m01R = k.din("m01R", [8, 128, 8 * 128], BF16)
    identb = k.din("identb", [128, 128], F32)
    onesf = k.din("onesf", [128, 128], F32)
    t1R = k.din("t1R", [11, 128, 16 * 512], BF16)
    t3R = k.din("t3R", [11, 128, 16 * 512], BF16)
    t2R = k.din("t2R", [4, 44, 128, 512], BF16)
    wsall = k.din("wsall", [8, 4], F32)
    out_d = k.dout("out_d", [1024, 2048], F32)
    dmk = k.dout if debug else k.dtmp
    x1r_d = dmk("x1r_d", [1024, 2048], F32)
    h_d = dmk("h_d", [1024, 5632], F32)
    hqT_d = dmk("hqT_d", [5632, 1024], BF16)
    lam_d = k.dtmp("lam_d", [1, 1], F32)
    sc_d = k.dtmp("sc_d", [1024, 4], F32)

    ident = k.sbuf("ident", [128, 128], BF16)
    onesfs = k.sbuf("onesfs", [128, 128], F32)
    onesb = k.sbuf("onesb", [128, 128], BF16)
    k.mksem("CN", "CNG")
    k.op("gpsimd", "dma_start", out=ident[:, :], in_=identb, inc="CNG", incn=16)
    k.op("gpsimd", "dma_start", out=onesb[:, :], in_=onesf, inc="CNG", incn=16)
    k.op("sync", "dma_start", out=onesfs[:, :], in_=onesf, inc="CN", incn=16)
    CN, CNG = k.now("CN"), k.now("CNG")
    vwc = lambda b: b[:, :].rearrange("p (c j) -> p c j", j=128)

    def wslab(src, ob):
        return src[:, ob * 128:(ob + 1) * 128].rearrange("(c p) j -> p c j", p=128)

    qn_d = k.dtmp("qn_d", [8, 128, 16 * 128], BF16)
    xqT_d = k.dtmp("xqT_d", [2048, 1024], BF16)
    sstat = k.sbuf("sstat", [128, 8 * 8], F32)
    lamn = k.sbuf("lamn", [128, 1], F32)
    k.push_phase()
    xn1T = k.sbuf("xn1T", [128, 16 * 1024], BF16)
    ltmp = k.sbuf("ltmp", [4, 256], F32)
    lsc = k.sbuf("lsc", [1, 8], F32)
    xqtile = k.sbuf("xqtile", [128, 16 * 128], BF16)
    k.mksem("XQS")
    k.mksem("LQK", "LAM")
    lrow = k.sbuf("lrow", [1, 1024], F32)
    k.op("sync", "dma_start", out=lrow[:, :],
         in_=lqk.rearrange("a b -> (a b)").unsqueeze(0), inc="LQK", incn=16)
    ve = k.op("vector", "tensor_tensor", wait=[("LQK", k.now("LQK"))],
              out=ltmp[0:1, 0:256], in0=lrow[:, 0:256], in1=lrow[:, 256:512],
              op=ALU.mult)
    k.op("vector", "reduce_sum", out=lsc[:, 0:1], in_=ltmp[0:1, 0:256], axis=AX.X)
    k.op("vector", "tensor_tensor", out=ltmp[0:1, 0:256], in0=lrow[:, 512:768],
         in1=lrow[:, 768:1024], op=ALU.mult)
    v_d2 = k.op("vector", "reduce_sum", out=lsc[:, 1:2], in_=ltmp[0:1, 0:256],
                axis=AX.X)
    ae1 = k.op("scalar", "activation", wait=[("VE", v_d2)], out=lsc[:, 2:3],
               in_=lsc[:, 0:1], func=AF.Exp)
    ae2 = k.op("scalar", "activation", out=lsc[:, 3:4], in_=lsc[:, 1:2], func=AF.Exp)
    k.op("vector", "tensor_tensor", wait=[("AE", ae2)], out=lsc[:, 4:5],
         in0=lsc[:, 3:4], in1=lsc[:, 2:3], op=ALU.subtract)
    ve_l = k.op("vector", "tensor_scalar", out=lsc[:, 5:6], in0=lsc[:, 4:5],
                scalar1=-0.8, scalar2=None, op0=ALU.add)
    st_l = k.op("sync", "dma_start", out=lam_d, in_=lsc[0:1, 5:6],
                wait=[("VE", ve_l)], inc="LAM", incn=16)
    k.mksem("LAM2")
    k.op("sync", "dma_start", wait=[("LAM", st_l)],
         out=lamn[:, :], in_=lam_d.broadcast_to((128, 1)), inc="LAM2", incn=16)

    xc = k.sbuf("xc", [128, 2048], F32)
    sq = k.sbuf("sq", [128, 2048], F32)
    st1 = k.sbuf("st1", [128, 16], F32)
    vst = k.sbuf("vst", [128, 2048], BF16)
    xnf = k.sbuf("xnf", [128, 2048], F32)
    ci32 = k.sbuf("ci32", [128, 2048], I32)
    cf32 = k.sbuf("cf32", [128, 2048], F32)
    dbuf = k.sbuf("dbuf", [128, 2048], F32)
    c1b = k.sbuf("c1b", [128, 2048], F32)
    qtm = k.sbuf("qtm", [128, 2048], BF16)
    x1l = LBuf(k, "x1l", 2, [128, 2048], F32)
    tpp = TPose(k, "tpp", ident, alt=True)
    wsz = k.sbuf("wsz", [128, 4], F32)
    msw = k.sbuf("msw", [128, 4], F32)
    k.mksem("WSL")
    ve_mz = k.op("vector", "memset", wsz[:, :], 0.0)
    wsld = ("WSL", k.op("sync", "dma_start", out=wsz[0:8, :], in_=wsall,
                        wait=[("VE", ve_mz)], inc="WSL", incn=16))
    ppool0 = k.psum("ppool0", [128, 4], F32)
    pe_ws = k.op("tensor", "matmul", ppool0[:, :], onesfs[:, :], wsz[:, :],
                 wait=[wsld, ("VE", ve_mz), ("CN", CN)], start=True, stop=True,
                 inc="PE")
    k.op("vector", "tensor_scalar", wait=[("PE", pe_ws)], out=msw[:, :],
         in0=ppool0[:, :], scalar1=NW2, scalar2=1e-5, op0=ALU.mult, op1=ALU.max)
    LN127 = float(np.log(np.float32(127.0)))
    ln127b = k.sbuf("ln127b", [128, 1], F32)
    k.op("vector", "memset", ln127b[:, :], LN127)
    last_tp_pe = 0
    for j in range(8):
        xt, xldw, xsl = x1l.load("sync", x1f[ts(j), :])
        k.op("vector", "reduce_sum", wait=[xldw], out=st1[:, 0:1],
             in_=xt[:, :], axis=AX.X)
        k.op("vector", "tensor_scalar", out=st1[:, 1:2], in0=st1[:, 0:1],
             scalar1=1.0 / 2048.0, scalar2=None, op0=ALU.mult)
        k.op("vector", "tensor_scalar", out=xc[:, :], in0=xt[:, :],
             scalar1=st1[:, 1:2], scalar2=None, op0=ALU.subtract)
        ve_rel = k.op("vector", "tensor_tensor", out=sq[:, :], in0=xc[:, :],
                      in1=xc[:, :], op=ALU.mult)
        x1l.release(xsl, "VE", ve_rel)
        k.op("vector", "reduce_sum", out=st1[:, 2:3], in_=sq[:, :], axis=AX.X)
        v_ve = k.op("vector", "tensor_scalar", out=st1[:, 3:4], in0=st1[:, 2:3],
                    scalar1=1.0 / 2048.0, scalar2=1e-5, op0=ALU.mult, op1=ALU.add)
        k.op("scalar", "activation", wait=[("VE", v_ve)], out=st1[:, 4:5],
             in_=st1[:, 3:4], func=AF.Ln)
        a2 = k.op("scalar", "activation", out=st1[:, 5:6], in_=st1[:, 4:5],
                  func=AF.Exp, scale=-0.5)
        ve_xn = k.op("vector", "tensor_scalar", wait=[("AE", a2)],
                     out=xnf[:, :], in0=xc[:, :], scalar1=st1[:, 5:6], scalar2=None,
                     op0=ALU.mult)
        ve2 = k.op("vector", "tensor_copy", wait=[("PE", last_tp_pe)],
                   out=vst[:, :], in_=xnf[:, :])
        for c in range(16):
            pe, _ = tpp.emit(vst[:, ts(c)],
                             xn1T[:, c * 1024 + j * 128: c * 1024 + (j + 1) * 128],
                             extra_wait=[("VE", ve2), ("CNG", CNG)])
            last_tp_pe = pe
        k.op("vector", "tensor_reduce", out=st1[:, 6:7], in_=xnf[:, :], axis=AX.X,
             op=ALU.max, apply_absolute_value=True)
        v_am = k.op("vector", "tensor_scalar", out=st1[:, 7:8], in0=st1[:, 6:7],
                    scalar1=1e-5, scalar2=None, op0=ALU.max)
        k.op("scalar", "activation", wait=[("VE", v_am)], out=st1[:, 8:9],
             in_=st1[:, 7:8], func=AF.Ln)
        a_sx = k.op("scalar", "activation", out=st1[:, 9:10], in_=st1[:, 8:9],
                    func=AF.Exp, scale=-1.0, bias=ln127b[:, 0:1])
        k.op("vector", "tensor_scalar", wait=[("AE", a_sx)], out=st1[:, 10:11],
             in0=st1[:, 7:8], scalar1=1.0 / 127.0, scalar2=None, op0=ALU.mult)
        k.op("vector", "tensor_scalar", out=xnf[:, :], in0=xnf[:, :],
             scalar1=st1[:, 9:10], scalar2=None, op0=ALU.mult)
        k.op("vector", "tensor_copy", out=ci32[:, :], in_=xnf[:, :])
        k.op("vector", "tensor_copy", out=cf32[:, :], in_=ci32[:, :])
        k.op("vector", "tensor_tensor", out=dbuf[:, :], in0=xnf[:, :],
             in1=cf32[:, :], op=ALU.subtract)
        k.op("vector", "scalar_tensor_tensor", out=c1b[:, :], in0=dbuf[:, :],
             scalar=0.5, in1=cf32[:, :], op0=ALU.is_gt, op1=ALU.add)
        ve_q = k.op("vector", "scalar_tensor_tensor", out=qtm[:, :], in0=dbuf[:, :],
                    scalar=-0.5, in1=c1b[:, :], op0=ALU.is_lt, op1=ALU.subtract)
        k.op("vector", "reduce_sum", out=sstat[:, j * 8 + 0: j * 8 + 1],
             in_=qtm[:, :], axis=AX.X)
        k.op("vector", "tensor_tensor", out=sstat[:, j * 8 + 1: j * 8 + 2],
             in0=st1[:, 10:11], in1=msw[:, 0:1], op=ALU.mult)
        k.op("vector", "tensor_scalar", out=sstat[:, j * 8 + 1: j * 8 + 2],
             in0=sstat[:, j * 8 + 1: j * 8 + 2], scalar1=-1.0, scalar2=None,
             op0=ALU.mult)
        k.op("vector", "tensor_tensor", out=sstat[:, j * 8 + 2: j * 8 + 3],
             in0=st1[:, 10:11], in1=msw[:, 2:3], op=ALU.mult)
        k.op("vector", "tensor_scalar", out=sstat[:, j * 8 + 2: j * 8 + 3],
             in0=sstat[:, j * 8 + 2: j * 8 + 3], scalar1=-1.0, scalar2=None,
             op0=ALU.mult)
        ve_sx = k.op("vector", "tensor_copy", out=sstat[:, j * 8 + 3: j * 8 + 4],
                     in_=st1[:, 10:11])
        for c in range(16):
            pe, _ = tpp.emit(qtm[:, ts(c)], xqtile[:, ts(c)],
                             extra_wait=[("VE", ve_q)],
                             copy_extra_wait=[("XQS", j * 16)] if c < 2 else [])
            last_tp_pe = pe
        k.op("sync", "dma_start", wait=[("VE", k.now("VE")), ("AE", k.now("AE"))],
             out=xqT_d[:, j * 128:(j + 1) * 128].rearrange("(c p) o -> p c o", p=128),
             in_=xqtile[:, :].rearrange("p (c o) -> p c o", o=128),
             inc="XQS", incn=16)
    xn1_ve = k.now("VE")
    xn1_ae = k.now("AE")
    wql = LBuf(k, "wql", 2, [128, 16 * 128], BF16)
    qstg = SOut(k, "qstg", 2, [128, 512], BF16)
    sqb = k.sbuf("sqb", [128, 512], F32)
    rstb = k.sbuf("rstb", [128, 512], F32)
    qpool = [k.psum(f"qpool{i}", [128, 512], F32) for i in range(3)]
    qread = [("VE", 0)] * 3
    for ob in range(16):
        _, wld, wi = wql.load("gpsimd", wqR[ob, :, :])
        for nh in range(2):
            sl = nh
            pe_last = None
            for c in range(16):
                wv = ([wld, qread[sl], ("VE", xn1_ve), ("AE", xn1_ae)]
                      if c == 0 else [])
                pe_last = k.op("tensor", "matmul", qpool[sl][:, :],
                               wql.bufs[wi][:, ts(c)],
                               xn1T[:, c * 1024 + nh * 512: c * 1024 + nh * 512 + 512],
                               start=(c == 0), stop=(c == 15), wait=wv,
                               inc=("PE" if c == 15 else None))
            ae_sq = k.op("scalar", "activation", wait=[("PE", pe_last)],
                         out=sqb[:, :], in_=qpool[sl][:, :], func=AF.Square)
            pe_ms = k.op("tensor", "matmul", qpool[2][:, :], onesfs[:, :],
                         sqb[:, :], start=True, stop=True,
                         wait=[("AE", ae_sq), qread[2], ("CN", CN)], inc="PE")
            v_ts = k.op("vector", "tensor_scalar", wait=[("PE", pe_ms)],
                        out=rstb[:, :], in0=qpool[2][:, :],
                        scalar1=1.0 / 128.0, scalar2=RMS_EPS,
                        op0=ALU.mult, op1=ALU.add)
            qread[2] = ("VE", v_ts)
            k.op("scalar", "activation", wait=[("VE", v_ts)], out=rstb[:, :],
                 in_=rstb[:, :], func=AF.Ln)
            ae_r = k.op("scalar", "activation", out=rstb[:, :], in_=rstb[:, :],
                        func=AF.Exp, scale=-0.5)
            qb_, qbw, qbi = qstg.acquire()
            ve = k.op("vector", "scalar_tensor_tensor", wait=[("AE", ae_r)] + qbw,
                      out=qb_[:, :],
                      in0=qpool[sl][:, :], scalar=float(1.0 / np.sqrt(np.float32(128.0))),
                      in1=rstb[:, :], op0=ALU.mult, op1=ALU.mult)
            qread[sl] = ("VE", ve)
            for jj in range(4):
                qstg.flush(qbi, qn_d[nh * 4 + jj, :, ob * 128:(ob + 1) * 128],
                           wait=[("VE", ve)],
                           src_view=lambda b, jj=jj: b[:, jj * 128:(jj + 1) * 128])
        wql.release(wi, "PE", k.now("PE"))
    k.pop_phase()

    k.push_phase()
    PstB = [k.sbuf(f"Pst1_{i}", [128, 64 * 256], BF16) for i in range(2)]
    pst_rel = [("PE", 0), ("PE", 0)]
    mbias = [k.sbuf(f"mbias{i}", [128, 8 * 128], BF16) for i in range(2)]
    k.mksem("MBL0", "MBL1")
    mb_use = [("VE", 0), ("VE", 0)]
    mb_ld = [("MBL0", 0), ("MBL1", 0)]
    rtmp = k.sbuf("rtmp1", [128, 256], F32)
    rinv12 = k.sbuf("rinv12", [128, 256], F32)
    r2l = k.sbuf("r2l", [128, 128], F32)
    ya = k.sbuf("ya", [128, 128], F32)
    yb = k.sbuf("yb", [128, 128], F32)
    yT = k.sbuf("yT", [128, 16 * 128], BF16)
    ksl = LBuf(k, "ksl1", 2, [128, 64 * 128], BF16)
    vsl = LBuf(k, "vsl1", 2, [128, 64 * 256], BF16)
    qnl = LBuf(k, "qnl", 2, [128, 16 * 128], BF16)
    wol = LBuf(k, "wol1", 2, [128, 8 * 512], BF16)
    x1fl = LBuf(k, "x1fl", 2, [128, 512], F32)
    xrst = SOut(k, "xrst1", 3, [128, 512], F32)
    sA = k.psum("sA", [128, 1024], F32)
    sB = k.psum("sB", [128, 1024], F32)
    spool = [sA, sB]
    prs = k.psum("prs", [128, 256], F32)
    pc1 = k.psum("pc1", [128, 256], F32)
    pc2 = k.psum("pc2", [128, 256], F32)
    ppj = k.psum("ppj", [128, 512], F32)
    sread = [("AE", 0), ("AE", 0)]
    rsbox = [("AE", 0)]
    cread = [("VE", 0), ("VE", 0)]

    seq = [(j, h) for j in range(8) for h in range(8)]
    stash = {}

    def stageA(j, h):
        nkt = 8 * j + 8
        pb = (j * 8 + h) % 2
        Pst = PstB[pb]
        if h == 0:
            mi = j % 2
            mld = k.op("sync", "dma_start", wait=[mb_use[mi]],
                       out=mbias[mi][:, :], in_=m01R[j, :, :],
                       inc=f"MBL{mi}", incn=16)
            mb_ld[mi] = (f"MBL{mi}", mld)
            qb_, qld_, qi_ = qnl.load("sync", qn_d[j, :, :])
            stash[("q", j)] = (qb_, qld_, qi_)
        qb_, qld_, qi_ = stash[("q", j)]
        kbufs = []
        for sx in range(2):
            kb_, kld_, ki_ = ksl.load(
                "sync", knR[h * 2 + sx, :, 0:nkt * 128],
                out_view=lambda b: b[:, 0:nkt * 128])
            kbufs.append((kb_, kld_, ki_))
        vb_, vld_, vi_ = vsl.load(
            "sync", vnR[h, :, 0:nkt * 256],
            out_view=lambda b: b[:, 0:nkt * 256])
        exps = []
        for kt0 in range(0, nkt, 4):
            sl = (kt0 // 4) % 2
            pe_l = None
            for dkt in range(min(4, nkt - kt0)):
                kt = kt0 + dkt
                for sx in range(2):
                    wv = []
                    if dkt == 0 and sx == 0:
                        wv = [kbufs[0][1], kbufs[1][1], qld_, sread[sl]]
                        if kt0 == 0:
                            wv.append(pst_rel[pb])
                    pe_l = k.op("tensor", "matmul",
                                spool[sl][:, dkt * 256 + sx * 128: dkt * 256 + (sx + 1) * 128],
                                kbufs[sx][0][:, kt * 128:(kt + 1) * 128],
                                qb_[:, (h * 2 + sx) * 128:(h * 2 + sx + 1) * 128],
                                start=True, stop=True, wait=wv, inc="PE")
            nb = min(4, nkt - kt0)
            ae = k.op("scalar", "activation", wait=[("PE", pe_l)],
                      out=Pst[:, kt0 * 256:(kt0 + nb) * 256],
                      in_=spool[sl][:, 0:nb * 256], func=AF.Exp)
            sread[sl] = ("AE", ae)
        stash[(j, h)] = (kbufs, vb_, vld_, vi_, k.now("AE"))

    def stageB(j, h):
        nkt = 8 * j + 8
        pb = (j * 8 + h) % 2
        Pst = PstB[pb]
        mi = j % 2
        kbufs, vb_, vld_, vi_, exp_last = stash.pop((j, h))
        ve_m = None
        for i in range(8):
            kt = 8 * j + i
            for sx in range(2):
                wv = []
                if i == 0 and sx == 0:
                    wv = [mb_ld[mi], ("AE", exp_last)]
                ve_m = k.op("vector", "tensor_tensor", wait=wv,
                            out=Pst[:, kt * 256 + sx * 128: kt * 256 + (sx + 1) * 128],
                            in0=Pst[:, kt * 256 + sx * 128: kt * 256 + (sx + 1) * 128],
                            in1=mbias[mi][:, i * 128:(i + 1) * 128], op=ALU.mult)
        mb_use[mi] = ("VE", ve_m)
        pe_r = None
        for kt in range(nkt):
            wv = [("VE", ve_m), rsbox[0]] if kt == 0 else []
            pe_r = k.op("tensor", "matmul", prs[:, 0:256], onesb[:, :],
                        Pst[:, kt * 256:(kt + 1) * 256],
                        start=(kt == 0), stop=(kt == nkt - 1), wait=wv,
                        inc=("PE" if kt == nkt - 1 else None))
        aeL = k.op("scalar", "activation", wait=[("PE", pe_r)], out=rtmp[:, :],
                   in_=prs[:, 0:256], func=AF.Ln)
        rsbox[0] = ("AE", aeL)
        aeR = k.op("scalar", "activation", out=rinv12[:, :], in_=rtmp[:, :],
                   func=AF.Exp, scale=-1.0)
        ve_rl = k.op("vector", "tensor_scalar",
                     wait=[("AE", aeR), ("LAM2", 16)],
                     out=r2l[:, :], in0=rinv12[:, 128:256],
                     scalar1=lamn[:, 0:1], scalar2=None, op0=ALU.mult)
        pe_c_last = 0
        for db in range(2):
            pcx = pc1 if db == 0 else pc2
            pe_c = None
            for kt in range(nkt):
                wv = [vld_, cread[db]] if kt == 0 else []
                pe_c = k.op("tensor", "matmul", pcx[:, 0:256],
                            vb_[:, kt * 256 + db * 128: kt * 256 + (db + 1) * 128],
                            Pst[:, kt * 256:(kt + 1) * 256],
                            start=(kt == 0), stop=(kt == nkt - 1), wait=wv,
                            inc=("PE" if kt == nkt - 1 else None))
            pe_c_last = pe_c
            ve_a = k.op("vector", "tensor_tensor", wait=[("PE", pe_c), ("VE", ve_rl)],
                        out=ya[:, :], in0=pcx[:, 0:128],
                        in1=rinv12[:, 0:128], op=ALU.mult)
            ve_b = k.op("vector", "tensor_tensor",
                        out=yb[:, :], in0=pcx[:, 128:256],
                        in1=r2l[:, :], op=ALU.mult)
            ve_y = k.op("vector", "tensor_tensor",
                        out=yT[:, (h * 2 + db) * 128:(h * 2 + db + 1) * 128],
                        in0=ya[:, :], in1=yb[:, :], op=ALU.add)
            cread[db] = ("VE", ve_y)
        pst_rel[pb] = ("PE", pe_c_last)
        ksl.release(kbufs[0][2], "PE", pe_c_last)
        ksl.release(kbufs[1][2], "PE", pe_c_last)
        vsl.release(vi_, "PE", pe_c_last)

    def outproj(j):
        ve_yT = k.now("VE")
        for ob in range(4):
            wolds = []
            for hf in range(2):
                wb_, wld_, wslot_ = wol.load(
                    "gpsimd",
                    woT[hf * 1024:(hf + 1) * 1024, ob * 512:(ob + 1) * 512]
                    .rearrange("(c p) o -> p c o", p=128),
                    out_view=lambda b: b[:, :].rearrange("p (c o) -> p c o", o=512))
                wolds.append((wld_, wb_, wslot_))
            pe_last = None
            for cc in range(16):
                wv = []
                if cc == 0:
                    wv = [wolds[0][0], pj[0], ("VE", ve_yT)]
                if cc == 8:
                    wv = [wolds[1][0]]
                wbuf = wolds[0][1] if cc < 8 else wolds[1][1]
                pe_last = k.op("tensor", "matmul", ppj[:, :],
                               yT[:, cc * 128:(cc + 1) * 128],
                               wbuf[:, (cc % 8) * 512:(cc % 8 + 1) * 512],
                               start=(cc == 0), stop=(cc == 15), wait=wv,
                               inc=("PE" if cc == 15 else None))
            wol.release(wolds[0][2], "PE", pe_last)
            wol.release(wolds[1][2], "PE", pe_last)
            xt_, xw_, xi_ = x1fl.load("sync", x1f[ts(j), ob * 512:(ob + 1) * 512])
            sb_, bw, bi = xrst.acquire()
            ve = k.op("vector", "scalar_tensor_tensor",
                      wait=[("PE", pe_last), xw_] + bw,
                      out=sb_[:, :], in0=ppj[:, :], scalar=0.2,
                      in1=xt_[:, :], op0=ALU.mult, op1=ALU.add)
            pj[0] = ("VE", ve)
            x1fl.release(xi_, "VE", ve)
            xrst.flush(bi, x1r_d[ts(j), ob * 512:(ob + 1) * 512], wait=[("VE", ve)])
        qb_, qld_, qi_ = stash.pop(("q", j))
        qnl.release(qi_, "PE", k.now("PE"))

    pj = [("VE", 0)]
    for idx, (j, h) in enumerate(seq):
        stageA(j, h)
        if idx > 0:
            pj_, ph_ = seq[idx - 1]
            stageB(pj_, ph_)
            if ph_ == 7:
                outproj(pj_)
    stageB(7, 7)
    outproj(7)
    k.pop_phase()

    k.push_phase()
    xqT = k.sbuf("xqT", [128, 16 * 1024], BF16)
    k.mksem("XQL")
    XQL = ("XQL", k.op("sync", "dma_start",
                       out=xqT[:, :].rearrange("p (c t) -> p c t", t=1024),
                       in_=xqT_d.rearrange("(c p) t -> p c t", p=128),
                       inc="XQL", incn=16))
    t1l = LBuf(k, "t1l", 2, [128, 16 * 512], BF16)
    t3l = LBuf(k, "t3l", 2, [128, 16 * 512], BF16)
    o1b = k.sbuf("o1b", [128, 512], F32)
    o3b = k.sbuf("o3b", [128, 512], F32)
    s1b = k.sbuf("s1b", [128, 512], F32)
    hst = SOut(k, "hst", 2, [128, 512], F32)
    hpool = [k.psum(f"hpool{i}", [128, 512], F32) for i in range(4)]
    hread = [("VE", 0)] * 4
    vw512b = lambda b: b[:, :].rearrange("p (c j) -> p c j", j=512)
    for fcol in range(11):
        _, t1ld, t1i = t1l.load("sync", t1R[fcol, :, :])
        _, t3ld, t3i = t3l.load("sync", t3R[fcol, :, :])
        for j in range(8):
            slA, slB = (0, 1) if j % 2 == 0 else (2, 3)
            pe1 = None
            for c in range(16):
                wv = [t1ld, hread[slA], XQL] if c == 0 else []
                pe1 = k.op("tensor", "matmul", hpool[slA][:, :],
                           xqT[:, c * 1024 + j * 128: c * 1024 + (j + 1) * 128],
                           t1l.bufs[t1i][:, c * 512:(c + 1) * 512],
                           start=(c == 0), stop=(c == 15), wait=wv,
                           inc=("PE" if c == 15 else None))
            pe3 = None
            for c in range(16):
                wv = [t3ld, hread[slB]] if c == 0 else []
                pe3 = k.op("tensor", "matmul", hpool[slB][:, :],
                           xqT[:, c * 1024 + j * 128: c * 1024 + (j + 1) * 128],
                           t3l.bufs[t3i][:, c * 512:(c + 1) * 512],
                           start=(c == 0), stop=(c == 15), wait=wv,
                           inc=("PE" if c == 15 else None))
            ve_o1 = k.op("vector", "tensor_scalar", wait=[("PE", pe1)],
                         out=o1b[:, :], in0=hpool[slA][:, :],
                         scalar1=sstat[:, j * 8 + 0: j * 8 + 1],
                         scalar2=sstat[:, j * 8 + 1: j * 8 + 2],
                         op0=ALU.subtract, op1=ALU.mult)
            hread[slA] = ("VE", ve_o1)
            ae_s = k.op("scalar", "activation", wait=[("VE", ve_o1)],
                        out=s1b[:, :], in_=o1b[:, :], func=AF.Silu)
            ve_o3 = k.op("vector", "tensor_scalar", wait=[("PE", pe3)],
                         out=o3b[:, :], in0=hpool[slB][:, :],
                         scalar1=sstat[:, j * 8 + 0: j * 8 + 1],
                         scalar2=sstat[:, j * 8 + 2: j * 8 + 3],
                         op0=ALU.subtract, op1=ALU.mult)
            hread[slB] = ("VE", ve_o3)
            sb_, bw, bi = hst.acquire()
            ve_h = k.op("vector", "tensor_tensor", wait=[("AE", ae_s)] + bw,
                        out=sb_[:, :], in0=s1b[:, :], in1=o3b[:, :], op=ALU.mult)
            hst.flush(bi, h_d[ts(j), fcol * 512:(fcol + 1) * 512],
                      wait=[("VE", ve_h)])
        t1l.release(t1i, "PE", k.now("PE"))
        t3l.release(t3i, "PE", k.now("PE"))
    k.pop_phase()

    k.push_phase()
    hld = LBuf(k, "hld", 2, [128, 5632], F32)
    hc32 = k.sbuf("hc32", [128, 5632], I32)
    hcf = k.sbuf("hcf", [128, 5632], F32)
    hdb = k.sbuf("hdb", [128, 5632], F32)
    hc1 = k.sbuf("hc1", [128, 5632], F32)
    hqtm = k.sbuf("hqtm", [128, 5632], BF16)
    hqT = k.sbuf("hqT", [128, 44 * 128], BF16)
    hstt = k.sbuf("hstt", [128, 16], F32)
    ln127c = k.sbuf("ln127c", [128, 1], F32)
    k.op("vector", "memset", ln127c[:, :], LN127)
    tph = TPose(k, "tph", ident, alt=True)
    k.mksem("HQS")
    hq_st = 0
    for j in range(8):
        hb_, hldw, hi = hld.load("sync", h_d[ts(j), :])
        k.op("vector", "tensor_reduce", wait=[hldw], out=hstt[:, 0:1],
             in_=hb_[:, :], axis=AX.X, op=ALU.max, apply_absolute_value=True)
        v_am = k.op("vector", "tensor_scalar", out=hstt[:, 1:2], in0=hstt[:, 0:1],
                    scalar1=1e-5, scalar2=None, op0=ALU.max)
        k.op("scalar", "activation", wait=[("VE", v_am)], out=hstt[:, 2:3],
             in_=hstt[:, 1:2], func=AF.Ln)
        a_sx = k.op("scalar", "activation", out=hstt[:, 3:4], in_=hstt[:, 2:3],
                    func=AF.Exp, scale=-1.0, bias=ln127c[:, 0:1])
        k.op("vector", "tensor_scalar", wait=[("AE", a_sx)], out=hstt[:, 4:5],
             in0=hstt[:, 1:2], scalar1=1.0 / 127.0, scalar2=None, op0=ALU.mult)
        ve_y = k.op("vector", "tensor_scalar", out=hdb[:, :], in0=hb_[:, :],
                    scalar1=hstt[:, 3:4], scalar2=None, op0=ALU.mult)
        a_c1 = k.op("scalar", "copy", wait=[("VE", ve_y)], out=hc32[:, :],
                    in_=hdb[:, :])
        a_c2 = k.op("scalar", "copy", out=hcf[:, :], in_=hc32[:, :])
        ve_rel = k.op("vector", "tensor_tensor", wait=[("AE", a_c2)],
                      out=hdb[:, :], in0=hdb[:, :],
                      in1=hcf[:, :], op=ALU.subtract)
        hld.release(hi, "VE", ve_rel)
        k.op("vector", "scalar_tensor_tensor", out=hc1[:, :], in0=hdb[:, :],
             scalar=0.5, in1=hcf[:, :], op0=ALU.is_gt, op1=ALU.add)
        ve_q = k.op("vector", "scalar_tensor_tensor", out=hqtm[:, :], in0=hdb[:, :],
                    scalar=-0.5, in1=hc1[:, :], op0=ALU.is_lt, op1=ALU.subtract)
        k.op("vector", "reduce_sum", out=sstat[:, j * 8 + 4: j * 8 + 5],
             in_=hqtm[:, :], axis=AX.X)
        ve_sx = k.op("vector", "tensor_copy", out=sstat[:, j * 8 + 5: j * 8 + 6],
                     in_=hstt[:, 4:5])
        for c in range(44):
            pe, ve_c = tph.emit(hqtm[:, ts(c)], hqT[:, ts(c)],
                                extra_wait=[("VE", ve_q)],
                                copy_extra_wait=[("HQS", hq_st)] if c < 2 else [])
        st = k.op("sync", "dma_start",
                  wait=[("VE", k.now("VE")), ("AE", k.now("AE")), ("HQS", hq_st)],
                  out=hqT_d[:, j * 128:(j + 1) * 128]
                  .rearrange("(c p) o -> p c o", p=128),
                  in_=hqT[:, :].rearrange("p (c o) -> p c o", o=128),
                  inc="HQS", incn=16)
        hq_st = st
    k._note_final("HQS", hq_st)
    k.pop_phase()

    k.push_phase()
    wsz2 = k.sbuf("wsz2", [128, 4], F32)
    msw2 = k.sbuf("msw2", [128, 4], F32)
    k.mksem("WSL2")
    ve_mz = k.op("vector", "memset", wsz2[:, :], 0.0)
    wsld = ("WSL2", k.op("sync", "dma_start", out=wsz2[0:8, :], in_=wsall,
                         wait=[("VE", ve_mz)], inc="WSL2", incn=16))
    k.push_phase()
    ppool2 = k.psum("ppool2", [128, 4], F32)
    pe_ws = k.op("tensor", "matmul", ppool2[:, :], onesfs[:, :], wsz2[:, :],
                 wait=[wsld, ("VE", ve_mz)], start=True, stop=True, inc="PE")
    k.op("vector", "tensor_scalar", wait=[("PE", pe_ws)], out=msw2[:, :],
         in0=ppool2[:, :], scalar1=NW2, scalar2=1e-5, op0=ALU.mult, op1=ALU.max)
    k.pop_phase()
    for j in range(8):
        k.op("vector", "tensor_tensor", out=sstat[:, j * 8 + 6: j * 8 + 7],
             in0=sstat[:, j * 8 + 5: j * 8 + 6], in1=msw2[:, 1:2], op=ALU.mult)
        k.op("vector", "tensor_scalar", out=sstat[:, j * 8 + 6: j * 8 + 7],
             in0=sstat[:, j * 8 + 6: j * 8 + 7], scalar1=-1.0, scalar2=None,
             op0=ALU.mult)
    hql = LBuf(k, "hql", 2, [128, 1024], BF16)
    t2l = LBuf(k, "t2l", 2, [128, 512], BF16)
    xrl = LBuf(k, "xrl2", 2, [128, 512], F32)
    ost = SOut(k, "ost", 3, [128, 512], F32)
    tbv = k.sbuf("tbv", [128, 512], F32)
    opool = [k.psum(f"opool{i}", [128, 512], F32) for i in range(8)]
    oread = [("VE", 0)] * 8
    for hb in range(4):
        hcols = slice(hb * 512, (hb + 1) * 512)
        mm_t = {}
        for fc in range(44):
            hb_, hqld, hqi = hql.load("sync", hqT_d[fc * 128:(fc + 1) * 128, :])
            tb_, t2ld, t2i = t2l.load("sync", t2R[hb, fc, :, :])
            for j in range(8):
                wv = [hqld, t2ld]
                if fc == 0:
                    wv = wv + [oread[j]]
                mm_t[j] = k.op("tensor", "matmul", opool[j][:, :],
                               hb_[:, ts(j)], tb_[:, :],
                               start=(fc == 0), stop=(fc == 43), wait=wv, inc="PE")
            hql.release(hqi, "PE", mm_t[7])
            t2l.release(t2i, "PE", mm_t[7])
        for j in range(8):
            xr_, xrw, xri = xrl.load("sync", x1r_d[ts(j), hcols])
            ve_t = k.op("vector", "tensor_scalar", wait=[("PE", mm_t[j])],
                        out=tbv[:, :], in0=opool[j][:, :],
                        scalar1=sstat[:, j * 8 + 4: j * 8 + 5],
                        scalar2=sstat[:, j * 8 + 6: j * 8 + 7],
                        op0=ALU.subtract, op1=ALU.mult)
            oread[j] = ("VE", ve_t)
            sb_, bw, bi = ost.acquire()
            ve = k.op("vector", "tensor_tensor", wait=[xrw] + bw,
                      out=sb_[:, :], in0=tbv[:, :], in1=xr_[:, :], op=ALU.add)
            xrl.release(xri, "VE", ve)
            ost.flush(bi, out_d[ts(j), hcols], wait=[("VE", ve)])
    k.pop_phase()
    k.finish()
    return k.build()


FOLD = [tuple(8 * j + c for j in range(8)) for c in range(8)]


def prep_EF(inp, resB):
    x1 = np.concatenate([r["x1_d"] for r in resB], axis=0)
    knf = np.concatenate([r["kL1_d"] for r in resB], axis=1)
    vnf = np.concatenate([r["vL1_d"] for r in resB], axis=0)
    t1f = np.concatenate([r["t1_d"] for r in resB], axis=0)
    t3f = np.concatenate([r["t3_d"] for r in resB], axis=0)
    t2f = np.concatenate([r["t2_d"] for r in resB], axis=0)[:5632]
    knR = np.ascontiguousarray(knf.reshape(16, 128, 8192))
    vnR = np.ascontiguousarray(
        vnf.reshape(64, 128, 8, 256).transpose(2, 1, 0, 3).reshape(8, 128, 64 * 256))
    t1R = np.ascontiguousarray(
        t1f.reshape(16, 128, 11, 512).transpose(2, 1, 0, 3).reshape(11, 128, 16 * 512))
    t3R = np.ascontiguousarray(
        t3f.reshape(16, 128, 11, 512).transpose(2, 1, 0, 3).reshape(11, 128, 16 * 512))
    t2R = np.ascontiguousarray(
        t2f.reshape(44, 128, 4, 512).transpose(2, 0, 1, 3))
    wqT = np.ascontiguousarray(inp["L1_attn_wq"].T)
    wqR = np.ascontiguousarray(
        wqT.reshape(16, 128, 16, 128).transpose(2, 1, 0, 3).reshape(16, 128, 2048))
    woT = np.ascontiguousarray(inp["L1_attn_wo"].T)
    lqk = np.stack([inp["L1_attn_lq1"], inp["L1_attn_lk1"],
                    inp["L1_attn_lq2"], inp["L1_attn_lk2"]]).astype(np.float32)
    identb = np.eye(128, dtype=np.float32)
    onesf = np.ones((128, 128), dtype=np.float32)
    import ml_dtypes
    w1T = np.ascontiguousarray(inp["L1_ffn_w1"].T)
    w2T = np.ascontiguousarray(inp["L1_ffn_w2"].T)
    w3T = np.ascontiguousarray(inp["L1_ffn_w3"].T)
    w2Tp = np.zeros((6144, 2048), dtype=np.float32)
    w2Tp[:5632] = w2T
    wsall = np.stack([np.array([np.abs(w1T[c*256:(c+1)*256]).sum(),
                                np.abs(w2Tp[c*768:(c+1)*768]).sum(),
                                np.abs(w3T[c*256:(c+1)*256]).sum(), 0.0],
                               dtype=np.float32) for c in range(8)])
    maps = []
    for c in range(8):
        rows = np.concatenate([np.arange(t * 128, (t + 1) * 128) for t in FOLD[c]])
        m01 = np.zeros((8, 8, 128, 128), dtype=np.float32)
        for j in range(8):
            qt = 8 * j + c
            for i in range(8):
                kt = 8 * j + i
                kg = kt * 128 + np.arange(128)[:, None]
                qg = qt * 128 + np.arange(128)[None, :]
                m01[j, i] = (kg <= qg).astype(np.float32)
        m01R = np.ascontiguousarray(m01.transpose(0, 2, 1, 3).reshape(8, 128, 1024))
        maps.append(dict(
            x1f=np.ascontiguousarray(x1[rows]),
            knR=knR, vnR=vnR, wqR=wqR, woT=woT, lqk=lqk,
            m01R=m01R.astype(ml_dtypes.bfloat16),
            identb=identb, onesf=onesf,
            t1R=t1R, t3R=t3R, t2R=t2R, wsall=wsall,
        ))
    return maps


def assemble_out(resE):
    full = np.zeros((8192, 2048), dtype=np.float32)
    for c in range(8):
        o = resE[c]["out_d"]
        for ji, t in enumerate(FOLD[c]):
            full[t * 128:(t + 1) * 128] = o[ji * 128:(ji + 1) * 128]
    return full.reshape(1, 8192, 2048)


from concourse.bass_utils import run_bass_kernel_spmd

_CACHE = {}


def _get(name, fn):
    if name not in _CACHE:
        _CACHE[name] = fn()
    return _CACHE[name]


def kernel(**inputs):
    inp = {k_: np.asarray(v) for k_, v in inputs.items()}
    ncA = _get("A", build_launch_A)
    resA = run_bass_kernel_spmd(ncA, prep_A(inp), core_ids=list(range(8))).results
    ncB = _get("BCD", build_launch_BCD)
    resB = run_bass_kernel_spmd(ncB, prep_BCD(inp, resA), core_ids=list(range(8))).results
    ncE = _get("EF", build_launch_EF)
    resE = run_bass_kernel_spmd(ncE, prep_EF(inp, resB), core_ids=list(range(8))).results
    return assemble_out(resE)


# version 05
# speedup vs baseline: 1.3418x; 1.0022x over previous
import contextlib
from collections import defaultdict

import numpy as np
import concourse.bass as bass
import concourse.mybir as mybir

F32 = mybir.dt.float32
BF16 = mybir.dt.bfloat16
I32 = mybir.dt.int32
AF = mybir.ActivationFunctionType
ALU = mybir.AluOpType
AX = mybir.AxisListType

ENGS = ("sync", "gpsimd", "tensor", "vector", "scalar")


class K:
    AUTO = {"vector": "VE", "scalar": "AE", "gpsimd": "GP"}

    def __init__(self, name="prog"):
        self.nc = bass.Bass("TRN2", target_bir_lowering=False)
        self.plan = {e: [] for e in ENGS}
        self.cnt = defaultdict(int)
        self.sem = {}
        self.stack = contextlib.ExitStack()
        self.phase_stacks = []
        self.dram = {}
        self.final_waits = []
        self.dma_watch = {}
        self.last_small = {}
        self.mksem("PE", "VE", "AE", "GP")
        self.nc.detect_race_conditions = False

    def din(self, name, shape, dtype=F32):
        ap = self.nc.dram_tensor(name, list(shape), dtype, kind="ExternalInput").ap()
        self.dram[name] = ap
        return ap

    def dout(self, name, shape, dtype=F32):
        ap = self.nc.dram_tensor(name, list(shape), dtype, kind="ExternalOutput").ap()
        self.dram[name] = ap
        return ap

    def dtmp(self, name, shape, dtype=F32):
        ap = self.nc.dram_tensor(name, list(shape), dtype, kind="Internal").ap()
        self.dram[name] = ap
        return ap

    def sbuf(self, name, shape, dtype=F32):
        st = self.phase_stacks[-1] if self.phase_stacks else self.stack
        return st.enter_context(self.nc.sbuf_tensor(name, list(shape), dtype))

    def psum(self, name, shape, dtype=F32):
        st = self.phase_stacks[-1] if self.phase_stacks else self.stack
        return st.enter_context(self.nc.psum_tensor(name, list(shape), dtype))

    def push_phase(self):
        self.phase_stacks.append(contextlib.ExitStack())

    def pop_phase(self):
        self.phase_stacks.pop().close()
        self.fence()

    def fence(self):
        eng_w = [("PE", self.cnt["PE"]), ("VE", self.cnt["VE"]),
                 ("AE", self.cnt["AE"]), ("GP", self.cnt["GP"])]
        dma_w = list(self.dma_watch.items())
        for e in ENGS:
            own = self.AUTO.get(e)
            w = [(s, v) for (s, v) in eng_w if s != own] + dma_w
            self.waiton(e, *w)

    def mksem(self, *names):
        for n in names:
            if n not in self.sem:
                self.sem[n] = self.stack.enter_context(self.nc.semaphore(n))

    _DTSZ = {F32: 4, BF16: 2, I32: 4}

    def _outbytes(self, args, kwargs):
        ap = kwargs.get("out")
        if ap is None and args:
            ap = args[0]
        try:
            return ap.free_size() * self._DTSZ.get(ap.dtype, 4)
        except Exception:
            return 0

    def op(self, eng, method, *args, wait=None, inc=None, incn=1, **kwargs):
        is_dma = method.startswith("dma") or method.startswith("indirect")
        auto = self.AUTO.get(eng) if not is_dma else None
        waits = list(wait or [])
        if auto is not None:
            if self.last_small.get(eng, True):
                waits.append((auto, self.cnt[auto]))
            self.last_small[eng] = self._outbytes(args, kwargs) < 2048
            if inc is None:
                inc = auto
            if inc == auto:
                incn = 1
        tgt = None
        if inc is not None:
            self.cnt[inc] += incn
            tgt = self.cnt[inc]
            if inc not in ("PE", "VE", "AE", "GP"):
                self.dma_watch[inc] = tgt
        waits = [(s, v) for (s, v) in waits if v > 0]
        semd, n_ = self.sem, incn

        def fn(e, method=method, args=args, kwargs=kwargs, waits=waits, inc=inc):
            for (s, v) in waits:
                e.wait_ge(semd[s], v)
            instr = getattr(e, method)(*args, **kwargs)
            if inc is not None:
                instr.then_inc(semd[inc], n_)
            return instr

        self.plan[eng].append(fn)
        return tgt

    def waiton(self, eng, *sv):
        sv = [(s, v) for (s, v) in sv if v > 0]
        if not sv:
            return

        def fn(e, sv=sv):
            for (s, v) in sv:
                e.wait_ge(self.sem[s], v)
        self.plan[eng].append(fn)

    def store(self, dst, src, eng="sync", sem="STG", wait=None):
        self.mksem(sem)
        t = self.op(eng, "dma_start", out=dst, in_=src, wait=wait, inc=sem, incn=16)
        self._note_final(sem, t)
        return t

    def _note_final(self, sem, tgt):
        self.final_waits = [(s, v) for (s, v) in self.final_waits if s != sem]
        self.final_waits.append((sem, tgt))

    def finish(self, eng="sync"):
        self.waiton(eng, *self.final_waits)

    def now(self, sem):
        return self.cnt[sem]

    def build(self):
        nc = self.nc
        with nc.Block() as block:
            for ename in ENGS:
                def driver(e, ename=ename):
                    for fn in self.plan[ename]:
                        fn(e)
                getattr(block, ename)(driver)
        self.stack.close()
        return nc


class LBuf:

    def __init__(self, k: K, name, n, shape, dtype):
        self.k, self.name, self.n = k, name, n
        self.bufs = [k.sbuf(f"{name}{i}", shape, dtype) for i in range(n)]
        self.sems = [f"{name}S{i}" for i in range(n)]
        k.mksem(*self.sems)
        self.uses = [0] * n
        self.released = [None] * n
        self.i = 0

    def load(self, eng, in_ap, view=None, extra_wait=(), out_view=None):
        b = self.i % self.n
        self.i += 1
        self.uses[b] += 1
        w = list(extra_wait)
        if self.released[b] is not None:
            w.append(self.released[b])
        out = self.bufs[b][:, :] if out_view is None else out_view(self.bufs[b])
        t = self.k.op(eng, "dma_start", out=out, in_=in_ap, wait=w,
                      inc=self.sems[b], incn=16)
        return self.bufs[b], (self.sems[b], t), b

    def release(self, b, sem, tgt):
        self.released[b] = (sem, tgt)


class SOut:

    def __init__(self, k: K, name, n, shape, dtype):
        self.k, self.name, self.n = k, name, n
        self.bufs = [k.sbuf(f"{name}{i}", shape, dtype) for i in range(n)]
        self.sems = [f"{name}S{i}" for i in range(n)]
        k.mksem(*self.sems)
        self.stored = [0] * n
        self.i = 0

    def acquire(self):
        b = self.i % self.n
        self.i += 1
        w = [(self.sems[b], self.stored[b] * 16)] if self.stored[b] else []
        return self.bufs[b], w, b

    def flush(self, b, dst, wait, eng="sync", src_view=None):
        src = self.bufs[b][:, :] if src_view is None else src_view(self.bufs[b])
        self.stored[b] += 1
        t = self.k.op(eng, "dma_start", out=dst, in_=src, wait=wait,
                      inc=self.sems[b], incn=16)
        self.k._note_final(self.sems[b], t)
        return t


class TPose:

    def __init__(self, k: K, name, ident, alt=False):
        self.k = k
        self.ps = [k.psum(name + "a", [128, 128], BF16),
                   k.psum(name + "b", [128, 128], BF16)]
        self.ident = ident
        self.i = 0
        self.alt = alt
        self.last_copy = [("VE", 0), ("VE", 0)]

    def emit(self, src_view, dst_view, extra_wait=(), copy_extra_wait=(), rows=128):
        k, b = self.k, self.i % 2
        use_sc = self.alt and (self.i % 2 == 1)
        self.i += 1
        w = [self.last_copy[b]] + list(extra_wait)
        pe = k.op("tensor", "transpose", self.ps[b][0:rows, :], src_view,
                  self.ident[:, :], wait=w, inc="PE")
        if use_sc:
            ve = k.op("scalar", "copy", wait=[("PE", pe)] + list(copy_extra_wait),
                      out=dst_view, in_=self.ps[b][0:rows, :])
            self.last_copy[b] = ("AE", ve)
        else:
            ve = k.op("vector", "tensor_copy", wait=[("PE", pe)] + list(copy_extra_wait),
                      out=dst_view, in_=self.ps[b][0:rows, :])
            self.last_copy[b] = ("VE", ve)
        return pe, ve


def ts(t, n=128):
    return slice(t * n, (t + 1) * n)


RSQRT_H = float(1.0 / np.sqrt(np.float32(2048.0)))



def build_launch_A():
    k = K("A")
    xA = k.din("xA", [1024, 2048], F32)
    wqkvR = k.din("wqkvR", [48, 128, 2048], F32)
    bqkv = k.din("bqkv", [48, 128], F32)
    identb = k.din("identb", [128, 128], F32)
    onesf = k.din("onesf", [128, 128], F32)
    w1s = k.din("w1s", [256, 5632], F32)
    w2s = k.din("w2s", [768, 2048], F32)
    w3s = k.din("w3s", [256, 5632], F32)
    xnT_d = k.dout("xnT_d", [2048, 1024], BF16)
    qT_d = k.dout("qT_d", [2048, 1024], BF16)
    kT_d = k.dout("kT_d", [2048, 1024], BF16)
    v_d = k.dout("v_d", [1024, 2048], BF16)
    wsums_d = k.dout("wsums_d", [1, 4], F32)

    xc = k.sbuf("xc", [128, 2048], F32)
    sq = k.sbuf("sq", [128, 2048], F32)
    st1 = k.sbuf("st1", [128, 8], F32)
    xnT = k.sbuf("xnT", [128, 16 * 1024], BF16)
    vTs = k.sbuf("vTs", [128, 16 * 1024], BF16)
    ident = k.sbuf("ident", [128, 128], BF16)
    onessb = k.sbuf("onessb", [128, 128], F32)
    bq = k.sbuf("bq", [128, 48], F32)
    acc = k.sbuf("acc", [128, 4], F32)
    red = k.sbuf("red", [128, 1], F32)
    wsum_st = k.sbuf("wsum_st", [1, 4], F32)
    vstp = [k.sbuf(f"vst{i}", [128, 2048], BF16) for i in range(2)]
    vst = vstp[0]

    pqkv = [k.psum(f"pqkv{i}", [128, 512], F32) for i in range(2)]
    pws = k.psum("pws", [128, 4], F32)

    k.mksem("CN", "CNG")
    tp = TPose(k, "ptp", ident)
    xsb = LBuf(k, "xsb", 2, [128, 2048], F32)
    wslb = LBuf(k, "wslb", 2, [128, 16 * 128], BF16)
    wshb = LBuf(k, "wshb", 2, [128, 5632], F32)
    qkst = SOut(k, "qkst", 3, [128, 512], BF16)

    k.op("sync", "dma_start", out=bq[:, :], inc="CN", incn=16,
         in_=bqkv.rearrange("b p -> p b"), allow_slow_non_contiguous=True)
    k.op("sync", "dma_start", out=onessb[:, :], in_=onesf, inc="CN", incn=16)
    k.op("gpsimd", "dma_start", out=ident[:, :], in_=identb, inc="CNG", incn=16)
    const_cn = k.now("CN")
    const_cng = k.now("CNG")

    tp_par = [0, 0]
    for t in range(8):
        xbuf, ldw, xslot = xsb.load("sync", xA[ts(t), :])
        k.op("vector", "reduce_sum", wait=[ldw],
             out=st1[:, 0:1], in_=xbuf[:, :], axis=AX.X)
        k.op("vector", "tensor_scalar", out=st1[:, 1:2], in0=st1[:, 0:1],
             scalar1=1.0 / 2048.0, scalar2=None, op0=ALU.mult)
        k.op("vector", "tensor_scalar", out=xc[:, :], in0=xbuf[:, :],
             scalar1=st1[:, 1:2], scalar2=None, op0=ALU.subtract)
        ve_xcdone = k.op("vector", "tensor_tensor", out=sq[:, :], in0=xc[:, :],
                         in1=xc[:, :], op=ALU.mult)
        xsb.release(xslot, "VE", ve_xcdone)
        k.op("vector", "reduce_sum", out=st1[:, 2:3], in_=sq[:, :], axis=AX.X)
        v_ve = k.op("vector", "tensor_scalar", out=st1[:, 3:4], in0=st1[:, 2:3],
                    scalar1=1.0 / 2048.0, scalar2=1e-5, op0=ALU.mult, op1=ALU.add)
        k.op("scalar", "activation", wait=[("VE", v_ve)], out=st1[:, 4:5],
             in_=st1[:, 3:4], func=AF.Ln)
        a2 = k.op("scalar", "activation", out=st1[:, 5:6], in_=st1[:, 4:5],
                  func=AF.Exp, scale=-0.5)
        vb = vstp[t % 2]
        ve2 = k.op("vector", "tensor_scalar",
                   wait=[("AE", a2), ("PE", tp_par[t % 2])],
                   out=vb[:, :], in0=xc[:, :], scalar1=st1[:, 5:6], scalar2=None,
                   op0=ALU.mult)
        for c in range(16):
            pe, _ = tp.emit(vb[:, ts(c)],
                            xnT[:, c * 1024 + t * 128: c * 1024 + (t + 1) * 128],
                            extra_wait=[("VE", ve2), ("CNG", const_cng)])
            tp_par[t % 2] = pe
    ve_xnT = k.now("VE")
    for c in range(16):
        k.store(xnT_d[ts(c), :], xnT[:, c * 1024:(c + 1) * 1024],
                sem="STX", wait=[("VE", ve_xnT)])

    pe_group = {}
    wld = {}
    _, wld[0], _ = wslb.load("gpsimd", wqkvR[0, :, :])
    for ob in range(48):
        if ob + 1 < 48:
            _, wld[ob + 1], _ = wslb.load(
                "gpsimd", wqkvR[ob + 1, :, :],
                extra_wait=[("PE", pe_group.get(ob - 1, 0))])
        w = wslb.bufs[ob % 2]
        for nh in range(2):
            pe_last = None
            for c in range(16):
                wv = []
                if c == 0:
                    wv = [wld[ob], ("VE", k.now("VE"))]
                pe_last = k.op(
                    "tensor", "matmul", pqkv[nh][:, :], w[:, ts(c)],
                    xnT[:, c * 1024 + nh * 512: c * 1024 + nh * 512 + 512],
                    start=(c == 0), stop=(c == 15),
                    wait=wv, inc=("PE" if c == 15 else None))
            pe_group[ob] = pe_last
            if ob < 32:
                buf, bw, bi = qkst.acquire()
                if ob < 16:
                    ve = k.op("vector", "tensor_scalar",
                              wait=[("PE", pe_last), ("CN", const_cn)] + bw,
                              out=buf[:, :], in0=pqkv[nh][:, :], scalar1=bq[:, ob:ob+1],
                              scalar2=RSQRT_H, op0=ALU.add, op1=ALU.mult)
                    dst = qT_d[ts(ob), nh * 512:(nh + 1) * 512]
                else:
                    ve = k.op("vector", "tensor_scalar",
                              wait=[("PE", pe_last), ("CN", const_cn)] + bw,
                              out=buf[:, :], in0=pqkv[nh][:, :], scalar1=bq[:, ob:ob+1],
                              scalar2=None, op0=ALU.add)
                    dst = kT_d[ts(ob - 16), nh * 512:(nh + 1) * 512]
                qkst.flush(bi, dst, wait=[("VE", ve)])
            else:
                vb = ob - 32
                k.op("vector", "tensor_scalar",
                     wait=[("PE", pe_last), ("CN", const_cn)],
                     out=vTs[:, vb * 1024 + nh * 512: vb * 1024 + nh * 512 + 512],
                     in0=pqkv[nh][:, :], scalar1=bq[:, ob:ob+1],
                     scalar2=None, op0=ALU.add)

    ve_v = k.now("VE")
    k.mksem("STV")
    stv = 0
    for t in range(8):
        for c in range(16):
            tp.emit(vTs[:, c * 1024 + t * 128: c * 1024 + (t + 1) * 128],
                    vst[:, ts(c)],
                    extra_wait=[("VE", ve_v)],
                    copy_extra_wait=[("STV", stv * 16)] if (c == 0 and stv) else [])
        stv += 1
        t_st = k.op("sync", "dma_start", out=v_d[ts(t), :], in_=vst[:, :],
                    wait=[("VE", k.now("VE"))], inc="STV", incn=16)
    k._note_final("STV", t_st)

    k.op("vector", "memset", acc[:, :], 0.0)
    for (wap, nch, cols, j) in [(w1s, 2, 5632, 0), (w2s, 6, 2048, 1), (w3s, 2, 5632, 2)]:
        for ch in range(nch):
            wbuf, ldw, wi = wshb.load("sync", wap[ts(ch), 0:cols],
                                      out_view=lambda b, cols=cols: b[:, 0:cols])
            k.op("vector", "reduce_sum", wait=[ldw],
                 out=red[:, :], in_=wbuf[:, 0:cols], axis=AX.X,
                 apply_absolute_value=True)
            ve = k.op("vector", "tensor_tensor", out=acc[:, j:j+1], in0=acc[:, j:j+1],
                      in1=red[:, :], op=ALU.add)
            wshb.release(wi, "VE", ve)
    pe = k.op("tensor", "matmul", pws[:, :], onessb[:, :], acc[:, :],
              wait=[("VE", k.now("VE")), ("CN", const_cn)],
              start=True, stop=True, inc="PE")
    ve = k.op("vector", "tensor_copy", wait=[("PE", pe)], out=wsum_st[:, :],
              in_=pws[0:1, :])
    k.store(wsums_d, wsum_st[:, :], sem="STW", wait=[("VE", ve)])

    k.finish()
    return k.build()


def prep_A(inp):
    x = np.ascontiguousarray(inp["L0_attn_hidden_states"].reshape(8192, 2048))
    wqkvT = np.ascontiguousarray(inp["L0_attn_w_qkv"].T)
    wqkvR = np.ascontiguousarray(
        wqkvT.reshape(16, 128, 48, 128).transpose(2, 1, 0, 3).reshape(48, 128, 2048))
    bqkv = np.ascontiguousarray(inp["L0_attn_b_qkv"].reshape(48, 128))
    identb = np.eye(128, dtype=np.float32)
    onesf = np.ones((128, 128), dtype=np.float32)
    w1T = np.ascontiguousarray(inp["L1_ffn_w1"].T)
    w2T = np.ascontiguousarray(inp["L1_ffn_w2"].T)
    w3T = np.ascontiguousarray(inp["L1_ffn_w3"].T)
    w2Tp = np.zeros((6144, 2048), dtype=np.float32)
    w2Tp[:5632] = w2T
    maps = []
    for c in range(8):
        maps.append(dict(
            xA=np.ascontiguousarray(x[c * 1024:(c + 1) * 1024]),
            wqkvR=wqkvR, bqkv=bqkv, identb=identb, onesf=onesf,
            w1s=np.ascontiguousarray(w1T[c * 256:(c + 1) * 256]),
            w2s=np.ascontiguousarray(w2Tp[c * 768:(c + 1) * 768]),
            w3s=np.ascontiguousarray(w3T[c * 256:(c + 1) * 256]),
        ))
    return maps




RMS_EPS = 1.1920929e-07
NW2 = float(1.0 / (5632.0 * 2048.0))


def build_launch_BCD(debug=False):
    k = K("BCD")
    xA = k.din("xA", [1024, 2048], F32)
    qR = k.din("qR", [2, 128, 16 * 512], BF16)
    kR = k.din("kR", [64, 128, 16 * 128], BF16)
    vR = k.din("vR", [16, 128, 64 * 128], BF16)
    xnR = k.din("xnR", [128, 16 * 1024], BF16)
    woutT = k.din("woutT", [2048, 2048], F32)
    bout = k.din("bout", [1, 2048], F32)
    identb = k.din("identb", [128, 128], F32)
    onesf = k.din("onesf", [128, 128], F32)
    desc = k.din("desc", [1, 8], F32)
    WgT = k.din("WgT", [2048, 8], F32)
    A1fT = k.din("A1fT", [2048, 256], F32)
    A3fT = k.din("A3fT", [2048, 256], F32)
    B1fT = k.din("B1fT", [256, 7168], F32)
    B3fT = k.din("B3fT", [256, 7168], F32)
    A2fT = k.din("A2fT", [7168, 256], F32)
    a2R = k.din("a2R", [128, 56 * 256], F32)
    B2fT = k.din("B2fT", [256, 2048], F32)
    w13R = k.din("w13R", [56, 128, 16 * 256], F32)
    bba2 = k.din("bba2", [56, 128, 768], F32)
    w2R = k.din("w2R", [4, 14, 128, 4 * 512], F32)
    pat1 = k.din("pat1", [128, 256], F32)
    pat2 = k.din("pat2", [128, 256], F32)
    wkR = k.din("wkR", [16, 128, 16 * 128], F32)
    wvR = k.din("wvR", [16, 128, 16 * 128], F32)
    wsall = k.din("wsall", [8, 4], F32)
    w1s = k.din("w1s", [256, 5632], F32)
    w2s = k.din("w2s", [768, 2048], F32)
    w3s = k.din("w3s", [256, 5632], F32)
    x1_d = k.dout("x1_d", [1024, 2048], F32)
    kL1_d = k.dout("kL1_d", [2048, 1024], BF16)
    vL1_d = k.dout("vL1_d", [1024, 2048], BF16)
    t1_d = k.dout("t1_d", [256, 5632], BF16)
    t2_d = k.dout("t2_d", [768, 2048], BF16)
    t3_d = k.dout("t3_d", [256, 5632], BF16)
    dmk = k.dout if debug else k.dtmp
    xr_d = dmk("xr_d", [1024, 2048], F32)
    z_d = dmk("z_d", [56, 128, 1024], BF16)
    t_d = k.dtmp("t_d", [2, 56, 128, 1024], BF16)

    ident = k.sbuf("ident", [128, 128], BF16)
    onesb = k.sbuf("onesb", [128, 128], BF16)
    onesfs = k.sbuf("onesfs", [128, 128], F32)
    boutrep = k.sbuf("boutrep", [128, 2048], F32)
    k.mksem("CN", "CNG")
    k.op("gpsimd", "dma_start", out=ident[:, :], in_=identb, inc="CNG", incn=16)
    k.op("gpsimd", "dma_start", out=onesb[:, :], in_=onesf, inc="CNG", incn=16)
    k.op("sync", "dma_start", out=onesfs[:, :], in_=onesf, inc="CN", incn=16)
    k.op("sync", "dma_start", out=boutrep[:, :],
         in_=bout.broadcast_to((128, 2048)), inc="CN", incn=16)
    CN = k.now("CN")
    CNG = k.now("CNG")

    k.push_phase()
    qTB = k.sbuf("qTB", [128, 16 * 512], BF16)
    k.mksem("QTB")
    Pst = k.sbuf("Pst", [128, 64 * 512], BF16)
    rtmp = k.sbuf("rtmp", [128, 512], F32)
    rinv = k.sbuf("rinv", [128, 512], F32)
    OT = k.sbuf("OT", [128, 16 * 512], BF16)
    evt = k.sbuf("evt", [128, 512], F32)
    pool = [k.psum(f"apool{i}", [128, 512], F32) for i in range(6)]
    pread = [("VE", 0)] * 6
    ktl = LBuf(k, "ktl", 2, [128, 16 * 128], BF16)
    vsl = LBuf(k, "vsl", 2, [128, 64 * 128], BF16)
    wol = LBuf(k, "wol", 2, [128, 8 * 512], BF16)
    xld = LBuf(k, "xld", 2, [128, 2048], F32)
    xrst = SOut(k, "xrst", 3, [128, 512], F32)
    vw128 = lambda b: b[:, :].rearrange("p (c j) -> p c j", j=128)
    vw512 = lambda b: b[:, :].rearrange("p (c j) -> p c j", j=512)

    qtb_rel = ("VE", 0)
    for TB in range(2):
        qld = ("QTB", k.op("sync", "dma_start", wait=[qtb_rel],
                           out=qTB[:, :], in_=qR[TB, :, :],
                           inc="QTB", incn=16))
        pst_free_pe = k.now("PE")
        exp_t = {}

        def rowsum0(kcr):
            rw_ = [("AE", exp_t[kcr])] + ([pread[2]] if kcr == 0 else [])
            return k.op("tensor", "matmul", pool[2][:, :], onesb[:, :],
                        Pst[:, kcr * 512:(kcr + 1) * 512],
                        start=(kcr == 0), stop=(kcr == 63), wait=rw_,
                        inc=("PE" if kcr == 63 else None))

        for kc in range(64):
            kt, kld, ksl = ktl.load("sync", kR[kc, :, :])
            sl = kc % 2
            pe_last = None
            for c in range(16):
                wv = [kld, qld, pread[sl], ("CNG", CNG)] if c == 0 else []
                pe_last = k.op("tensor", "matmul", pool[sl][:, :],
                               kt[:, ts(c)], qTB[:, c * 512:(c + 1) * 512],
                               start=(c == 0), stop=(c == 15), wait=wv,
                               inc=("PE" if c == 15 else None))
            ktl.release(ksl, "PE", pe_last)
            ae = k.op("scalar", "activation", wait=[("PE", pe_last), ("PE", pst_free_pe)],
                      out=Pst[:, kc * 512:(kc + 1) * 512], in_=pool[sl][:, :],
                      func=AF.Exp)
            pread[sl] = ("AE", ae)
            exp_t[kc] = ae
            if kc >= 2:
                pe_r = rowsum0(kc - 2)
        pe_r = rowsum0(62)
        pe_r = rowsum0(63)
        exp_last = k.now("AE")
        aeL = k.op("scalar", "activation", wait=[("PE", pe_r)], out=rtmp[:, :],
                   in_=pool[2][:, :], func=AF.Ln)
        pread[2] = ("AE", aeL)
        aeR = k.op("scalar", "activation", out=rinv[:, :], in_=rtmp[:, :],
                   func=AF.Exp, scale=-1.0)
        for db in range(16):
            vt, vld, vslot = vsl.load("sync", vR[db, :, :])
            sl = 3 + db % 2
            pe_last = None
            for kc in range(64):
                wv = [vld, pread[sl], ("AE", exp_last)] if kc == 0 else []
                pe_last = k.op("tensor", "matmul", pool[sl][:, :],
                               vt[:, ts(kc)], Pst[:, kc * 512:(kc + 1) * 512],
                               start=(kc == 0), stop=(kc == 63), wait=wv,
                               inc=("PE" if kc == 63 else None))
            vsl.release(vslot, "PE", pe_last)
            ve = k.op("vector", "tensor_tensor",
                      wait=[("PE", pe_last), ("PE", pst_free_pe), ("AE", aeR)],
                      out=OT[:, db * 512:(db + 1) * 512], in0=pool[sl][:, :],
                      in1=rinv[:, :], op=ALU.mult)
            pread[sl] = ("VE", ve)
        ot_ve = k.now("VE")
        qtb_rel = ("PE", k.now("PE"))
        for ob in range(4):
            wolds = []
            for hf in range(2):
                wb_, wld_, wslot_ = wol.load(
                    "gpsimd",
                    woutT[hf * 1024:(hf + 1) * 1024, ob * 512:(ob + 1) * 512]
                    .rearrange("(c p) j -> p c j", p=128),
                    out_view=vw512)
                wolds.append((wld_, wb_, wslot_))
            for tt in range(4):
                g = TB * 4 + tt
                sl = tt % 2
                pe_last = None
                for dc in range(16):
                    wv = []
                    if dc == 0:
                        wv = [wolds[0][0], pread[sl], ("VE", ot_ve)]
                    if dc == 8:
                        wv = [wolds[1][0]]
                    wbuf = wolds[0][1] if dc < 8 else wolds[1][1]
                    pe_last = k.op("tensor", "matmul", pool[sl][:, :],
                                   OT[:, dc * 512 + tt * 128: dc * 512 + (tt + 1) * 128],
                                   wbuf[:, (dc % 8) * 512:(dc % 8 + 1) * 512],
                                   start=(dc == 0), stop=(dc == 15), wait=wv,
                                   inc=("PE" if dc == 15 else None))
                xt, xldw, xslot = xld.load("sync", xA[ts(g), :])
                ve1 = k.op("vector", "tensor_tensor", wait=[("PE", pe_last), ("CN", CN)],
                           out=evt[:, :], in0=pool[sl][:, :],
                           in1=boutrep[:, ob * 512:(ob + 1) * 512], op=ALU.add)
                pread[sl] = ("VE", ve1)
                sb_, bw, bi = xrst.acquire()
                ve2 = k.op("vector", "tensor_tensor", wait=[xldw] + bw,
                           out=sb_[:, :], in0=evt[:, :],
                           in1=xt[:, ob * 512:(ob + 1) * 512], op=ALU.add)
                xld.release(xslot, "VE", ve2)
                xrst.flush(bi, xr_d[ts(g), ob * 512:(ob + 1) * 512], wait=[("VE", ve2)])
            wol.release(wolds[0][2], "PE", k.now("PE"))
            wol.release(wolds[1][2], "PE", k.now("PE"))
    k.pop_phase()

    k.push_phase()
    xnT = k.sbuf("xnT", [128, 16 * 1024], BF16)
    k.mksem("XNT")
    XNL = ("XNT", k.op("sync", "dma_start",
                       out=xnT[:, :], in_=xnR, inc="XNT", incn=16))
    mr1 = k.sbuf("mr1", [128, 2 * 1024], BF16)
    mr2 = k.sbuf("mr2", [128, 2 * 1024], BF16)
    rw1rep = k.sbuf("rw1rep", [128, 1024], F32)
    rw2rep = k.sbuf("rw2rep", [128, 1024], F32)
    za11 = k.sbuf("za11", [128, 2 * 1024], BF16)
    za12 = k.sbuf("za12", [128, 2 * 1024], BF16)
    za31 = k.sbuf("za31", [128, 2 * 1024], BF16)
    za32 = k.sbuf("za32", [128, 2 * 1024], BF16)
    za2w1 = k.sbuf("za2w1", [128, 2 * 1024], BF16)
    za2w2 = k.sbuf("za2w2", [128, 2 * 1024], BF16)

    k.push_phase()
    WgTs = k.sbuf("WgTs", [128, 16 * 8], BF16)
    k.op("gpsimd", "dma_start", out=WgTs[:, :].rearrange("p (c j) -> p c j", j=8),
         in_=WgT.rearrange("(c p) j -> p c j", p=128), inc="CNG", incn=16)
    descr = k.sbuf("descr", [128, 8], F32)
    k.op("sync", "dma_start", out=descr[:, :], in_=desc.broadcast_to((128, 8)),
         inc="CN", incn=16)
    A1s = k.sbuf("A1s", [128, 16 * 256], BF16)
    A3s = k.sbuf("A3s", [128, 16 * 256], BF16)
    k.op("gpsimd", "dma_start", out=A1s[:, :].rearrange("p (c j) -> p c j", j=256),
         in_=A1fT.rearrange("(c p) j -> p c j", p=128), inc="CNG", incn=16)
    k.op("gpsimd", "dma_start", out=A3s[:, :].rearrange("p (c j) -> p c j", j=256),
         in_=A3fT.rearrange("(c p) j -> p c j", p=128), inc="CNG", incn=16)
    p1s = k.sbuf("p1s", [128, 256], BF16)
    p2s = k.sbuf("p2s", [128, 256], BF16)
    k.op("gpsimd", "dma_start", out=p1s[:, :], in_=pat1, inc="CNG", incn=16)
    k.op("gpsimd", "dma_start", out=p2s[:, :], in_=pat2, inc="CNG", incn=16)
    CN2, CNG2 = k.now("CN"), k.now("CNG")

    m1stg = k.sbuf("m1stg", [128, 1024], BF16)
    m2stg = k.sbuf("m2stg", [128, 1024], BF16)
    estg = k.sbuf("estg", [128, 1024], BF16)
    mt1stg = k.sbuf("mt1stg", [128, 1024], F32)
    mt2stg = k.sbuf("mt2stg", [128, 1024], F32)
    esb = k.sbuf("esb", [128, 8], F32)
    tq = k.sbuf("tq", [128, 48], F32)
    stk = k.sbuf("stk", [128, 24], BF16)
    num1 = k.sbuf("num1", [128, 1024], F32)
    num2 = k.sbuf("num2", [128, 1024], F32)
    rpool = [k.psum(f"rpool{i}", [128, 512], F32) for i in range(3)]
    plg = k.psum("plg", [128, 8], F32)
    tpr = TPose(k, "tpr", ident)
    k.op("vector", "memset", m1stg[:, :], 0.0)
    k.op("vector", "memset", m2stg[:, :], 0.0)
    k.op("vector", "memset", estg[:, :], 0.0)
    k.op("vector", "memset", mt1stg[:, :], 0.0)
    k.op("vector", "memset", mt2stg[:, :], 0.0)

    lg_read = ("VE", 0)
    for g in range(8):
        pe_last = None
        for c in range(16):
            wv = [XNL, ("CNG", CNG2), lg_read] if c == 0 else []
            pe_last = k.op("tensor", "matmul", plg[:, :],
                           xnT[:, c * 1024 + g * 128: c * 1024 + (g + 1) * 128],
                           WgTs[:, c * 8:(c + 1) * 8],
                           start=(c == 0), stop=(c == 15), wait=wv,
                           inc=("PE" if c == 15 else None))
        k.op("vector", "reduce_max", wait=[("PE", pe_last)], out=tq[:, 0:1],
             in_=plg[:, :], axis=AX.X, negate=True)
        v_nm = k.now("VE")
        ae = k.op("scalar", "activation", wait=[("VE", v_nm)],
                  out=esb[:, :], in_=plg[:, :], func=AF.Exp, bias=tq[:, 0:1])
        lg_read = ("AE", ae)
        k.op("vector", "reduce_max", wait=[("AE", ae), ("CN", CN2)],
             out=tq[:, 1:2], in_=esb[:, :], axis=AX.X)
        k.op("vector", "tensor_scalar", out=tq[:, 8:16], in0=esb[:, :],
             scalar1=tq[:, 1:2], scalar2=None, op0=ALU.is_ge)
        k.op("vector", "tensor_tensor", out=tq[:, 8:16], in0=tq[:, 8:16],
             in1=descr[:, :], op=ALU.mult)
        k.op("vector", "reduce_max", out=tq[:, 2:3], in_=tq[:, 8:16], axis=AX.X)
        k.op("vector", "tensor_scalar", out=tq[:, 16:24], in0=tq[:, 8:16],
             scalar1=tq[:, 2:3], scalar2=None, op0=ALU.is_ge)
        k.op("vector", "tensor_copy", out=stk[:, 0:8], in_=tq[:, 16:24])
        k.op("vector", "scalar_tensor_tensor", out=tq[:, 24:32], in0=tq[:, 16:24],
             scalar=1.0, in1=esb[:, :], op0=ALU.subtract, op1=ALU.mult)
        k.op("vector", "tensor_reduce", out=tq[:, 3:4], in_=tq[:, 24:32],
             axis=AX.X, op=ALU.min)
        k.op("vector", "tensor_scalar", out=tq[:, 32:40], in0=tq[:, 24:32],
             scalar1=tq[:, 3:4], scalar2=None, op0=ALU.is_le)
        k.op("vector", "tensor_tensor", out=tq[:, 32:40], in0=tq[:, 32:40],
             in1=descr[:, :], op=ALU.mult)
        k.op("vector", "reduce_max", out=tq[:, 4:5], in_=tq[:, 32:40], axis=AX.X)
        k.op("vector", "tensor_scalar", out=stk[:, 8:16], in0=tq[:, 32:40],
             scalar1=tq[:, 4:5], scalar2=None, op0=ALU.is_ge)
        ve_e = k.op("vector", "tensor_copy", out=stk[:, 16:24], in_=esb[:, :])
        tpr.emit(stk[:, 0:8], m1stg[0:8, ts(g)], extra_wait=[("VE", ve_e)], rows=8)
        tpr.emit(stk[:, 8:16], m2stg[0:8, ts(g)], rows=8)
        tpr.emit(stk[:, 16:24], estg[0:8, ts(g)], rows=8)
    ve1 = k.op("vector", "tensor_tensor", out=mt1stg[0:8, :], in0=m1stg[0:8, :],
               in1=estg[0:8, :], op=ALU.mult)
    ve2 = k.op("vector", "tensor_tensor", out=mt2stg[0:8, :], in0=m2stg[0:8, :],
               in1=estg[0:8, :], op=ALU.mult)
    for half in range(2):
        p1 = k.op("tensor", "matmul", rpool[0][:, :], onesfs[:, :],
                  mt1stg[:, half * 512:(half + 1) * 512], start=True, stop=True,
                  wait=[("VE", k.now("VE")), ("CN", CN)], inc="PE")
        p2 = k.op("tensor", "matmul", rpool[1][:, :], onesfs[:, :],
                  mt2stg[:, half * 512:(half + 1) * 512], start=True, stop=True,
                  inc="PE")
        k.op("vector", "tensor_copy", wait=[("PE", p1)],
             out=num1[:, half * 512:(half + 1) * 512], in_=rpool[0][:, :])
        k.op("vector", "tensor_copy", wait=[("PE", p2)],
             out=num2[:, half * 512:(half + 1) * 512], in_=rpool[1][:, :])
    ve3 = k.op("vector", "tensor_tensor", out=rw1rep[:, :], in0=num1[:, :],
               in1=num2[:, :], op=ALU.add)
    k.op("scalar", "activation", wait=[("VE", ve3)], out=rw2rep[:, :],
         in_=rw1rep[:, :], func=AF.Ln)
    aeEx = k.op("scalar", "activation", out=rw2rep[:, :], in_=rw2rep[:, :],
                func=AF.Exp, scale=-1.0)
    k.op("vector", "tensor_tensor", wait=[("AE", aeEx)], out=rw1rep[:, :],
         in0=num1[:, :], in1=rw2rep[:, :], op=ALU.mult)
    ve5 = k.op("vector", "tensor_tensor", out=rw2rep[:, :],
               in0=num2[:, :], in1=rw2rep[:, :], op=ALU.mult)
    for eb in range(2):
        for half in range(2):
            pm1 = k.op("tensor", "matmul", rpool[0][:, :], p1s[:, ts(eb)],
                       m1stg[:, half * 512:(half + 1) * 512], start=True, stop=True,
                       wait=[("VE", k.now("VE")), ("CNG", CNG2)], inc="PE")
            pm2 = k.op("tensor", "matmul", rpool[1][:, :], p2s[:, ts(eb)],
                       m2stg[:, half * 512:(half + 1) * 512], start=True, stop=True,
                       inc="PE")
            off = eb * 1024 + half * 512
            k.op("vector", "tensor_copy", wait=[("PE", pm1)],
                 out=mr1[:, off:off + 512], in_=rpool[0][:, :])
            k.op("vector", "tensor_copy", wait=[("PE", pm2)],
                 out=mr2[:, off:off + 512], in_=rpool[1][:, :])
    for (Asrc, z1, z2) in [(A1s, za11, za12), (A3s, za31, za32)]:
        for eb in range(2):
            for half in range(2):
                pe_last = None
                for c in range(16):
                    wv = [("VE", k.now("VE"))] if c == 0 else []
                    pe_last = k.op("tensor", "matmul", rpool[2][:, :],
                                   Asrc[:, c * 256 + eb * 128: c * 256 + (eb + 1) * 128],
                                   xnT[:, c * 1024 + half * 512: c * 1024 + (half + 1) * 512],
                                   start=(c == 0), stop=(c == 15), wait=wv,
                                   inc=("PE" if c == 15 else None))
                off = eb * 1024 + half * 512
                k.op("vector", "tensor_tensor", wait=[("PE", pe_last)],
                     out=z1[:, off:off + 512], in0=rpool[2][:, :],
                     in1=mr1[:, off:off + 512], op=ALU.mult)
                k.op("vector", "tensor_tensor",
                     out=z2[:, off:off + 512], in0=rpool[2][:, :],
                     in1=mr2[:, off:off + 512], op=ALU.mult)
    k.pop_phase()

    k.push_phase()
    fpool = [k.psum(f"fpool{i}", [128, 512], F32) for i in range(8)]
    fread = [("VE", 0)] * 8
    w13l = LBuf(k, "w13l", 3, [128, 16 * 256], BF16)
    bbal = LBuf(k, "bbal", 3, [128, 768], BF16)
    a2full = k.sbuf("a2full", [128, 56 * 256], BF16)
    tres = LBuf(k, "tres", 3, [128, 512], BF16)
    k.mksem("A2F")
    bbq = [( "A2F", k.op("gpsimd", "dma_start", out=a2full[:, :],
                         in_=a2R, inc="A2F", incn=16))]
    base1sb = [k.sbuf(f"base1sb{i}", [128, 512], F32) for i in range(2)]
    base3sb = [k.sbuf(f"base3sb{i}", [128, 512], F32) for i in range(2)]
    x1b = [k.sbuf(f"x1b{i}", [128, 512], F32) for i in range(2)]
    s1b = [k.sbuf(f"s1b{i}", [128, 512], BF16) for i in range(2)]
    x3b = [k.sbuf(f"x3b{i}", [128, 512], BF16) for i in range(2)]
    ub = k.sbuf("ub", [128, 512], BF16)
    tspill = [SOut(k, f"tsp{i}", 2, [128, 512], BF16) for i in range(2)]
    tlast = [None, None]
    zst = SOut(k, "zst", 2, [128, 512], BF16)
    vwc = lambda b: b[:, :].rearrange("p (c j) -> p c j", j=128)

    def wslab(src, fb):
        return src[:, fb * 128:(fb + 1) * 128].rearrange("(c p) j -> p c j", p=128)

    def bslab(src, fb):
        return src[:, fb * 128:(fb + 1) * 128].rearrange("(e p) j -> p e j", p=128)

    prev_silu = 0
    pend = {}
    for TB in range(2):
        cols = slice(TB * 512, (TB + 1) * 512)
        pe_a2_tgts = []

        def chains(fbp):
            nonlocal prev_silu
            bbuf_, bbi, bpar = pend.pop(fbp)
            b1b = b3b = a2b = bbuf_
            zas = [(za11, za31, rw1rep), (za12, za32, rw2rep)]
            l1s, l3s = (2, 4), (3, 5)
            peL1, peL3 = {}, {}
            for kk in range(2):
                za1k = zas[kk][0]
                pe_ = None
                for eb in range(2):
                    wv = [fread[l1s[kk]]] if eb == 0 else []
                    pe_ = k.op("tensor", "matmul", fpool[l1s[kk]][:, :],
                               b1b[:, eb * 128:(eb + 1) * 128],
                               za1k[:, eb * 1024 + TB * 512: eb * 1024 + TB * 512 + 512],
                               start=(eb == 0), stop=(eb == 1), wait=wv,
                               inc=("PE" if eb == 1 else None))
                peL1[kk] = pe_
            for kk in range(2):
                za3k = zas[kk][1]
                pe_ = None
                for eb in range(2):
                    wv = [fread[l3s[kk]]] if eb == 0 else []
                    pe_ = k.op("tensor", "matmul", fpool[l3s[kk]][:, :],
                                b3b[:, 256 + eb * 128: 256 + (eb + 1) * 128],
                                za3k[:, eb * 1024 + TB * 512: eb * 1024 + TB * 512 + 512],
                                start=(eb == 0), stop=(eb == 1), wait=wv,
                                inc=("PE" if eb == 1 else None))
                peL3[kk] = pe_
            sil = {}
            for kk in range(2):
                ve_x1 = k.op("vector", "scalar_tensor_tensor",
                             wait=[("PE", peL1[kk]), ("AE", prev_silu)],
                             out=x1b[kk][:, :], in0=fpool[l1s[kk]][:, :],
                             scalar=0.0, in1=base1sb[bpar][:, :],
                             op0=ALU.bypass, op1=ALU.add)
                fread[l1s[kk]] = ("VE", ve_x1)
                ae_s = k.op("scalar", "activation", wait=[("VE", ve_x1)],
                            out=s1b[kk][:, :], in_=x1b[kk][:, :], func=AF.Silu)
                sil[kk] = ae_s
                prev_silu = ae_s
            vx3 = {}
            for kk in range(2):
                ve_x3 = k.op("vector", "scalar_tensor_tensor",
                             wait=[("PE", peL3[kk])], out=x3b[kk][:, :],
                             in0=fpool[l3s[kk]][:, :],
                             scalar=0.0, in1=base3sb[bpar][:, :],
                             op0=ALU.bypass, op1=ALU.add)
                fread[l3s[kk]] = ("VE", ve_x3)
            for kk in range(2):
                rwrep = zas[kk][2]
                ve_u = k.op("vector", "tensor_tensor", wait=[("AE", sil[kk])],
                            out=ub[:, :], in0=s1b[kk][:, :], in1=x3b[kk][:, :],
                            op=ALU.mult)
                tsb, tsw, tsi = tspill[kk].acquire()
                ve_t = k.op("vector", "tensor_tensor", wait=tsw,
                            out=tsb[:, :], in0=ub[:, :], in1=rwrep[:, cols],
                            op=ALU.mult)
                tspill[kk].flush(tsi, t_d[kk, fbp, :, TB * 512:(TB + 1) * 512],
                                 wait=[("VE", ve_t)])
                tlast[kk] = tsb
            pe_now = k.now("PE")
            bbal.release(bbi, "PE", pe_now)
            zb, zw, zi = zst.acquire()
            ve_z = k.op("vector", "tensor_tensor", wait=zw, out=zb[:, :],
                        in0=tlast[0][:, :], in1=tlast[1][:, :], op=ALU.add)
            zst.flush(zi, z_d[fbp, :, cols], wait=[("VE", ve_z)])

        for fb in range(56):
            wbuf_, w13ld, w13i = w13l.load("gpsimd", w13R[fb, :, :])
            bbuf_, bbld, bbi = bbal.load("gpsimd", bba2[fb, :, :])
            bpar = fb % 2
            pe1 = None
            for c in range(16):
                wv = [w13ld, fread[0], XNL] if c == 0 else []
                pe1 = k.op("tensor", "matmul", fpool[0][:, :],
                           wbuf_[:, c * 256: c * 256 + 128],
                           xnT[:, c * 1024 + TB * 512: c * 1024 + TB * 512 + 512],
                           start=(c == 0), stop=(c == 15), wait=wv,
                           inc=("PE" if c == 15 else None))
            pe3 = None
            for c in range(16):
                wv = [fread[1], bbld] if c == 0 else []
                pe3 = k.op("tensor", "matmul", fpool[1][:, :],
                           wbuf_[:, c * 256 + 128: c * 256 + 256],
                           xnT[:, c * 1024 + TB * 512: c * 1024 + TB * 512 + 512],
                           start=(c == 0), stop=(c == 15), wait=wv,
                           inc=("PE" if c == 15 else None))
            w13l.release(w13i, "PE", pe3)
            pend[fb] = (bbuf_, bbi, bpar)
            if fb > 0:
                chains(fb - 1)
            ve_b1 = k.op("vector", "tensor_copy", wait=[("PE", pe1)],
                         out=base1sb[bpar][:, :], in_=fpool[0][:, :])
            fread[0] = ("VE", ve_b1)
            ve_b3 = k.op("vector", "tensor_copy", wait=[("PE", pe3)],
                         out=base3sb[bpar][:, :], in_=fpool[1][:, :])
            fread[1] = ("VE", ve_b3)
        chains(55)
        k.waiton("sync", *[(tspill[i].sems[b], tspill[i].stored[b] * 16)
                           for i in range(2) for b in range(2)])
        for kk in range(2):
            for eb in range(2):
                slot = (0 if kk == 0 else 6) + eb
                for fb in range(56):
                    tb_, tld, ti = tres.load("sync", t_d[kk, fb, :, cols])
                    wv = [tld]
                    if fb == 0:
                        wv.append(fread[slot])
                        wv.append(bbq[0])
                    t_ = k.op("tensor", "matmul", fpool[slot][:, :],
                              a2full[:, fb * 256 + eb * 128: fb * 256 + (eb + 1) * 128],
                              tb_[:, :], start=(fb == 0), stop=(fb == 55), wait=wv,
                              inc="PE")
                    tres.release(ti, "PE", t_)
        pe_fin = k.now("PE")
        for kk, (za2wk, mrk, sA, sB) in enumerate(
                [(za2w1, mr1, 0, 1), (za2w2, mr2, 6, 7)]):
            for eb in range(2):
                slot = sA if eb == 0 else sB
                off = eb * 1024 + TB * 512
                ve = k.op("vector", "tensor_tensor", wait=[("PE", pe_fin)],
                          out=za2wk[:, off:off + 512], in0=fpool[slot][:, :],
                          in1=mrk[:, off:off + 512], op=ALU.mult)
                fread[slot] = ("VE", ve)
    k.pop_phase()

    k.push_phase()
    dpool = [k.psum(f"dpool{i}", [128, 512], F32) for i in range(8)]
    dread = [("VE", 0)] * 8
    zl = LBuf(k, "zl", 2, [128, 4 * 1024], BF16)
    w2l = LBuf(k, "w2l", 2, [128, 4 * 512], BF16)
    xrl = LBuf(k, "xrl", 2, [128, 512], F32)
    B2s = k.sbuf("B2s", [128, 2 * 2048], BF16)
    k.mksem("B2L")
    b2ld = ("B2L", k.op("gpsimd", "dma_start",
                        out=B2s[:, :].rearrange("p (e j) -> p e j", j=2048),
                        in_=B2fT.rearrange("(e p) j -> p e j", p=128),
                        inc="B2L", incn=16))
    x1st = SOut(k, "x1st", 3, [128, 512], F32)
    for hb in range(4):
        hcols = slice(hb * 512, (hb + 1) * 512)
        mm_t = {}
        for g4 in range(14):
            zb_, zld, zi = zl.load(
                "sync", z_d[4 * g4:4 * g4 + 4, :, :].rearrange("f p t -> p f t"))
            wb_, wld, wi = w2l.load("gpsimd", w2R[hb, g4, :, :])
            for fi in range(4):
                fb = 4 * g4 + fi
                for g in range(8):
                    wv = []
                    if fi == 0 and g == 0:
                        wv = [zld, wld]
                    if fb == 0:
                        wv = wv + [dread[g]]
                    mm_t[g] = k.op("tensor", "matmul", dpool[g][:, :],
                                   zb_[:, fi * 1024 + g * 128: fi * 1024 + (g + 1) * 128],
                                   wb_[:, fi * 512:(fi + 1) * 512],
                                   start=(fb == 0), stop=False, wait=wv, inc="PE")
            zl.release(zi, "PE", mm_t[7])
            w2l.release(wi, "PE", mm_t[7])
        for g in range(8):
            for kk, za2wk in enumerate([za2w1, za2w2]):
                for eb in range(2):
                    wv = [b2ld] if (hb == 0 and g == 0 and kk == 0 and eb == 0) else []
                    mm_t[g] = k.op("tensor", "matmul", dpool[g][:, :],
                                   za2wk[:, eb * 1024 + g * 128: eb * 1024 + (g + 1) * 128],
                                   B2s[:, eb * 2048 + hb * 512: eb * 2048 + hb * 512 + 512],
                                   start=False, stop=(kk == 1 and eb == 1),
                                   wait=wv, inc="PE")
            xr_, xrw, xri = xrl.load("sync", xr_d[ts(g), hcols])
            sb_, bw, bi = x1st.acquire()
            ve = k.op("vector", "tensor_tensor",
                      wait=[("PE", mm_t[g]), xrw] + bw,
                      out=sb_[:, :], in0=dpool[g][:, :], in1=xr_[:, :], op=ALU.add)
            dread[g] = ("VE", ve)
            xrl.release(xri, "VE", ve)
            x1st.flush(bi, x1_d[ts(g), hcols], wait=[("VE", ve)])
    k.pop_phase()
    k.pop_phase()

    k.push_phase()
    xn1T = k.sbuf("xn1T", [128, 16 * 1024], BF16)
    xc = k.sbuf("xc", [128, 2048], F32)
    sq = k.sbuf("sq", [128, 2048], F32)
    st1 = k.sbuf("st1", [128, 8], F32)
    vstp = [k.sbuf(f"vst{i}", [128, 2048], BF16) for i in range(2)]
    vst = vstp[0]
    vTs = k.sbuf("vTs", [128, 16 * 1024], BF16)
    sqb = k.sbuf("sqb", [128, 512], F32)
    rstb = k.sbuf("rstb", [128, 512], F32)
    x1l = LBuf(k, "x1l", 2, [128, 2048], F32)
    wkl = LBuf(k, "wkl", 2, [128, 16 * 128], BF16)
    kst = SOut(k, "kst", 2, [128, 512], BF16)
    lpool = [k.psum(f"lpool{i}", [128, 512], F32) for i in range(3)]
    lread = [("VE", 0)] * 3
    tpl = TPose(k, "tpl", ident, alt=True)
    tp_par = [0, 0]
    for g in range(8):
        xt, xldw, xsl = x1l.load("sync", x1_d[ts(g), :])
        k.op("vector", "reduce_sum", wait=[xldw], out=st1[:, 0:1],
             in_=xt[:, :], axis=AX.X)
        k.op("vector", "tensor_scalar", out=st1[:, 1:2], in0=st1[:, 0:1],
             scalar1=1.0 / 2048.0, scalar2=None, op0=ALU.mult)
        k.op("vector", "tensor_scalar", out=xc[:, :], in0=xt[:, :],
             scalar1=st1[:, 1:2], scalar2=None, op0=ALU.subtract)
        ve_rel = k.op("vector", "tensor_tensor", out=sq[:, :], in0=xc[:, :],
                      in1=xc[:, :], op=ALU.mult)
        x1l.release(xsl, "VE", ve_rel)
        k.op("vector", "reduce_sum", out=st1[:, 2:3], in_=sq[:, :], axis=AX.X)
        v_ve = k.op("vector", "tensor_scalar", out=st1[:, 3:4], in0=st1[:, 2:3],
                    scalar1=1.0 / 2048.0, scalar2=1e-5, op0=ALU.mult, op1=ALU.add)
        k.op("scalar", "activation", wait=[("VE", v_ve)], out=st1[:, 4:5],
             in_=st1[:, 3:4], func=AF.Ln)
        a2 = k.op("scalar", "activation", out=st1[:, 5:6], in_=st1[:, 4:5],
                  func=AF.Exp, scale=-0.5)
        vb = vstp[g % 2]
        ve2 = k.op("vector", "tensor_scalar",
                   wait=[("AE", a2), ("PE", tp_par[g % 2])],
                   out=vb[:, :], in0=xc[:, :], scalar1=st1[:, 5:6], scalar2=None,
                   op0=ALU.mult)
        for c in range(16):
            pe, _ = tpl.emit(vb[:, ts(c)],
                             xn1T[:, c * 1024 + g * 128: c * 1024 + (g + 1) * 128],
                             extra_wait=[("VE", ve2)])
            tp_par[g % 2] = pe
    xn1_ve = k.now("VE")
    xn1_ae = k.now("AE")
    for which, wT in [("k", wkR), ("v", wvR)]:
        for ob in range(16):
            _, wld, wi = wkl.load("gpsimd", wT[ob, :, :])
            for nh in range(2):
                sl = nh
                pe_last = None
                for c in range(16):
                    wv = ([wld, lread[sl], ("VE", xn1_ve), ("AE", xn1_ae)]
                          if c == 0 else [])
                    pe_last = k.op("tensor", "matmul", lpool[sl][:, :],
                                   wkl.bufs[wi][:, ts(c)],
                                   xn1T[:, c * 1024 + nh * 512: c * 1024 + nh * 512 + 512],
                                   start=(c == 0), stop=(c == 15), wait=wv,
                                   inc=("PE" if c == 15 else None))
                if which == "k":
                    ae_sq = k.op("scalar", "activation", wait=[("PE", pe_last)],
                                 out=sqb[:, :], in_=lpool[sl][:, :], func=AF.Square)
                    pe_ms = k.op("tensor", "matmul", lpool[2][:, :], onesfs[:, :],
                                 sqb[:, :], start=True, stop=True,
                                 wait=[("AE", ae_sq), lread[2], ("CN", CN)], inc="PE")
                    v_ts = k.op("vector", "tensor_scalar", wait=[("PE", pe_ms)],
                                out=rstb[:, :], in0=lpool[2][:, :],
                                scalar1=1.0 / 128.0, scalar2=RMS_EPS,
                                op0=ALU.mult, op1=ALU.add)
                    lread[2] = ("VE", v_ts)
                    k.op("scalar", "activation", wait=[("VE", v_ts)], out=rstb[:, :],
                         in_=rstb[:, :], func=AF.Ln)
                    ae_r = k.op("scalar", "activation", out=rstb[:, :], in_=rstb[:, :],
                                func=AF.Exp, scale=-0.5)
                    sb_, bw, bi = kst.acquire()
                    ve = k.op("vector", "tensor_tensor", wait=[("AE", ae_r)] + bw,
                              out=sb_[:, :], in0=lpool[sl][:, :], in1=rstb[:, :],
                              op=ALU.mult)
                    lread[sl] = ("VE", ve)
                    kst.flush(bi, kL1_d[ts(ob), nh * 512:(nh + 1) * 512],
                              wait=[("VE", ve)])
                else:
                    ve = k.op("vector", "tensor_copy", wait=[("PE", pe_last)],
                              out=vTs[:, ob * 1024 + nh * 512: ob * 1024 + nh * 512 + 512],
                              in_=lpool[sl][:, :])
                    lread[sl] = ("VE", ve)
            wkl.release(wi, "PE", k.now("PE"))
    ve_v = k.now("VE")
    k.mksem("STV")
    stv = 0
    for g in range(8):
        for c in range(16):
            tpl.emit(vTs[:, c * 1024 + g * 128: c * 1024 + (g + 1) * 128],
                     vst[:, ts(c)],
                     extra_wait=[("VE", ve_v)],
                     copy_extra_wait=[("STV", stv * 16)] if (c < 2 and stv) else [])
        stv += 1
        t_st = k.op("sync", "dma_start", out=vL1_d[ts(g), :], in_=vst[:, :],
                    wait=[("VE", k.now("VE")), ("AE", k.now("AE"))],
                    inc="STV", incn=16)
    k._note_final("STV", t_st)
    k.pop_phase()

    k.push_phase()
    wsz = k.sbuf("wsz", [128, 4], F32)
    th = k.sbuf("th", [128, 4], F32)
    nth = k.sbuf("nth", [128, 4], F32)
    ubuf = k.sbuf("ubuf", [128, 5632], BF16)
    vbuf = k.sbuf("vbuf", [128, 5632], BF16)
    wshl = LBuf(k, "wshl", 2, [128, 5632], F32)
    tst = SOut(k, "tst", 2, [128, 5632], BF16)
    ppool = k.psum("ppool", [128, 4], F32)
    k.mksem("WSL")
    ve_mz = k.op("vector", "memset", wsz[:, :], 0.0)
    wsld = ("WSL", k.op("sync", "dma_start", out=wsz[0:8, :], in_=wsall,
                        wait=[("VE", ve_mz)], inc="WSL", incn=16))
    pe_ws = k.op("tensor", "matmul", ppool[:, :], onesfs[:, :], wsz[:, :],
                 wait=[wsld, ("VE", ve_mz), ("CN", CN)], start=True, stop=True,
                 inc="PE")
    k.op("vector", "tensor_scalar", wait=[("PE", pe_ws)], out=th[:, :],
         in0=ppool[:, :], scalar1=NW2, scalar2=1e-5, op0=ALU.mult, op1=ALU.max)
    k.op("vector", "tensor_scalar", out=th[:, :], in0=th[:, :],
         scalar1=0.5, scalar2=None, op0=ALU.mult)
    k.op("vector", "tensor_scalar", out=nth[:, :], in0=th[:, :],
         scalar1=-1.0, scalar2=None, op0=ALU.mult)
    for (wap, nch, cols, j, td) in [(w1s, 2, 5632, 0, t1_d), (w2s, 6, 2048, 1, t2_d),
                                    (w3s, 2, 5632, 2, t3_d)]:
        for ch in range(nch):
            wb_, wld, wi = wshl.load("sync", wap[ts(ch), 0:cols],
                                     out_view=lambda b, cols=cols: b[:, 0:cols])
            k.op("vector", "tensor_scalar", wait=[wld], out=ubuf[:, 0:cols],
                 in0=wb_[:, 0:cols], scalar1=th[:, j:j+1], scalar2=None, op0=ALU.is_gt)
            ve_w = k.op("vector", "tensor_scalar", out=vbuf[:, 0:cols],
                        in0=wb_[:, 0:cols], scalar1=nth[:, j:j+1], scalar2=None,
                        op0=ALU.is_ge)
            wshl.release(wi, "VE", ve_w)
            sb_, bw, bi = tst.acquire()
            ve_t = k.op("vector", "tensor_tensor", wait=bw, out=sb_[:, 0:cols],
                        in0=ubuf[:, 0:cols], in1=vbuf[:, 0:cols], op=ALU.add)
            tst.flush(bi, td[ts(ch), 0:cols], wait=[("VE", ve_t)],
                      src_view=lambda b, cols=cols: b[:, 0:cols])
    k.pop_phase()
    k.finish()
    return k.build()


def prep_BCD(inp, resA):
    x = np.ascontiguousarray(inp["L0_attn_hidden_states"].reshape(8192, 2048))
    kTf = np.ascontiguousarray(np.concatenate([r["kT_d"] for r in resA], axis=1))
    vf = np.ascontiguousarray(np.concatenate([r["v_d"] for r in resA], axis=0))
    kRf = np.ascontiguousarray(
        kTf.reshape(16, 128, 64, 128).transpose(2, 1, 0, 3).reshape(64, 128, 2048))
    vRf = np.ascontiguousarray(
        vf.reshape(64, 128, 16, 128).transpose(2, 1, 0, 3).reshape(16, 128, 8192))
    wsall = np.ascontiguousarray(np.stack([r["wsums_d"][0] for r in resA]))
    woutT = np.ascontiguousarray(inp["L0_attn_w_out"].T)
    bout = np.ascontiguousarray(inp["L0_attn_b_out"].reshape(1, 2048))
    identb = np.eye(128, dtype=np.float32)
    onesf = np.ones((128, 128), dtype=np.float32)
    desc = np.arange(8, 0, -1, dtype=np.float32).reshape(1, 8)
    WgT = np.ascontiguousarray(inp["L0_ffn_Wg"].T)
    A1fT = np.ascontiguousarray(inp["L0_ffn_A1"].reshape(256, 2048).T)
    A3fT = np.ascontiguousarray(inp["L0_ffn_A3"].reshape(256, 2048).T)
    B1fT = np.ascontiguousarray(inp["L0_ffn_B1"].transpose(0, 2, 1).reshape(256, 7168))
    B3fT = np.ascontiguousarray(inp["L0_ffn_B3"].transpose(0, 2, 1).reshape(256, 7168))
    A2fT = np.ascontiguousarray(inp["L0_ffn_A2"].reshape(256, 7168).T)
    B2fT = np.ascontiguousarray(inp["L0_ffn_B2"].transpose(0, 2, 1).reshape(256, 2048))
    w1T0 = np.ascontiguousarray(inp["L0_ffn_W1"].T)
    w3T0 = np.ascontiguousarray(inp["L0_ffn_W3"].T)
    w2T0 = np.ascontiguousarray(inp["L0_ffn_W2"].T)
    w2R = np.empty((4, 14, 128, 4 * 512), dtype=np.float32)
    for hb in range(4):
        for g4 in range(14):
            for fi in range(4):
                w2R[hb, g4, :, fi * 512:(fi + 1) * 512] = \
                    w2T0[(4 * g4 + fi) * 128:(4 * g4 + fi + 1) * 128,
                         hb * 512:(hb + 1) * 512]
    w13R = np.empty((56, 128, 16 * 256), dtype=np.float32)
    w1R4 = w1T0.reshape(16, 128, 56, 128)
    w3R4 = w3T0.reshape(16, 128, 56, 128)
    for fb in range(56):
        blk = w13R[fb].reshape(128, 16, 256)
        blk[:, :, 0:128] = w1R4[:, :, fb, :].transpose(1, 0, 2)
        blk[:, :, 128:256] = w3R4[:, :, fb, :].transpose(1, 0, 2)
    bba2 = np.empty((56, 128, 768), dtype=np.float32)
    for fb in range(56):
        bba2[fb, :, 0:256] = B1fT[:, fb * 128:(fb + 1) * 128].reshape(2, 128, 128).transpose(1, 0, 2).reshape(128, 256)
        bba2[fb, :, 256:512] = B3fT[:, fb * 128:(fb + 1) * 128].reshape(2, 128, 128).transpose(1, 0, 2).reshape(128, 256)
        bba2[fb, :, 512:768] = A2fT[fb * 128:(fb + 1) * 128, :]
    a2R = np.ascontiguousarray(
        A2fT.reshape(56, 128, 256).transpose(1, 0, 2).reshape(128, 56 * 256))
    pat1 = np.zeros((128, 256), dtype=np.float32)
    pat2 = np.zeros((128, 256), dtype=np.float32)
    for e in range(8):
        pat1[e, e * 32:(e + 1) * 32] = 1.0
        pat2[e, e * 32:(e + 1) * 32] = 1.0
    wkT = np.ascontiguousarray(inp["L1_attn_wk"].T)
    wvT = np.ascontiguousarray(inp["L1_attn_wv"].T)
    _rt = lambda w: np.ascontiguousarray(
        w.reshape(16, 128, 16, 128).transpose(2, 1, 0, 3).reshape(16, 128, 2048))
    wkR = _rt(wkT)
    wvR = _rt(wvT)
    w1T = np.ascontiguousarray(inp["L1_ffn_w1"].T)
    w2T = np.ascontiguousarray(inp["L1_ffn_w2"].T)
    w3T = np.ascontiguousarray(inp["L1_ffn_w3"].T)
    w2Tp = np.zeros((6144, 2048), dtype=np.float32)
    w2Tp[:5632] = w2T
    maps = []
    for c in range(8):
        qTc = resA[c]["qT_d"]
        qRc = np.ascontiguousarray(
            qTc.reshape(16, 128, 2, 512).transpose(2, 1, 0, 3).reshape(2, 128, 8192))
        xnc = resA[c]["xnT_d"]
        xnRc = np.ascontiguousarray(
            xnc.reshape(16, 128, 1024).transpose(1, 0, 2).reshape(128, 16384))
        maps.append(dict(
            xA=np.ascontiguousarray(x[c * 1024:(c + 1) * 1024]),
            qR=qRc, kR=kRf, vR=vRf, xnR=xnRc,
            woutT=woutT, bout=bout, identb=identb, onesf=onesf, desc=desc,
            WgT=WgT, A1fT=A1fT, A3fT=A3fT, B1fT=B1fT, B3fT=B3fT,
            A2fT=A2fT, a2R=a2R, B2fT=B2fT, w13R=w13R, bba2=bba2, w2R=w2R,
            pat1=pat1, pat2=pat2, wkR=wkR, wvR=wvR, wsall=wsall,
            w1s=np.ascontiguousarray(w1T[c * 256:(c + 1) * 256]),
            w2s=np.ascontiguousarray(w2Tp[c * 768:(c + 1) * 768]),
            w3s=np.ascontiguousarray(w3T[c * 256:(c + 1) * 256]),
        ))
    return maps


def build_launch_EF(debug=False):
    k = K("EF")
    x1f = k.din("x1f", [1024, 2048], F32)
    knR = k.din("knR", [16, 128, 64 * 128], BF16)
    vnR = k.din("vnR", [8, 128, 64 * 256], BF16)
    wqR = k.din("wqR", [16, 128, 16 * 128], F32)
    woT = k.din("woT", [2048, 2048], F32)
    lqk = k.din("lqk", [4, 256], F32)
    m01R = k.din("m01R", [8, 128, 8 * 128], BF16)
    identb = k.din("identb", [128, 128], F32)
    onesf = k.din("onesf", [128, 128], F32)
    t1R = k.din("t1R", [11, 128, 16 * 512], BF16)
    t3R = k.din("t3R", [11, 128, 16 * 512], BF16)
    t2R = k.din("t2R", [4, 44, 128, 512], BF16)
    wsall = k.din("wsall", [8, 4], F32)
    out_d = k.dout("out_d", [1024, 2048], F32)
    dmk = k.dout if debug else k.dtmp
    x1r_d = dmk("x1r_d", [1024, 2048], F32)
    h_d = dmk("h_d", [1024, 5632], F32)
    hqT_d = dmk("hqT_d", [5632, 1024], BF16)
    lam_d = k.dtmp("lam_d", [1, 1], F32)
    sc_d = k.dtmp("sc_d", [1024, 4], F32)

    ident = k.sbuf("ident", [128, 128], BF16)
    onesfs = k.sbuf("onesfs", [128, 128], F32)
    onesb = k.sbuf("onesb", [128, 128], BF16)
    k.mksem("CN", "CNG")
    k.op("gpsimd", "dma_start", out=ident[:, :], in_=identb, inc="CNG", incn=16)
    k.op("gpsimd", "dma_start", out=onesb[:, :], in_=onesf, inc="CNG", incn=16)
    k.op("sync", "dma_start", out=onesfs[:, :], in_=onesf, inc="CN", incn=16)
    CN, CNG = k.now("CN"), k.now("CNG")
    vwc = lambda b: b[:, :].rearrange("p (c j) -> p c j", j=128)

    def wslab(src, ob):
        return src[:, ob * 128:(ob + 1) * 128].rearrange("(c p) j -> p c j", p=128)

    qn_d = k.dtmp("qn_d", [8, 128, 16 * 128], BF16)
    xqT_d = k.dtmp("xqT_d", [2048, 1024], BF16)
    sstat = k.sbuf("sstat", [128, 8 * 8], F32)
    lamn = k.sbuf("lamn", [128, 1], F32)
    k.push_phase()
    xn1T = k.sbuf("xn1T", [128, 16 * 1024], BF16)
    ltmp = k.sbuf("ltmp", [4, 256], F32)
    lsc = k.sbuf("lsc", [1, 8], F32)
    xqtile = k.sbuf("xqtile", [128, 16 * 128], BF16)
    k.mksem("XQS")
    k.mksem("LQK", "LAM")
    lrow = k.sbuf("lrow", [1, 1024], F32)
    k.op("sync", "dma_start", out=lrow[:, :],
         in_=lqk.rearrange("a b -> (a b)").unsqueeze(0), inc="LQK", incn=16)
    ve = k.op("vector", "tensor_tensor", wait=[("LQK", k.now("LQK"))],
              out=ltmp[0:1, 0:256], in0=lrow[:, 0:256], in1=lrow[:, 256:512],
              op=ALU.mult)
    k.op("vector", "reduce_sum", out=lsc[:, 0:1], in_=ltmp[0:1, 0:256], axis=AX.X)
    k.op("vector", "tensor_tensor", out=ltmp[0:1, 0:256], in0=lrow[:, 512:768],
         in1=lrow[:, 768:1024], op=ALU.mult)
    v_d2 = k.op("vector", "reduce_sum", out=lsc[:, 1:2], in_=ltmp[0:1, 0:256],
                axis=AX.X)
    ae1 = k.op("scalar", "activation", wait=[("VE", v_d2)], out=lsc[:, 2:3],
               in_=lsc[:, 0:1], func=AF.Exp)
    ae2 = k.op("scalar", "activation", out=lsc[:, 3:4], in_=lsc[:, 1:2], func=AF.Exp)
    k.op("vector", "tensor_tensor", wait=[("AE", ae2)], out=lsc[:, 4:5],
         in0=lsc[:, 3:4], in1=lsc[:, 2:3], op=ALU.subtract)
    ve_l = k.op("vector", "tensor_scalar", out=lsc[:, 5:6], in0=lsc[:, 4:5],
                scalar1=-0.8, scalar2=None, op0=ALU.add)
    st_l = k.op("sync", "dma_start", out=lam_d, in_=lsc[0:1, 5:6],
                wait=[("VE", ve_l)], inc="LAM", incn=16)
    k.mksem("LAM2")
    k.op("sync", "dma_start", wait=[("LAM", st_l)],
         out=lamn[:, :], in_=lam_d.broadcast_to((128, 1)), inc="LAM2", incn=16)

    xc = k.sbuf("xc", [128, 2048], F32)
    sq = k.sbuf("sq", [128, 2048], F32)
    st1 = k.sbuf("st1", [128, 16], F32)
    vstp = [k.sbuf(f"vstq{i}", [128, 2048], BF16) for i in range(2)]
    xnf = k.sbuf("xnf", [128, 2048], F32)
    ci32 = k.sbuf("ci32", [128, 2048], I32)
    cf32 = k.sbuf("cf32", [128, 2048], F32)
    dbuf = k.sbuf("dbuf", [128, 2048], F32)
    c1b = k.sbuf("c1b", [128, 2048], F32)
    qtm = k.sbuf("qtm", [128, 2048], BF16)
    x1l = LBuf(k, "x1l", 2, [128, 2048], F32)
    tpp = TPose(k, "tpp", ident, alt=True)
    wsz = k.sbuf("wsz", [128, 4], F32)
    msw = k.sbuf("msw", [128, 4], F32)
    k.mksem("WSL")
    ve_mz = k.op("vector", "memset", wsz[:, :], 0.0)
    wsld = ("WSL", k.op("sync", "dma_start", out=wsz[0:8, :], in_=wsall,
                        wait=[("VE", ve_mz)], inc="WSL", incn=16))
    ppool0 = k.psum("ppool0", [128, 4], F32)
    pe_ws = k.op("tensor", "matmul", ppool0[:, :], onesfs[:, :], wsz[:, :],
                 wait=[wsld, ("VE", ve_mz), ("CN", CN)], start=True, stop=True,
                 inc="PE")
    k.op("vector", "tensor_scalar", wait=[("PE", pe_ws)], out=msw[:, :],
         in0=ppool0[:, :], scalar1=NW2, scalar2=1e-5, op0=ALU.mult, op1=ALU.max)
    LN127 = float(np.log(np.float32(127.0)))
    ln127b = k.sbuf("ln127b", [128, 1], F32)
    k.op("vector", "memset", ln127b[:, :], LN127)
    tp_par = [0, 0]
    for j in range(8):
        xt, xldw, xsl = x1l.load("sync", x1f[ts(j), :])
        k.op("vector", "reduce_sum", wait=[xldw], out=st1[:, 0:1],
             in_=xt[:, :], axis=AX.X)
        k.op("vector", "tensor_scalar", out=st1[:, 1:2], in0=st1[:, 0:1],
             scalar1=1.0 / 2048.0, scalar2=None, op0=ALU.mult)
        k.op("vector", "tensor_scalar", out=xc[:, :], in0=xt[:, :],
             scalar1=st1[:, 1:2], scalar2=None, op0=ALU.subtract)
        ve_rel = k.op("vector", "tensor_tensor", out=sq[:, :], in0=xc[:, :],
                      in1=xc[:, :], op=ALU.mult)
        x1l.release(xsl, "VE", ve_rel)
        k.op("vector", "reduce_sum", out=st1[:, 2:3], in_=sq[:, :], axis=AX.X)
        v_ve = k.op("vector", "tensor_scalar", out=st1[:, 3:4], in0=st1[:, 2:3],
                    scalar1=1.0 / 2048.0, scalar2=1e-5, op0=ALU.mult, op1=ALU.add)
        k.op("scalar", "activation", wait=[("VE", v_ve)], out=st1[:, 4:5],
             in_=st1[:, 3:4], func=AF.Ln)
        a2 = k.op("scalar", "activation", out=st1[:, 5:6], in_=st1[:, 4:5],
                  func=AF.Exp, scale=-0.5)
        ve_xn = k.op("vector", "tensor_scalar", wait=[("AE", a2)],
                     out=xnf[:, :], in0=xc[:, :], scalar1=st1[:, 5:6], scalar2=None,
                     op0=ALU.mult)
        vb = vstp[j % 2]
        ve2 = k.op("vector", "tensor_copy", wait=[("PE", tp_par[j % 2])],
                   out=vb[:, :], in_=xnf[:, :])
        for c in range(16):
            pe, _ = tpp.emit(vb[:, ts(c)],
                             xn1T[:, c * 1024 + j * 128: c * 1024 + (j + 1) * 128],
                             extra_wait=[("VE", ve2), ("CNG", CNG)])
            tp_par[j % 2] = pe
        k.op("vector", "tensor_reduce", out=st1[:, 6:7], in_=xnf[:, :], axis=AX.X,
             op=ALU.max, apply_absolute_value=True)
        v_am = k.op("vector", "tensor_scalar", out=st1[:, 7:8], in0=st1[:, 6:7],
                    scalar1=1e-5, scalar2=None, op0=ALU.max)
        k.op("scalar", "activation", wait=[("VE", v_am)], out=st1[:, 8:9],
             in_=st1[:, 7:8], func=AF.Ln)
        a_sx = k.op("scalar", "activation", out=st1[:, 9:10], in_=st1[:, 8:9],
                    func=AF.Exp, scale=-1.0, bias=ln127b[:, 0:1])
        k.op("vector", "tensor_scalar", wait=[("AE", a_sx)], out=st1[:, 10:11],
             in0=st1[:, 7:8], scalar1=1.0 / 127.0, scalar2=None, op0=ALU.mult)
        k.op("vector", "tensor_scalar", out=xnf[:, :], in0=xnf[:, :],
             scalar1=st1[:, 9:10], scalar2=None, op0=ALU.mult)
        k.op("vector", "tensor_copy", out=ci32[:, :], in_=xnf[:, :])
        k.op("vector", "tensor_copy", out=cf32[:, :], in_=ci32[:, :])
        k.op("vector", "tensor_tensor", out=dbuf[:, :], in0=xnf[:, :],
             in1=cf32[:, :], op=ALU.subtract)
        k.op("vector", "scalar_tensor_tensor", out=c1b[:, :], in0=dbuf[:, :],
             scalar=0.5, in1=cf32[:, :], op0=ALU.is_gt, op1=ALU.add)
        ve_q = k.op("vector", "scalar_tensor_tensor", out=qtm[:, :], in0=dbuf[:, :],
                    scalar=-0.5, in1=c1b[:, :], op0=ALU.is_lt, op1=ALU.subtract)
        k.op("vector", "reduce_sum", out=sstat[:, j * 8 + 0: j * 8 + 1],
             in_=qtm[:, :], axis=AX.X)
        k.op("vector", "tensor_tensor", out=sstat[:, j * 8 + 1: j * 8 + 2],
             in0=st1[:, 10:11], in1=msw[:, 0:1], op=ALU.mult)
        k.op("vector", "tensor_scalar", out=sstat[:, j * 8 + 1: j * 8 + 2],
             in0=sstat[:, j * 8 + 1: j * 8 + 2], scalar1=-1.0, scalar2=None,
             op0=ALU.mult)
        k.op("vector", "tensor_tensor", out=sstat[:, j * 8 + 2: j * 8 + 3],
             in0=st1[:, 10:11], in1=msw[:, 2:3], op=ALU.mult)
        k.op("vector", "tensor_scalar", out=sstat[:, j * 8 + 2: j * 8 + 3],
             in0=sstat[:, j * 8 + 2: j * 8 + 3], scalar1=-1.0, scalar2=None,
             op0=ALU.mult)
        ve_sx = k.op("vector", "tensor_copy", out=sstat[:, j * 8 + 3: j * 8 + 4],
                     in_=st1[:, 10:11])
        for c in range(16):
            pe, _ = tpp.emit(qtm[:, ts(c)], xqtile[:, ts(c)],
                             extra_wait=[("VE", ve_q)],
                             copy_extra_wait=[("XQS", j * 16)] if c < 2 else [])
            last_tp_pe = pe
        k.op("sync", "dma_start", wait=[("VE", k.now("VE")), ("AE", k.now("AE"))],
             out=xqT_d[:, j * 128:(j + 1) * 128].rearrange("(c p) o -> p c o", p=128),
             in_=xqtile[:, :].rearrange("p (c o) -> p c o", o=128),
             inc="XQS", incn=16)
    xn1_ve = k.now("VE")
    xn1_ae = k.now("AE")
    wql = LBuf(k, "wql", 2, [128, 16 * 128], BF16)
    qstg = SOut(k, "qstg", 2, [128, 512], BF16)
    sqb = k.sbuf("sqb", [128, 512], F32)
    rstb = k.sbuf("rstb", [128, 512], F32)
    qpool = [k.psum(f"qpool{i}", [128, 512], F32) for i in range(3)]
    qread = [("VE", 0)] * 3
    for ob in range(16):
        _, wld, wi = wql.load("gpsimd", wqR[ob, :, :])
        for nh in range(2):
            sl = nh
            pe_last = None
            for c in range(16):
                wv = ([wld, qread[sl], ("VE", xn1_ve), ("AE", xn1_ae)]
                      if c == 0 else [])
                pe_last = k.op("tensor", "matmul", qpool[sl][:, :],
                               wql.bufs[wi][:, ts(c)],
                               xn1T[:, c * 1024 + nh * 512: c * 1024 + nh * 512 + 512],
                               start=(c == 0), stop=(c == 15), wait=wv,
                               inc=("PE" if c == 15 else None))
            ae_sq = k.op("scalar", "activation", wait=[("PE", pe_last)],
                         out=sqb[:, :], in_=qpool[sl][:, :], func=AF.Square)
            pe_ms = k.op("tensor", "matmul", qpool[2][:, :], onesfs[:, :],
                         sqb[:, :], start=True, stop=True,
                         wait=[("AE", ae_sq), qread[2], ("CN", CN)], inc="PE")
            v_ts = k.op("vector", "tensor_scalar", wait=[("PE", pe_ms)],
                        out=rstb[:, :], in0=qpool[2][:, :],
                        scalar1=1.0 / 128.0, scalar2=RMS_EPS,
                        op0=ALU.mult, op1=ALU.add)
            qread[2] = ("VE", v_ts)
            k.op("scalar", "activation", wait=[("VE", v_ts)], out=rstb[:, :],
                 in_=rstb[:, :], func=AF.Ln)
            ae_r = k.op("scalar", "activation", out=rstb[:, :], in_=rstb[:, :],
                        func=AF.Exp, scale=-0.5)
            qb_, qbw, qbi = qstg.acquire()
            ve = k.op("vector", "scalar_tensor_tensor", wait=[("AE", ae_r)] + qbw,
                      out=qb_[:, :],
                      in0=qpool[sl][:, :], scalar=float(1.0 / np.sqrt(np.float32(128.0))),
                      in1=rstb[:, :], op0=ALU.mult, op1=ALU.mult)
            qread[sl] = ("VE", ve)
            for jj in range(4):
                qstg.flush(qbi, qn_d[nh * 4 + jj, :, ob * 128:(ob + 1) * 128],
                           wait=[("VE", ve)],
                           src_view=lambda b, jj=jj: b[:, jj * 128:(jj + 1) * 128])
        wql.release(wi, "PE", k.now("PE"))
    k.pop_phase()

    k.push_phase()
    PstB = [k.sbuf(f"Pst1_{i}", [128, 64 * 256], BF16) for i in range(2)]
    pst_rel = [("PE", 0), ("PE", 0)]
    mbias = [k.sbuf(f"mbias{i}", [128, 8 * 128], BF16) for i in range(2)]
    k.mksem("MBL0", "MBL1")
    mb_use = [("VE", 0), ("VE", 0)]
    mb_ld = [("MBL0", 0), ("MBL1", 0)]
    rtmp = k.sbuf("rtmp1", [128, 256], F32)
    rinv12 = k.sbuf("rinv12", [128, 256], F32)
    r2l = k.sbuf("r2l", [128, 128], F32)
    ya = k.sbuf("ya", [128, 128], F32)
    yb = k.sbuf("yb", [128, 128], F32)
    yT = k.sbuf("yT", [128, 16 * 128], BF16)
    ksl = LBuf(k, "ksl1", 2, [128, 64 * 128], BF16)
    vsl = LBuf(k, "vsl1", 2, [128, 64 * 256], BF16)
    qnl = LBuf(k, "qnl", 2, [128, 16 * 128], BF16)
    wol = LBuf(k, "wol1", 2, [128, 8 * 512], BF16)
    x1fl = LBuf(k, "x1fl", 2, [128, 512], F32)
    xrst = SOut(k, "xrst1", 3, [128, 512], F32)
    sA = k.psum("sA", [128, 1024], F32)
    sB = k.psum("sB", [128, 1024], F32)
    spool = [sA, sB]
    prs = k.psum("prs", [128, 256], F32)
    pc1 = k.psum("pc1", [128, 256], F32)
    pc2 = k.psum("pc2", [128, 256], F32)
    ppj = k.psum("ppj", [128, 512], F32)
    sread = [("AE", 0), ("AE", 0)]
    rsbox = [("AE", 0)]
    cread = [("VE", 0), ("VE", 0)]

    seq = [(j, h) for j in range(8) for h in range(8)]
    stash = {}

    def stageA(j, h):
        nkt = 8 * j + 8
        pb = (j * 8 + h) % 2
        Pst = PstB[pb]
        if h == 0:
            mi = j % 2
            mld = k.op("sync", "dma_start", wait=[mb_use[mi]],
                       out=mbias[mi][:, :], in_=m01R[j, :, :],
                       inc=f"MBL{mi}", incn=16)
            mb_ld[mi] = (f"MBL{mi}", mld)
            qb_, qld_, qi_ = qnl.load("sync", qn_d[j, :, :])
            stash[("q", j)] = (qb_, qld_, qi_)
        qb_, qld_, qi_ = stash[("q", j)]
        kbufs = []
        for sx in range(2):
            kb_, kld_, ki_ = ksl.load(
                "sync", knR[h * 2 + sx, :, 0:nkt * 128],
                out_view=lambda b: b[:, 0:nkt * 128])
            kbufs.append((kb_, kld_, ki_))
        vb_, vld_, vi_ = vsl.load(
            "sync", vnR[h, :, 0:nkt * 256],
            out_view=lambda b: b[:, 0:nkt * 256])
        exps = []
        for kt0 in range(0, nkt, 4):
            sl = (kt0 // 4) % 2
            pe_l = None
            for dkt in range(min(4, nkt - kt0)):
                kt = kt0 + dkt
                for sx in range(2):
                    wv = []
                    if dkt == 0 and sx == 0:
                        wv = [kbufs[0][1], kbufs[1][1], qld_, sread[sl]]
                        if kt0 == 0:
                            wv.append(pst_rel[pb])
                    pe_l = k.op("tensor", "matmul",
                                spool[sl][:, dkt * 256 + sx * 128: dkt * 256 + (sx + 1) * 128],
                                kbufs[sx][0][:, kt * 128:(kt + 1) * 128],
                                qb_[:, (h * 2 + sx) * 128:(h * 2 + sx + 1) * 128],
                                start=True, stop=True, wait=wv, inc="PE")
            nb = min(4, nkt - kt0)
            ae = k.op("scalar", "activation", wait=[("PE", pe_l)],
                      out=Pst[:, kt0 * 256:(kt0 + nb) * 256],
                      in_=spool[sl][:, 0:nb * 256], func=AF.Exp)
            sread[sl] = ("AE", ae)
        stash[(j, h)] = (kbufs, vb_, vld_, vi_, k.now("AE"))

    def stageB(j, h):
        nkt = 8 * j + 8
        pb = (j * 8 + h) % 2
        Pst = PstB[pb]
        mi = j % 2
        kbufs, vb_, vld_, vi_, exp_last = stash.pop((j, h))
        ve_m = None
        for i in range(8):
            kt = 8 * j + i
            for sx in range(2):
                wv = []
                if i == 0 and sx == 0:
                    wv = [mb_ld[mi], ("AE", exp_last)]
                ve_m = k.op("vector", "tensor_tensor", wait=wv,
                            out=Pst[:, kt * 256 + sx * 128: kt * 256 + (sx + 1) * 128],
                            in0=Pst[:, kt * 256 + sx * 128: kt * 256 + (sx + 1) * 128],
                            in1=mbias[mi][:, i * 128:(i + 1) * 128], op=ALU.mult)
        mb_use[mi] = ("VE", ve_m)
        pe_r = None
        for kt in range(nkt):
            wv = [("VE", ve_m), rsbox[0]] if kt == 0 else []
            pe_r = k.op("tensor", "matmul", prs[:, 0:256], onesb[:, :],
                        Pst[:, kt * 256:(kt + 1) * 256],
                        start=(kt == 0), stop=(kt == nkt - 1), wait=wv,
                        inc=("PE" if kt == nkt - 1 else None))
        aeL = k.op("scalar", "activation", wait=[("PE", pe_r)], out=rtmp[:, :],
                   in_=prs[:, 0:256], func=AF.Ln)
        rsbox[0] = ("AE", aeL)
        aeR = k.op("scalar", "activation", out=rinv12[:, :], in_=rtmp[:, :],
                   func=AF.Exp, scale=-1.0)
        ve_rl = k.op("vector", "tensor_scalar",
                     wait=[("AE", aeR), ("LAM2", 16)],
                     out=r2l[:, :], in0=rinv12[:, 128:256],
                     scalar1=lamn[:, 0:1], scalar2=None, op0=ALU.mult)
        pe_c_last = 0
        for db in range(2):
            pcx = pc1 if db == 0 else pc2
            pe_c = None
            for kt in range(nkt):
                wv = [vld_, cread[db]] if kt == 0 else []
                pe_c = k.op("tensor", "matmul", pcx[:, 0:256],
                            vb_[:, kt * 256 + db * 128: kt * 256 + (db + 1) * 128],
                            Pst[:, kt * 256:(kt + 1) * 256],
                            start=(kt == 0), stop=(kt == nkt - 1), wait=wv,
                            inc=("PE" if kt == nkt - 1 else None))
            pe_c_last = pe_c
            ve_a = k.op("vector", "tensor_tensor", wait=[("PE", pe_c), ("VE", ve_rl)],
                        out=ya[:, :], in0=pcx[:, 0:128],
                        in1=rinv12[:, 0:128], op=ALU.mult)
            ve_b = k.op("vector", "tensor_tensor",
                        out=yb[:, :], in0=pcx[:, 128:256],
                        in1=r2l[:, :], op=ALU.mult)
            ve_y = k.op("vector", "tensor_tensor",
                        out=yT[:, (h * 2 + db) * 128:(h * 2 + db + 1) * 128],
                        in0=ya[:, :], in1=yb[:, :], op=ALU.add)
            cread[db] = ("VE", ve_y)
        pst_rel[pb] = ("PE", pe_c_last)
        ksl.release(kbufs[0][2], "PE", pe_c_last)
        ksl.release(kbufs[1][2], "PE", pe_c_last)
        vsl.release(vi_, "PE", pe_c_last)

    def outproj(j):
        ve_yT = k.now("VE")
        for ob in range(4):
            wolds = []
            for hf in range(2):
                wb_, wld_, wslot_ = wol.load(
                    "gpsimd",
                    woT[hf * 1024:(hf + 1) * 1024, ob * 512:(ob + 1) * 512]
                    .rearrange("(c p) o -> p c o", p=128),
                    out_view=lambda b: b[:, :].rearrange("p (c o) -> p c o", o=512))
                wolds.append((wld_, wb_, wslot_))
            pe_last = None
            for cc in range(16):
                wv = []
                if cc == 0:
                    wv = [wolds[0][0], pj[0], ("VE", ve_yT)]
                if cc == 8:
                    wv = [wolds[1][0]]
                wbuf = wolds[0][1] if cc < 8 else wolds[1][1]
                pe_last = k.op("tensor", "matmul", ppj[:, :],
                               yT[:, cc * 128:(cc + 1) * 128],
                               wbuf[:, (cc % 8) * 512:(cc % 8 + 1) * 512],
                               start=(cc == 0), stop=(cc == 15), wait=wv,
                               inc=("PE" if cc == 15 else None))
            wol.release(wolds[0][2], "PE", pe_last)
            wol.release(wolds[1][2], "PE", pe_last)
            xt_, xw_, xi_ = x1fl.load("sync", x1f[ts(j), ob * 512:(ob + 1) * 512])
            sb_, bw, bi = xrst.acquire()
            ve = k.op("vector", "scalar_tensor_tensor",
                      wait=[("PE", pe_last), xw_] + bw,
                      out=sb_[:, :], in0=ppj[:, :], scalar=0.2,
                      in1=xt_[:, :], op0=ALU.mult, op1=ALU.add)
            pj[0] = ("VE", ve)
            x1fl.release(xi_, "VE", ve)
            xrst.flush(bi, x1r_d[ts(j), ob * 512:(ob + 1) * 512], wait=[("VE", ve)])
        qb_, qld_, qi_ = stash.pop(("q", j))
        qnl.release(qi_, "PE", k.now("PE"))

    pj = [("VE", 0)]
    for idx, (j, h) in enumerate(seq):
        stageA(j, h)
        if idx > 0:
            pj_, ph_ = seq[idx - 1]
            stageB(pj_, ph_)
            if ph_ == 7:
                outproj(pj_)
    stageB(7, 7)
    outproj(7)
    k.pop_phase()

    k.push_phase()
    xqT = k.sbuf("xqT", [128, 16 * 1024], BF16)
    k.mksem("XQL")
    XQL = ("XQL", k.op("sync", "dma_start",
                       out=xqT[:, :].rearrange("p (c t) -> p c t", t=1024),
                       in_=xqT_d.rearrange("(c p) t -> p c t", p=128),
                       inc="XQL", incn=16))
    t1l = LBuf(k, "t1l", 2, [128, 16 * 512], BF16)
    t3l = LBuf(k, "t3l", 2, [128, 16 * 512], BF16)
    o1b = k.sbuf("o1b", [128, 512], F32)
    o3b = k.sbuf("o3b", [128, 512], F32)
    s1b = k.sbuf("s1b", [128, 512], F32)
    hst = SOut(k, "hst", 2, [128, 512], F32)
    hpool = [k.psum(f"hpool{i}", [128, 512], F32) for i in range(4)]
    hread = [("VE", 0)] * 4
    vw512b = lambda b: b[:, :].rearrange("p (c j) -> p c j", j=512)
    for fcol in range(11):
        _, t1ld, t1i = t1l.load("sync", t1R[fcol, :, :])
        _, t3ld, t3i = t3l.load("sync", t3R[fcol, :, :])
        for j in range(8):
            slA, slB = (0, 1) if j % 2 == 0 else (2, 3)
            pe1 = None
            for c in range(16):
                wv = [t1ld, hread[slA], XQL] if c == 0 else []
                pe1 = k.op("tensor", "matmul", hpool[slA][:, :],
                           xqT[:, c * 1024 + j * 128: c * 1024 + (j + 1) * 128],
                           t1l.bufs[t1i][:, c * 512:(c + 1) * 512],
                           start=(c == 0), stop=(c == 15), wait=wv,
                           inc=("PE" if c == 15 else None))
            pe3 = None
            for c in range(16):
                wv = [t3ld, hread[slB]] if c == 0 else []
                pe3 = k.op("tensor", "matmul", hpool[slB][:, :],
                           xqT[:, c * 1024 + j * 128: c * 1024 + (j + 1) * 128],
                           t3l.bufs[t3i][:, c * 512:(c + 1) * 512],
                           start=(c == 0), stop=(c == 15), wait=wv,
                           inc=("PE" if c == 15 else None))
            ve_o1 = k.op("vector", "tensor_scalar", wait=[("PE", pe1)],
                         out=o1b[:, :], in0=hpool[slA][:, :],
                         scalar1=sstat[:, j * 8 + 0: j * 8 + 1],
                         scalar2=sstat[:, j * 8 + 1: j * 8 + 2],
                         op0=ALU.subtract, op1=ALU.mult)
            hread[slA] = ("VE", ve_o1)
            ae_s = k.op("scalar", "activation", wait=[("VE", ve_o1)],
                        out=s1b[:, :], in_=o1b[:, :], func=AF.Silu)
            ve_o3 = k.op("vector", "tensor_scalar", wait=[("PE", pe3)],
                         out=o3b[:, :], in0=hpool[slB][:, :],
                         scalar1=sstat[:, j * 8 + 0: j * 8 + 1],
                         scalar2=sstat[:, j * 8 + 2: j * 8 + 3],
                         op0=ALU.subtract, op1=ALU.mult)
            hread[slB] = ("VE", ve_o3)
            sb_, bw, bi = hst.acquire()
            ve_h = k.op("vector", "tensor_tensor", wait=[("AE", ae_s)] + bw,
                        out=sb_[:, :], in0=s1b[:, :], in1=o3b[:, :], op=ALU.mult)
            hst.flush(bi, h_d[ts(j), fcol * 512:(fcol + 1) * 512],
                      wait=[("VE", ve_h)])
        t1l.release(t1i, "PE", k.now("PE"))
        t3l.release(t3i, "PE", k.now("PE"))
    k.pop_phase()

    k.push_phase()
    hld = LBuf(k, "hld", 2, [128, 5632], F32)
    hc32 = k.sbuf("hc32", [128, 5632], I32)
    hcf = k.sbuf("hcf", [128, 5632], F32)
    hdb = k.sbuf("hdb", [128, 5632], F32)
    hc1 = k.sbuf("hc1", [128, 5632], F32)
    hqtm = k.sbuf("hqtm", [128, 5632], BF16)
    hqT = k.sbuf("hqT", [128, 44 * 128], BF16)
    hstt = k.sbuf("hstt", [128, 16], F32)
    ln127c = k.sbuf("ln127c", [128, 1], F32)
    k.op("vector", "memset", ln127c[:, :], LN127)
    tph = TPose(k, "tph", ident, alt=True)
    k.mksem("HQS")
    hq_st = 0
    for j in range(8):
        hb_, hldw, hi = hld.load("sync", h_d[ts(j), :])
        k.op("vector", "tensor_reduce", wait=[hldw], out=hstt[:, 0:1],
             in_=hb_[:, :], axis=AX.X, op=ALU.max, apply_absolute_value=True)
        v_am = k.op("vector", "tensor_scalar", out=hstt[:, 1:2], in0=hstt[:, 0:1],
                    scalar1=1e-5, scalar2=None, op0=ALU.max)
        k.op("scalar", "activation", wait=[("VE", v_am)], out=hstt[:, 2:3],
             in_=hstt[:, 1:2], func=AF.Ln)
        a_sx = k.op("scalar", "activation", out=hstt[:, 3:4], in_=hstt[:, 2:3],
                    func=AF.Exp, scale=-1.0, bias=ln127c[:, 0:1])
        k.op("vector", "tensor_scalar", wait=[("AE", a_sx)], out=hstt[:, 4:5],
             in0=hstt[:, 1:2], scalar1=1.0 / 127.0, scalar2=None, op0=ALU.mult)
        ve_y = k.op("vector", "tensor_scalar", out=hdb[:, :], in0=hb_[:, :],
                    scalar1=hstt[:, 3:4], scalar2=None, op0=ALU.mult)
        a_c1 = k.op("scalar", "copy", wait=[("VE", ve_y)], out=hc32[:, :],
                    in_=hdb[:, :])
        a_c2 = k.op("scalar", "copy", out=hcf[:, :], in_=hc32[:, :])
        ve_rel = k.op("vector", "tensor_tensor", wait=[("AE", a_c2)],
                      out=hdb[:, :], in0=hdb[:, :],
                      in1=hcf[:, :], op=ALU.subtract)
        hld.release(hi, "VE", ve_rel)
        k.op("vector", "scalar_tensor_tensor", out=hc1[:, :], in0=hdb[:, :],
             scalar=0.5, in1=hcf[:, :], op0=ALU.is_gt, op1=ALU.add)
        ve_q = k.op("vector", "scalar_tensor_tensor", out=hqtm[:, :], in0=hdb[:, :],
                    scalar=-0.5, in1=hc1[:, :], op0=ALU.is_lt, op1=ALU.subtract)
        k.op("vector", "reduce_sum", out=sstat[:, j * 8 + 4: j * 8 + 5],
             in_=hqtm[:, :], axis=AX.X)
        ve_sx = k.op("vector", "tensor_copy", out=sstat[:, j * 8 + 5: j * 8 + 6],
                     in_=hstt[:, 4:5])
        for c in range(44):
            pe, ve_c = tph.emit(hqtm[:, ts(c)], hqT[:, ts(c)],
                                extra_wait=[("VE", ve_q)],
                                copy_extra_wait=[("HQS", hq_st)] if c < 2 else [])
        st = k.op("sync", "dma_start",
                  wait=[("VE", k.now("VE")), ("AE", k.now("AE")), ("HQS", hq_st)],
                  out=hqT_d[:, j * 128:(j + 1) * 128]
                  .rearrange("(c p) o -> p c o", p=128),
                  in_=hqT[:, :].rearrange("p (c o) -> p c o", o=128),
                  inc="HQS", incn=16)
        hq_st = st
    k._note_final("HQS", hq_st)
    k.pop_phase()

    k.push_phase()
    wsz2 = k.sbuf("wsz2", [128, 4], F32)
    msw2 = k.sbuf("msw2", [128, 4], F32)
    k.mksem("WSL2")
    ve_mz = k.op("vector", "memset", wsz2[:, :], 0.0)
    wsld = ("WSL2", k.op("sync", "dma_start", out=wsz2[0:8, :], in_=wsall,
                         wait=[("VE", ve_mz)], inc="WSL2", incn=16))
    k.push_phase()
    ppool2 = k.psum("ppool2", [128, 4], F32)
    pe_ws = k.op("tensor", "matmul", ppool2[:, :], onesfs[:, :], wsz2[:, :],
                 wait=[wsld, ("VE", ve_mz)], start=True, stop=True, inc="PE")
    k.op("vector", "tensor_scalar", wait=[("PE", pe_ws)], out=msw2[:, :],
         in0=ppool2[:, :], scalar1=NW2, scalar2=1e-5, op0=ALU.mult, op1=ALU.max)
    k.pop_phase()
    for j in range(8):
        k.op("vector", "tensor_tensor", out=sstat[:, j * 8 + 6: j * 8 + 7],
             in0=sstat[:, j * 8 + 5: j * 8 + 6], in1=msw2[:, 1:2], op=ALU.mult)
        k.op("vector", "tensor_scalar", out=sstat[:, j * 8 + 6: j * 8 + 7],
             in0=sstat[:, j * 8 + 6: j * 8 + 7], scalar1=-1.0, scalar2=None,
             op0=ALU.mult)
    hql = LBuf(k, "hql", 2, [128, 1024], BF16)
    t2l = LBuf(k, "t2l", 2, [128, 512], BF16)
    xrl = LBuf(k, "xrl2", 2, [128, 512], F32)
    ost = SOut(k, "ost", 3, [128, 512], F32)
    tbv = k.sbuf("tbv", [128, 512], F32)
    opool = [k.psum(f"opool{i}", [128, 512], F32) for i in range(8)]
    oread = [("VE", 0)] * 8
    for hb in range(4):
        hcols = slice(hb * 512, (hb + 1) * 512)
        mm_t = {}
        for fc in range(44):
            hb_, hqld, hqi = hql.load("sync", hqT_d[fc * 128:(fc + 1) * 128, :])
            tb_, t2ld, t2i = t2l.load("sync", t2R[hb, fc, :, :])
            for j in range(8):
                wv = [hqld, t2ld]
                if fc == 0:
                    wv = wv + [oread[j]]
                mm_t[j] = k.op("tensor", "matmul", opool[j][:, :],
                               hb_[:, ts(j)], tb_[:, :],
                               start=(fc == 0), stop=(fc == 43), wait=wv, inc="PE")
            hql.release(hqi, "PE", mm_t[7])
            t2l.release(t2i, "PE", mm_t[7])
        for j in range(8):
            xr_, xrw, xri = xrl.load("sync", x1r_d[ts(j), hcols])
            ve_t = k.op("vector", "tensor_scalar", wait=[("PE", mm_t[j])],
                        out=tbv[:, :], in0=opool[j][:, :],
                        scalar1=sstat[:, j * 8 + 4: j * 8 + 5],
                        scalar2=sstat[:, j * 8 + 6: j * 8 + 7],
                        op0=ALU.subtract, op1=ALU.mult)
            oread[j] = ("VE", ve_t)
            sb_, bw, bi = ost.acquire()
            ve = k.op("vector", "tensor_tensor", wait=[xrw] + bw,
                      out=sb_[:, :], in0=tbv[:, :], in1=xr_[:, :], op=ALU.add)
            xrl.release(xri, "VE", ve)
            ost.flush(bi, out_d[ts(j), hcols], wait=[("VE", ve)])
    k.pop_phase()
    k.finish()
    return k.build()


FOLD = [tuple(8 * j + c for j in range(8)) for c in range(8)]


def prep_EF(inp, resB):
    x1 = np.concatenate([r["x1_d"] for r in resB], axis=0)
    knf = np.concatenate([r["kL1_d"] for r in resB], axis=1)
    vnf = np.concatenate([r["vL1_d"] for r in resB], axis=0)
    t1f = np.concatenate([r["t1_d"] for r in resB], axis=0)
    t3f = np.concatenate([r["t3_d"] for r in resB], axis=0)
    t2f = np.concatenate([r["t2_d"] for r in resB], axis=0)[:5632]
    knR = np.ascontiguousarray(knf.reshape(16, 128, 8192))
    vnR = np.ascontiguousarray(
        vnf.reshape(64, 128, 8, 256).transpose(2, 1, 0, 3).reshape(8, 128, 64 * 256))
    t1R = np.ascontiguousarray(
        t1f.reshape(16, 128, 11, 512).transpose(2, 1, 0, 3).reshape(11, 128, 16 * 512))
    t3R = np.ascontiguousarray(
        t3f.reshape(16, 128, 11, 512).transpose(2, 1, 0, 3).reshape(11, 128, 16 * 512))
    t2R = np.ascontiguousarray(
        t2f.reshape(44, 128, 4, 512).transpose(2, 0, 1, 3))
    wqT = np.ascontiguousarray(inp["L1_attn_wq"].T)
    wqR = np.ascontiguousarray(
        wqT.reshape(16, 128, 16, 128).transpose(2, 1, 0, 3).reshape(16, 128, 2048))
    woT = np.ascontiguousarray(inp["L1_attn_wo"].T)
    lqk = np.stack([inp["L1_attn_lq1"], inp["L1_attn_lk1"],
                    inp["L1_attn_lq2"], inp["L1_attn_lk2"]]).astype(np.float32)
    identb = np.eye(128, dtype=np.float32)
    onesf = np.ones((128, 128), dtype=np.float32)
    import ml_dtypes
    w1T = np.ascontiguousarray(inp["L1_ffn_w1"].T)
    w2T = np.ascontiguousarray(inp["L1_ffn_w2"].T)
    w3T = np.ascontiguousarray(inp["L1_ffn_w3"].T)
    w2Tp = np.zeros((6144, 2048), dtype=np.float32)
    w2Tp[:5632] = w2T
    wsall = np.stack([np.array([np.abs(w1T[c*256:(c+1)*256]).sum(),
                                np.abs(w2Tp[c*768:(c+1)*768]).sum(),
                                np.abs(w3T[c*256:(c+1)*256]).sum(), 0.0],
                               dtype=np.float32) for c in range(8)])
    maps = []
    for c in range(8):
        rows = np.concatenate([np.arange(t * 128, (t + 1) * 128) for t in FOLD[c]])
        m01 = np.zeros((8, 8, 128, 128), dtype=np.float32)
        for j in range(8):
            qt = 8 * j + c
            for i in range(8):
                kt = 8 * j + i
                kg = kt * 128 + np.arange(128)[:, None]
                qg = qt * 128 + np.arange(128)[None, :]
                m01[j, i] = (kg <= qg).astype(np.float32)
        m01R = np.ascontiguousarray(m01.transpose(0, 2, 1, 3).reshape(8, 128, 1024))
        maps.append(dict(
            x1f=np.ascontiguousarray(x1[rows]),
            knR=knR, vnR=vnR, wqR=wqR, woT=woT, lqk=lqk,
            m01R=m01R.astype(ml_dtypes.bfloat16),
            identb=identb, onesf=onesf,
            t1R=t1R, t3R=t3R, t2R=t2R, wsall=wsall,
        ))
    return maps


def assemble_out(resE):
    full = np.zeros((8192, 2048), dtype=np.float32)
    for c in range(8):
        o = resE[c]["out_d"]
        for ji, t in enumerate(FOLD[c]):
            full[t * 128:(t + 1) * 128] = o[ji * 128:(ji + 1) * 128]
    return full.reshape(1, 8192, 2048)


from concourse.bass_utils import run_bass_kernel_spmd

_CACHE = {}


def _get(name, fn):
    if name not in _CACHE:
        _CACHE[name] = fn()
    return _CACHE[name]


def kernel(**inputs):
    inp = {k_: np.asarray(v) for k_, v in inputs.items()}
    ncA = _get("A", build_launch_A)
    resA = run_bass_kernel_spmd(ncA, prep_A(inp), core_ids=list(range(8))).results
    ncB = _get("BCD", build_launch_BCD)
    resB = run_bass_kernel_spmd(ncB, prep_BCD(inp, resA), core_ids=list(range(8))).results
    ncE = _get("EF", build_launch_EF)
    resE = run_bass_kernel_spmd(ncE, prep_EF(inp, resB), core_ids=list(range(8))).results
    return assemble_out(resE)
